# version 02
# baseline (speedup 1.0000x reference)
import numpy as np
import ml_dtypes
import concourse.bass as bass
import concourse.mybir as mybir
from concourse.bass_utils import run_bass_kernel_spmd

F32 = mybir.dt.float32
BF16 = mybir.dt.bfloat16
I32 = mybir.dt.int32
U16 = mybir.dt.uint16
U8 = mybir.dt.uint8
AF = mybir.ActivationFunctionType
ALU = mybir.AluOpType
AX = mybir.AxisListType.X

D = 4096
T = 1024
TH = 512
NCH = 32
S = 4096
GROUPS = [[0, 1, 2, 3], [4, 5, 6, 7]]
ALLG = [list(range(8))]
BF = ml_dtypes.bfloat16
ts = bass.ts



class Pack:

    def __init__(self, dtype):
        self.dtype = dtype
        self.parts = []
        self.offs = {}
        self.n = 0

    def add(self, name, arr):
        assert arr.ndim == 2 and arr.shape[0] == 128, (name, arr.shape)
        arr = np.ascontiguousarray(arr.astype(self.dtype, copy=False))
        F = arr.shape[1]
        Fp = -(-F // 4) * 4
        if Fp != F:
            a2 = np.zeros((128, Fp), dtype=self.dtype)
            a2[:, :F] = arr
            arr = a2
        self.offs[name] = (self.n, Fp)
        self.parts.append(arr)
        self.n += 128 * Fp
        return self.offs[name]

    def add_flat(self, name, arr1d):
        arr1d = np.ascontiguousarray(arr1d.reshape(-1).astype(self.dtype, copy=False))
        n = arr1d.size
        npad = -(-n // 512) * 512
        a2 = np.zeros(npad, dtype=self.dtype)
        a2[:n] = arr1d
        self.offs[name] = (self.n, npad // 128)
        self.parts.append(a2.reshape(128, -1))
        self.n += npad
        return self.offs[name]

    def finish(self):
        flat = (np.concatenate([p.reshape(-1) for p in self.parts])
                if self.parts else np.zeros((8 * 512,), dtype=self.dtype))
        rows = -(-flat.size // 512)
        rows = max(-(-rows // 8) * 8, 8)
        out = np.zeros((rows, 512), dtype=self.dtype)
        out.reshape(-1)[:flat.size] = flat
        return out


def img_lhsT(W):
    K, M = W.shape
    assert K % 128 == 0
    return np.ascontiguousarray(
        W.reshape(K // 128, 128, M).transpose(1, 0, 2).reshape(128, -1))



SEMS = ["ld", "gd", "pe", "ve", "ac", "po", "cc"]



class KB:
    def __init__(self, nc, blk, sems, sb, ps, wall, atlas, dram, cst_ap, catlas):
        self.nc = nc
        self.blk = blk
        self.sems = sems
        self.cnt = {k: 0 for k in SEMS}
        self.sb = sb
        self.ps = ps
        self.wall = wall
        self.atlas = atlas
        self.dram = dram
        self.cst = cst_ap
        self.catlas = catlas
        self.recs = {n: Rec(n) for n in
                     ("tensor", "vector", "scalar", "sync", "gpsimd")}

    def inc(self, inst, sem, k=1):
        if sem in ("ve", "ac", "po"):
            eng = {"ve": "vector", "ac": "scalar", "po": "gpsimd"}[sem]
            self.cnt[sem] = self.recs[eng].chn
            return self.cnt[sem]
        inst.then_inc(self.sems[sem], k)
        self.cnt[sem] += k
        return self.cnt[sem]

    def wreg(self, pack, name, rows=128, c0=0, cols=None):
        off, F = self.atlas[pack][name]
        base = self.wall[pack].rearrange("r c -> (r c)")
        reg = base[off: off + 128 * F].rearrange("(p f) -> p f", f=F)
        if cols is None:
            cols = F - c0
        return reg[0:rows, c0:c0 + cols]

    def wflat(self, pack, name):
        off, F = self.atlas[pack][name]
        base = self.wall[pack].rearrange("r c -> (r c)")
        return base[off: off + 128 * F]

    def creg(self, name, rows=128, c0=0, cols=None):
        off, F = self.catlas[name]
        base = self.cst.rearrange("r c -> (r c)")
        reg = base[off: off + 128 * F].rearrange("(p f) -> p f", f=F)
        if cols is None:
            cols = F - c0
        return reg[0:rows, c0:c0 + cols]

    def bodies(self, *names):
        return [self.recs[n] for n in names]

    NOCHAIN = {"dma_start", "dma_start_transpose", "wait_ge", "sem_inc",
               "collective_compute", "indirect_dma_start", "engine_nop"}
    CHAINED = {"vector", "scalar", "gpsimd"}

    def finalize(self):
        for n, rec in self.recs.items():
            if not rec.ops:
                continue
            chain = n in KB.CHAINED
            csem = self.sems.get("ch_" + n)

            def body(eng, rec=rec, chain=chain, csem=csem):
                nch = 0
                for (meth, args, kwargs, incs) in rec.ops:
                    do_chain = chain and meth not in KB.NOCHAIN
                    if do_chain and nch > 0:
                        eng.wait_ge(csem, nch)
                    inst = getattr(eng, meth)(*args, **kwargs)
                    assert len(incs) <= 1 or not do_chain, (n, meth, incs)
                    for (sem, k) in incs:
                        inst.then_inc(sem, k)
                    if do_chain:
                        assert not incs, (n, meth, "chained op with explicit inc")
                        inst.then_inc(csem, 1)
                        nch += 1
            getattr(self.blk, n)(body)


class RecInst:
    def __init__(self):
        self.incs = []

    def then_inc(self, sem, k=1):
        self.incs.append((sem, k))
        return self


class Rec:

    def __init__(self, name):
        self.name = name
        self.ops = []
        self.chn = 0

    def __getattr__(self, meth):
        if meth.startswith("_"):
            raise AttributeError(meth)

        def call(*args, **kwargs):
            ri = RecInst()
            self.ops.append((meth, args, kwargs, ri.incs))
            if self.name in KB.CHAINED and meth not in KB.NOCHAIN:
                self.chn += 1
            return ri
        return call



def emit_preamble(kb, use_packs, skip_x=False):
    (g,) = kb.bodies("gpsimd")
    g.memset(kb.sb["ones_bf"][:, :], 1.0)
    g.memset(kb.sb["rowsb"][0:1, 0:512], 1.0)
    g.memset(kb.sb["id_bf"][:, :], 0.0)
    g.affine_select(out=kb.sb["id_bf"][:, :], in_=kb.sb["id_bf"][:, :],
                    compare_op=ALU.not_equal, fill=1.0, base=0,
                    pattern=[[-1, 128]], channel_multiplier=1)
    g.memset(kb.sb["id_f"][:, :], 0.0)
    inst = g.affine_select(out=kb.sb["id_f"][:, :], in_=kb.sb["id_f"][:, :],
                           compare_op=ALU.not_equal, fill=1.0, base=0,
                           pattern=[[-1, 128]], channel_multiplier=1)
    kb.inc(inst, "po", 1)
    if "B" in use_packs:
        kb.inc(g.dma_start(out=kb.dram["cbf"].ap(), in_=kb.dram["wshB"].ap()), "gd", 16)
    if "F" in use_packs:
        kb.inc(g.dma_start(out=kb.dram["cf"].ap(), in_=kb.dram["wshF"].ap()), "gd", 16)
    if "I" in use_packs:
        kb.inc(g.dma_start(out=kb.dram["ci"].ap(), in_=kb.dram["wshI"].ap()), "gd", 16)
    g.wait_ge(kb.sems["gd"], kb.cnt["gd"])
    for p in use_packs:
        src = {"B": "cbf", "F": "cf", "I": "ci"}[p]
        dst = {"B": "WB", "F": "WF", "I": "WI"}[p]
        inst = g.collective_compute(
            "AllGather", ALU.bypass, replica_groups=ALLG,
            ins=[kb.dram[src].ap().opt()], outs=[kb.dram[dst].ap().opt()])
        kb.inc(inst, "cc", 1)
        g.wait_ge(kb.sems["cc"], kb.cnt["cc"])

    (s,) = kb.bodies("sync")
    for k in range(NCH):
        inst = s.dma_start(
            out=kb.sb["xres"][:, k * T:(k + 1) * T],
            in_=kb.dram["xT"].ap()[k * 128:(k + 1) * 128, :])
        kb.inc(inst, "ld", 16)

    ac, ve, t_ = kb.bodies("scalar", "vector", "tensor")
    ac.wait_ge(kb.sems["ld"], kb.cnt["ld"])
    ve.wait_ge(kb.sems["ld"], kb.cnt["ld"])
    st = {}
    for m in range(NCH):
        for h in range(2):
            emit_stats_chunk(kb, ac, ve, t_, m, h, first=(m == 0), st=st)


def emit_stats_chunk(kb, ac, ve, t_, m, h, first, st):
    xb = kb.sb["xb"]
    rows = kb.sb["rows"]
    pE = kb.ps["pE"]
    hist = st.setdefault("pe_hist", [])
    slot = len(hist) % 2
    xbs = xb[:, slot * 1024:(slot + 1) * 1024]
    xs = kb.sb["xres"][:, m * T + h * TH: m * T + h * TH + TH]
    ac.wait_ge(kb.sems["ve"], kb.cnt["ve"])
    if len(hist) >= 2:
        ac.wait_ge(kb.sems["pe"], hist[-2])
        ve.wait_ge(kb.sems["pe"], hist[-2])
    kb.inc(ac.activation(out=xbs[:, TH:2 * TH], in_=xs, func=AF.Square), "ac", 1)
    kb.inc(ve.tensor_copy(xbs[:, 0:TH], xs), "ve", 1)
    v_ac, v_ve = kb.cnt["ac"], kb.cnt["ve"]
    t_.wait_ge(kb.sems["ac"], v_ac)
    t_.wait_ge(kb.sems["ve"], v_ve)
    if st.get("guard") is not None:
        t_.wait_ge(kb.sems["ve"], st["guard"])
    t_.matmul(pE[0:1, :], kb.sb["ones_bf"][:, 0:1], xbs[:, 0:TH], start=True, stop=True)
    kb.inc(t_.matmul(pE[32:33, :], kb.sb["ones_bf"][:, 0:1], xbs[:, TH:2 * TH],
                     start=True, stop=True), "pe", 1)
    v_pe = kb.cnt["pe"]
    hist.append(v_pe)
    ve.wait_ge(kb.sems["pe"], v_pe)
    su = rows[0:1, h * TH:(h + 1) * TH]
    sq = rows[32:33, h * TH:(h + 1) * TH]
    if first:
        ve.tensor_copy(su, pE[0:1, :])
        kb.inc(ve.tensor_copy(sq, pE[32:33, :]), "ve", 1)
    else:
        ve.tensor_add(su, su, pE[0:1, :])
        kb.inc(ve.tensor_add(sq, sq, pE[32:33, :]), "ve", 1)
    st["guard"] = kb.cnt["ve"]
    return st["guard"]


def emit_yprod(kb, h):
    rows, rowsb = kb.sb["rows"], kb.sb["rowsb"]
    pA, pB = kb.ps["pA"], kb.ps["pB"]
    ve, ac, t_ = kb.bodies("vector", "scalar", "tensor")
    r = lambda b, c: rows[b:b + 1, c * TH:(c + 1) * TH]
    rb = lambda c: rowsb[64:65, c * TH:(c + 1) * TH]
    mean, var, trow = r(64, 0), r(64, 1), r(64, 2)
    ve.tensor_scalar_mul(mean, r(0, h), 1.0 / D)
    ve.tensor_scalar_mul(var, r(32, h), 1.0 / D)
    ve.tensor_mul(trow, mean, mean)
    ve.tensor_sub(var, var, trow)
    kb.inc(ve.tensor_scalar_add(var, var, 1e-5), "ve", 1)
    ac.wait_ge(kb.sems["ve"], kb.cnt["ve"])
    kb.inc(ac.activation(out=var, in_=var, func=AF.Sqrt), "ac", 1)
    ve.wait_ge(kb.sems["ac"], kb.cnt["ac"])
    ve.reciprocal(var, var)
    ve.tensor_copy(rb(0), mean)
    ve.tensor_sub(trow, mean, rb(0))
    ve.tensor_copy(rb(1), trow)
    ve.tensor_copy(rb(2), var)
    ve.tensor_sub(trow, var, rb(2))
    kb.inc(ve.tensor_copy(rb(3), trow), "ve", 1)
    t_.wait_ge(kb.sems["ve"], kb.cnt["ve"])
    t_.matmul(pA[:, 0:TH], kb.sb["ones_bf"][64:65, :], rb(2), start=True, stop=False)
    t_.matmul(pA[:, 0:TH], kb.sb["ones_bf"][64:65, :], rb(3), start=False, stop=True)
    t_.matmul(pB[:, 0:TH], kb.sb["ones_bf"][64:65, :], rb(0), start=True, stop=False)
    kb.inc(t_.matmul(pB[:, 0:TH], kb.sb["ones_bf"][64:65, :], rb(1),
                     start=False, stop=True), "pe", 1)
    ve.wait_ge(kb.sems["pe"], kb.cnt["pe"])
    tsc = kb.sb["tscr"]
    inst = None
    for k in range(NCH):
        xs = kb.sb["xres"][:, k * T + h * TH: k * T + h * TH + TH]
        ve.tensor_sub(tsc[:, :], xs, pB[:, 0:TH])
        inst = ve.tensor_mul(kb.sb["yh"][:, k * TH:(k + 1) * TH], tsc[:, :], pA[:, 0:TH])
    kb.inc(inst, "ve", 1)
    return kb.cnt["ve"]


def emit_store_x(kb):
    (s,) = kb.bodies("sync")
    s.wait_ge(kb.sems["ve"], kb.cnt["ve"])
    s.wait_ge(kb.sems["ac"], kb.cnt["ac"])
    for k in range(NCH):
        inst = s.dma_start(
            out=kb.dram["xO"].ap()[k * 128:(k + 1) * 128, :],
            in_=kb.sb["xres"][:, k * T:(k + 1) * T])
        kb.inc(inst, "ld", 16)
    s.wait_ge(kb.sems["ld"], kb.cnt["ld"])



def pack_s6(inputs, PB, PF, PI):
    rw = np.asarray(inputs["L1_ffn_router_w"], np.float32)
    A = np.asarray(inputs["L1_ffn_A"], np.float32)
    Bm = np.asarray(inputs["L1_ffn_B"], np.float32)
    PB.add("s6_rt", img_lhsT(np.ascontiguousarray(rw.T)))
    aimg = np.concatenate(
        [img_lhsT(np.ascontiguousarray(A[e].T)) for e in range(4)], axis=1)
    PB.add("s6_a", aimg)
    tiles = []
    for mc in range(32):
        for e in range(4):
            tiles.append(np.ascontiguousarray(Bm[e].T[:, mc * 128:(mc + 1) * 128]))
    PB.add_flat("s6_b", np.stack(tiles).reshape(-1))


def emit_s6(kb):
    pA, pB, pC2, pE = kb.ps["pA"], kb.ps["pB"], kb.ps["pC2"], kb.ps["pE"]
    ws = kb.sb["ws"]
    wsb = ws.bitcast(BF16)
    wsf = ws.bitcast(F32)
    A0, RT0, MID0, BT0 = 0, 2048, 2176, 6272
    GF0 = 3648
    LG0, GT0, GS0, NEG0, SC0 = 5696, 6720, 6752, 6784, 6788
    GHI4, GLO4, GSLH, GSLL = 9344, 10368, 11392, 0
    mid = lambda e, h: wsb[0:16, MID0 + (e * 2 + h) * TH: MID0 + (e * 2 + h + 1) * TH]

    (s,) = kb.bodies("sync")
    s.wait_ge(kb.sems["cc"], kb.cnt["cc"])
    kb.inc(s.dma_start(out=wsb[:, A0:A0 + 2048], in_=kb.wreg("B", "s6_a")), "ld", 16)
    kb.inc(s.dma_start(out=wsb[:, RT0:RT0 + 128], in_=kb.wreg("B", "s6_rt")), "ld", 16)
    v_wld = kb.cnt["ld"]
    (ve0,) = kb.bodies("vector")
    kb.inc(ve0.memset(wsf[:, NEG0:NEG0 + 4], -1e30), "ve", 1)

    mtgt = [(pC2, 0), (pC2, TH), (pA, 0), (pA, TH)]
    for h in range(2):
        ydone = emit_yprod(kb, h)
        t_, ve = kb.bodies("tensor", "vector")
        t_.wait_ge(kb.sems["ve"], ydone)
        t_.wait_ge(kb.sems["ld"], v_wld)
        lrow = 0 if h == 0 else 32
        for k in range(NCH):
            t_.matmul(pE[lrow:lrow + 4, :], wsb[:, RT0 + k * 4:RT0 + (k + 1) * 4],
                      kb.sb["yh"][:, k * TH:(k + 1) * TH],
                      start=(k == 0), stop=(k == NCH - 1))
        last = None
        for e in range(4):
            pt, c0 = mtgt[e]
            for k in range(NCH):
                last = t_.matmul(
                    pt[0:16, c0:c0 + TH],
                    wsb[:, A0 + (e * 32 + k) * 16:A0 + (e * 32 + k + 1) * 16],
                    kb.sb["yh"][:, k * TH:(k + 1) * TH],
                    start=(k == 0), stop=(k == NCH - 1))
        kb.inc(last, "pe", 1)
        ve.wait_ge(kb.sems["pe"], kb.cnt["pe"])
        ve.tensor_copy(wsf[0:4, LG0 + h * TH:LG0 + (h + 1) * TH], pE[lrow:lrow + 4, :])
        last = None
        for e in range(4):
            pt, c0 = mtgt[e]
            last = ve.tensor_copy(mid(e, h), pt[0:16, c0:c0 + TH])
        kb.inc(last, "ve", 1)

    t_, ve, ac = kb.bodies("tensor", "vector", "scalar")
    t_.wait_ge(kb.sems["ve"], kb.cnt["ve"])
    t_.wait_ge(kb.sems["po"], kb.cnt["po"])
    last = None
    for j in range(8):
        last = t_.transpose(pC2[:, j * 4:(j + 1) * 4],
                            wsf[0:4, LG0 + j * 128:LG0 + (j + 1) * 128],
                            kb.sb["id_f"][0:4, 0:4])
    kb.inc(last, "pe", 1)
    ve.wait_ge(kb.sems["pe"], kb.cnt["pe"])
    kb.inc(ve.tensor_copy(wsf[:, GT0:GT0 + 32], pC2[:, 0:32]), "ve", 1)
    for j in range(8):
        gt = wsf[:, GT0 + j * 4:GT0 + (j + 1) * 4]
        gs = wsf[:, GS0 + j * 4:GS0 + (j + 1) * 4]
        m1 = wsf[:, SC0:SC0 + 1]
        m2 = wsf[:, SC0 + 1:SC0 + 2]
        nm1 = wsf[:, SC0 + 2:SC0 + 3]
        eqm = ws[:, 27200:27204]
        eqf = wsf[:, SC0 + 4:SC0 + 8]
        ve.reduce_max(m1, gt, axis=AX)
        ve.tensor_single_scalar(eqm, gt, m1, ALU.is_ge)
        ve.select(gs, eqm, wsf[:, NEG0:NEG0 + 4], gt)
        ve.reduce_max(m2, gs, axis=AX)
        kb.inc(ve.tensor_scalar_mul(nm1, m1, -1.0), "ve", 1)
        ac.wait_ge(kb.sems["ve"], kb.cnt["ve"])
        kb.inc(ac.activation(out=gs, in_=gt, func=AF.Exp, bias=nm1, scale=1.0), "ac", 1)
        ve.wait_ge(kb.sems["ac"], kb.cnt["ac"])
        ve.tensor_single_scalar(eqf, gt, m2, ALU.is_ge)
        ve.tensor_mul(gs, gs, eqf)
        ve.reduce_sum(m2, gs, axis=AX)
        ve.reciprocal(m2, m2)
        ve.tensor_scalar_mul(gs, gs, m2)
        kb.inc(ve.tensor_scalar_mul(gs, gs, 4.0), "ve", 1)
    t_.wait_ge(kb.sems["ve"], kb.cnt["ve"])
    last = None
    for j in range(8):
        last = t_.transpose(pA[0:4, j * 128:(j + 1) * 128],
                            wsf[:, GS0 + j * 4:GS0 + (j + 1) * 4],
                            kb.sb["id_f"][:, :])
    kb.inc(last, "pe", 1)
    v_tp = kb.cnt["pe"]
    ve.wait_ge(kb.sems["pe"], v_tp)
    g_f = wsf[0:4, GF0:GF0 + 1024]
    ghi4 = wsb[0:4, GHI4:GHI4 + 1024]
    glo4 = wsb[0:4, GLO4:GLO4 + 1024]
    gslh = wsb[0:1, GSLH:GSLH + 4096]
    gsll = wsb[0:1, GSLL:GSLL + 4096]
    ve.tensor_copy(g_f, pA[0:4, 0:1024])
    ve.tensor_copy(ghi4, g_f)
    ve.tensor_sub(g_f, g_f, ghi4)
    kb.inc(ve.tensor_copy(glo4, g_f), "ve", 1)
    (gd,) = kb.bodies("gpsimd")
    gd.wait_ge(kb.sems["ve"], kb.cnt["ve"])
    gd.wait_ge(kb.sems["pe"], v_tp)
    scrf = kb.dram["scr"].ap().rearrange("r c -> (r c)")
    kb.inc(gd.dma_start(out=scrf[0:4096].rearrange("(p f) -> p f", f=1024), in_=ghi4),
           "gd", 16)
    kb.inc(gd.dma_start(out=scrf[4096:8192].rearrange("(p f) -> p f", f=1024), in_=glo4),
           "gd", 16)
    gd.wait_ge(kb.sems["gd"], kb.cnt["gd"])
    kb.inc(gd.dma_start(out=gslh, in_=scrf[0:4096].rearrange("(p f) -> p f", f=4096)),
           "gd", 16)
    kb.inc(gd.dma_start(out=gsll, in_=scrf[4096:8192].rearrange("(p f) -> p f", f=4096)),
           "gd", 16)
    vg = kb.cnt["gd"]

    for h in range(2):
        t_, ve, ac, s = kb.bodies("tensor", "vector", "scalar", "sync")
        t_.wait_ge(kb.sems["gd"], vg)
        btgt = [(pB, 0), (pB, TH), (pA, 0), (pA, TH)]
        last = None
        for e in range(4):
            pt, c0 = btgt[e]
            t_.matmul(pt[0:16, c0:c0 + TH], kb.sb["ones_bf"][0:1, 0:16],
                      wsb[0:1, GSLH + e * 1024 + h * TH:GSLH + e * 1024 + (h + 1) * TH],
                      start=True, stop=False)
            last = t_.matmul(pt[0:16, c0:c0 + TH], kb.sb["ones_bf"][0:1, 0:16],
                             wsb[0:1, GSLL + e * 1024 + h * TH:GSLL + e * 1024 + (h + 1) * TH],
                             start=False, stop=True)
        kb.inc(last, "pe", 1)
        ve.wait_ge(kb.sems["pe"], kb.cnt["pe"])
        last = None
        for e in range(4):
            pt, c0 = btgt[e]
            last = ve.tensor_mul(mid(e, h), mid(e, h), pt[0:16, c0:c0 + TH])
        kb.inc(last, "ve", 1)
        vmid = kb.cnt["ve"]
        stats_st = {} if h == 0 else stats_st_carry
        v_prev_mm = kb.cnt["pe"]
        for mg in range(16):
            ms = [2 * mg, 2 * mg + 1]
            s.wait_ge(kb.sems["pe"], v_prev_mm)
            for i, m in enumerate(ms):
                src = kb.wflat("B", "s6_b")[m * 8192:(m + 1) * 8192] \
                    .rearrange("(e r mm) -> r e mm", e=4, r=16)
                dst = wsb[0:16, BT0 + i * 512:BT0 + (i + 1) * 512] \
                    .rearrange("r (e mm) -> r e mm", e=4)
                kb.inc(s.dma_start(out=dst, in_=src), "ld", 16)
            t_.wait_ge(kb.sems["ld"], kb.cnt["ld"])
            if mg == 0:
                t_.wait_ge(kb.sems["ve"], vmid)
            last = None
            for i, m in enumerate(ms):
                for e in range(4):
                    last = t_.matmul(
                        pC2[:, i * TH:(i + 1) * TH],
                        wsb[0:16, BT0 + i * 512 + e * 128:BT0 + i * 512 + (e + 1) * 128],
                        mid(e, h), start=(e == 0), stop=(e == 3))
            kb.inc(last, "pe", 1)
            v_prev_mm = kb.cnt["pe"]
            ve.wait_ge(kb.sems["pe"], v_prev_mm)
            for i, m in enumerate(ms):
                xs = kb.sb["xres"][:, m * T + h * TH: m * T + h * TH + TH]
                kb.inc(ve.tensor_add(xs, xs, pC2[:, i * TH:(i + 1) * TH]), "ve", 1)
                emit_stats_chunk(kb, ac, ve, t_, m, h,
                                 first=(mg == 0 and i == 0), st=stats_st)
        stats_st_carry = stats_st



def build_program(name, emits, packs_np, catlas, cst_rows, use_packs, scr_bytes=0,
                  scrw_bytes=0, kv_specs=(), store=True, int_rows=0, halo=False):
    nc = bass.Bass("TRN2", target_bir_lowering=False, debug=False)
    dram = {}
    dram["xT"] = nc.dram_tensor("xT", [D, T], F32, kind="ExternalInput")
    dram["xO"] = nc.dram_tensor("xO", [D, T], F32, kind="ExternalOutput")
    cst = nc.dram_tensor("cst", [cst_rows, 512], F32, kind="ExternalInput")
    if halo:
        dram["xh"] = nc.dram_tensor("xh", [128, 64], F32, kind="ExternalInput")
    shapes = {"B": BF16, "F": F32, "I": I32}
    wall = {}
    for p in use_packs:
        rows = packs_np[p].shape[0]
        dt = shapes[p]
        dram["wsh" + p] = nc.dram_tensor("wsh" + p, [rows // 8, 512],
                                         F32 if p != "I" else I32, kind="ExternalInput")
        dram[{"B": "cbf", "F": "cf", "I": "ci"}[p]] = nc.dram_tensor(
            "c" + p, [rows // 8, 512], dt)
        dram[{"B": "WB", "F": "WF", "I": "WI"}[p]] = nc.dram_tensor(
            "W" + p, [rows, 512], dt)
    if int_rows:
        dram["wshI"] = nc.dram_tensor("wshI", [int_rows, 512], I32, kind="ExternalInput")
    if scr_bytes:
        dram["scr"] = nc.dram_tensor("scr", [scr_bytes // 512 // 2, 512], BF16)
    if scrw_bytes:
        dram["scrW"] = nc.dram_tensor("scrW", [scrw_bytes // 512 // 2, 512], BF16)
    for kname, in_shape, out_shape, dt in kv_specs:
        dram[kname + "_in"] = nc.dram_tensor(kname + "_in", list(in_shape), dt)
        dram[kname + "_out"] = nc.dram_tensor(kname + "_out", list(out_shape), dt)

    from contextlib import ExitStack
    with ExitStack() as ctx:
        xres = ctx.enter_context(nc.sbuf_tensor("xres", [128, NCH * T], F32))
        yh = ctx.enter_context(nc.sbuf_tensor("yh", [128, NCH * TH], BF16))
        ws = ctx.enter_context(nc.sbuf_tensor("ws", [128, 31232], U8))
        rows = ctx.enter_context(nc.sbuf_tensor("rows", [128, 1536], F32))
        rowsb = ctx.enter_context(nc.sbuf_tensor("rowsb", [128, 2048], BF16))
        xb = ctx.enter_context(nc.sbuf_tensor("xb", [128, 2048], BF16))
        tscr = ctx.enter_context(nc.sbuf_tensor("tscr", [128, 512], F32))
        ones_bf = ctx.enter_context(nc.sbuf_tensor("ones_bf", [128, 128], BF16))
        id_bf = ctx.enter_context(nc.sbuf_tensor("id_bf", [128, 128], BF16))
        id_f = ctx.enter_context(nc.sbuf_tensor("id_f", [128, 128], F32))
        pA = ctx.enter_context(nc.psum_tensor("pA", [128, 1024], F32))
        pB = ctx.enter_context(nc.psum_tensor("pB", [128, 1024], F32))
        pC2 = ctx.enter_context(nc.psum_tensor("pC2", [128, 1024], F32))
        pE = ctx.enter_context(nc.psum_tensor("pE", [128, 512], F32))
        pT = ctx.enter_context(nc.psum_tensor("pT", [128, 512], BF16))
        s_ld = ctx.enter_context(nc.semaphore("s_ld"))
        s_gd = ctx.enter_context(nc.semaphore("s_gd"))
        s_pe = ctx.enter_context(nc.semaphore("s_pe"))
        s_ve = ctx.enter_context(nc.semaphore("s_ve"))
        s_ac = ctx.enter_context(nc.semaphore("s_ac"))
        s_po = ctx.enter_context(nc.semaphore("s_po"))
        s_cc = ctx.enter_context(nc.semaphore("s_cc"))
        s_chv = ctx.enter_context(nc.semaphore("s_chv"))
        s_chs = ctx.enter_context(nc.semaphore("s_chs"))
        s_chg = ctx.enter_context(nc.semaphore("s_chg"))
        blk = ctx.enter_context(nc.Block())
        sems = {"ld": s_ld, "gd": s_gd, "pe": s_pe, "ve": s_chv, "ac": s_chs,
                "po": s_chg, "cc": s_cc, "ch_vector": s_chv, "ch_scalar": s_chs,
                "ch_gpsimd": s_chg}
        sb = {"xres": xres[:, :], "yh": yh[:, :], "ws": ws[:, :], "rows": rows[:, :],
              "rowsb": rowsb[:, :], "xb": xb[:, :], "tscr": tscr[:, :],
              "ones_bf": ones_bf[:, :], "id_bf": id_bf[:, :], "id_f": id_f[:, :]}
        ps = {"pA": pA[:, :], "pB": pB[:, :], "pC2": pC2[:, :], "pE": pE[:, :],
              "pT": pT[:, :]}
        wall_aps = {}
        atlas = {}
        for p in use_packs:
            wall_aps[p] = dram[{"B": "WB", "F": "WF", "I": "WI"}[p]].ap()
            atlas[p] = PACK_OFFS[p]
        kb = KB(nc, blk, sems, sb, ps, wall_aps, atlas, dram, cst.ap(), catlas)
        emit_preamble(kb, use_packs)
        for em in emits:
            em(kb)
        if store:
            emit_store_x(kb)
        kb.finalize()
    return nc


PACK_OFFS = {}



_CACHE = {}


CONSTS_PARTS = ("s2", "att", "s3", "s5")


def _consts_pack(core, inputs, parts=CONSTS_PARTS):
    P = Pack(np.float32)
    P.add("zero", np.zeros((128, 4), np.float32))
    if "s2" in parts:
        s2_consts(inputs, core % 8, P)
    if "att" in parts:
        att_consts(core, P)
    if "s3" in parts:
        s3_consts(inputs, P)
    if "s5" in parts:
        att5_consts(core, P)
    return P.finish(), dict(P.offs)


SUB_ORDER = ["s1", "s2", "s3", "s4", "s5", "s6", "s7", "s8"]
SUB_KEY = {"s4": "s2", "s8": "s6"}


def _build_sub(which, inputs):
    PB = Pack(np.float32)
    PF = Pack(np.float32)
    PI = Pack(np.int32)
    kwargs = {}
    if which == "s2":
        cparts = ("s2",)
        use = ("B",)
        emits = [emit_s2]
        kwargs = dict(scr_bytes=2 * S2_END + 1024, scrw_bytes=2 * S2_WEND + 1024,
                      int_rows=S2L_ROWS)
    elif which == "s6":
        pack_s6(inputs, PB, PF, PI)
        cparts = ()
        use = ("B",)
        emits = [emit_s6]
        kwargs = dict(scr_bytes=65536)
    elif which == "s1":
        pack_s1(inputs, PB)
        cparts = ("att",)
        use = ("B",)
        emits = [emit_s1]
        kwargs = dict(scr_bytes=2 * S1_END + 1024)
    elif which == "s7":
        pack_s7(inputs, PB, PF)
        cparts = ("att",)
        use = ("B", "F")
        emits = [emit_s7]
        kwargs = dict(scr_bytes=2 * S7_END + 1024)
    elif which == "s3":
        pack_s3(inputs, PB, PF)
        cparts = ("s3",)
        use = ("B", "F")
        emits = [emit_s3]
        kwargs = dict(scr_bytes=2 * S3_END + 1024, halo=True)
    elif which == "s5":
        pack_s5(inputs, PB)
        cparts = ("s5",)
        use = ("B",)
        emits = [emit_s5]
        kwargs = dict(scr_bytes=2 * S5_END + 1024)
    packs = {"B": PB.finish(), "F": PF.finish(), "I": PI.finish()}
    PACK_OFFS["B"] = dict(PB.offs)
    PACK_OFFS["F"] = dict(PF.offs)
    PACK_OFFS["I"] = dict(PI.offs)
    csts = [_consts_pack(c, inputs, cparts) for c in range(8)]
    nc = _CACHE.get(which)
    if nc is None:
        nc = build_program(which, emits, packs, csts[0][1], csts[0][0].shape[0],
                           use, **kwargs)
        _CACHE[which] = nc
    return nc, packs, csts, use


def _run_sub(which, built, x, inputs, ishards):
    nc, packs, csts, use = built
    xT = [np.ascontiguousarray(x[c // 4].T[:, (c % 4) * T:(c % 4 + 1) * T])
          for c in range(8)]
    in_maps = []
    for c in range(8):
        m = {"xT": xT[c], "cst": csts[c][0]}
        for p in use:
            R = packs[p].shape[0]
            m["wsh" + p] = packs[p][(c * R // 8):((c + 1) * R // 8)]
        if which == "s2":
            m["wshI"] = ishards[c]
        if which == "s3":
            m["xh"] = halo_cols(x, c)
        in_maps.append(m)
    res = run_bass_kernel_spmd(nc, in_maps, core_ids=list(range(8)))
    out = np.empty((2, S, D), np.float32)
    for c in range(8):
        out[c // 4, (c % 4) * T:(c % 4 + 1) * T, :] = res.results[c]["xO"].T
    return out


def kernel(**inputs):
    import os
    which = os.environ.get("KSUB", "all")
    x = np.asarray(inputs["L0_attn_a_x"], np.float32)
    subs = SUB_ORDER if which == "all" else [which]
    ishards = None
    built = {}
    for sub in subs:
        key = SUB_KEY.get(sub, sub)
        if key == "s2" and ishards is None:
            ishards = [s2_int_shard(inputs, c % 8) for c in range(8)]
        if key not in built:
            built[key] = _build_sub(key, inputs)
        x = _run_sub(key, built[key], x, inputs, ishards)
    return x


IN2 = 11008
NIC = 86

def _pi_perm():
    p = np.zeros(128, np.int64)
    for j in range(4):
        for c in range(16):
            for b in range(2):
                p[j * 32 + 2 * c + b] = 8 * c + 4 * b + j
    return p


def _perm_cols_128(a):
    p = _pi_perm()
    R, N = a.shape
    a2 = a.reshape(R, N // 128, 128)[:, :, p]
    return np.ascontiguousarray(a2.reshape(R, N))


def pack_s2(inputs, PB, PF, PI):
    pass


S2_KCPR = 4
S2_ICPR = 11
S2L_QG = 0
S2L_ZG = 128 * S2_KCPR * 2752
S2L_QD = S2L_ZG + S2_KCPR * 2752
S2L_ZD = S2L_QD + 128 * S2_ICPR * 512
S2L_END = S2L_ZD + S2_ICPR * 512
S2L_ROWS = (-(-S2L_END // 512) // 8 * 8 + 8)


def s2_int_shard(inputs, r):
    qgu = np.asarray(inputs["L0_ffn_qweight_gu"])
    qzg = np.asarray(inputs["L0_ffn_qzeros_gu"])
    qdn = np.asarray(inputs["L0_ffn_qweight_dn"])
    qzd = np.asarray(inputs["L0_ffn_qzeros_dn"])
    gimg = img_lhsT(qgu)
    dimg = img_lhsT(qdn)
    out = np.zeros(S2L_ROWS * 512, np.int32)
    out[S2L_QG:S2L_QG + 128 * S2_KCPR * 2752] = \
        gimg[:, r * S2_KCPR * 2752:(r + 1) * S2_KCPR * 2752].reshape(-1)
    out[S2L_ZG:S2L_ZG + S2_KCPR * 2752] = \
        qzg[r * S2_KCPR:(r + 1) * S2_KCPR].reshape(-1)
    ics = list(range(r * S2_ICPR, min((r + 1) * S2_ICPR, NIC)))
    dpart = np.zeros((128, S2_ICPR * 512), np.int32)
    dpart[:, 0:len(ics) * 512] = dimg[:, ics[0] * 512:(ics[-1] + 1) * 512]
    out[S2L_QD:S2L_QD + 128 * S2_ICPR * 512] = dpart.reshape(-1)
    zpart = np.zeros((S2_ICPR, 512), np.int32)
    zpart[0:len(ics)] = qzd[ics[0]:ics[-1] + 1]
    out[S2L_ZD:S2L_ZD + S2_ICPR * 512] = zpart.reshape(-1)
    return out.reshape(S2L_ROWS, 512)


def s2_consts(inputs, r, CP):
    sg = np.asarray(inputs["L0_ffn_scales_gu"], np.float32)
    sd = np.asarray(inputs["L0_ffn_scales_dn"], np.float32)
    sgp = np.concatenate([_perm_cols_128(sg[:, :IN2]), _perm_cols_128(sg[:, IN2:])],
                         axis=1)
    CP.add_flat("s2_sg", sgp[r * S2_KCPR:(r + 1) * S2_KCPR].reshape(-1))
    ics = list(range(r * S2_ICPR, min((r + 1) * S2_ICPR, NIC)))
    sdp = np.zeros((S2_ICPR, 4096), np.float32)
    sdp[0:len(ics)] = sd[ics[0]:ics[-1] + 1]
    CP.add_flat("s2_sd", sdp.reshape(-1))


I16 = mybir.dt.int16I16 = mybir.dt.int16


def _unpack_pi(ve, src_u16, dst_u16):
    s3 = src_u16.rearrange("p (blk u) -> p blk u", u=32)
    d4 = dst_u16.rearrange("p (blk q u) -> p blk q u", q=4, u=32)
    last = None
    for j in range(4):
        last = ve.tensor_scalar(out=d4[:, :, j, :], in0=s3, scalar1=4 * j,
                                scalar2=15, op0=ALU.logical_shift_right,
                                op1=ALU.bitwise_and)
    return last


def _unpack_strided(ve, src_u16, dst_u16):
    sv = src_u16.rearrange("p (w b2) -> p w b2", b2=2)
    d = dst_u16.rearrange("p (w n) -> p w n", n=8)
    last = None
    for b in range(2):
        for j in range(4):
            last = ve.tensor_scalar(out=d[:, :, 4 * b + j], in0=sv[:, :, b],
                                    scalar1=4 * j, scalar2=15,
                                    op0=ALU.logical_shift_right,
                                    op1=ALU.bitwise_and)
    return last


S2_SGL, S2_ZGL = 0, 88064
S2_SGL2 = 176128
S2_SDL, S2_ZDL = 264192, 309248
S2_SDL2 = 354304
S2_CWG = 399360
S2_CWD = 11671552
S2_WDF = 17438720
S2_H0 = 63576064
S2_END = S2_H0 + IN2 * 1024
S2_WEND = 32 * 128 * 22016


def emit_s2_dequant(kb):
    pC2 = kb.ps["pC2"]
    ws = kb.sb["ws"]
    wsb, wsi, wsu, wsf = (ws.bitcast(BF16), ws.bitcast(I32), ws.bitcast(U16),
                          ws.bitcast(F32))
    scr = kb.dram["scr"].ap().rearrange("r c -> (r c)")
    ish = kb.dram["wshI"].ap().rearrange("r c -> (r c)")

    def r2(off, rows, cols):
        return scr[off: off + rows * cols].rearrange("(r c) -> r c", c=cols)

    def ri2(off, rows, cols):
        return ish[off: off + rows * cols].rearrange("(r c) -> r c", c=cols)

    t_, ve, s, g = kb.bodies("tensor", "vector", "sync", "gpsimd")
    s.wait_ge(kb.sems["cc"], kb.cnt["cc"])

    kb.inc(s.dma_start(out=wsi[0:S2_KCPR, 0:2752], in_=ri2(S2L_ZG, S2_KCPR, 2752)),
           "ld", 16)
    sg_c = kb.wflat
    sgl = kb.creg("s2_sg", rows=1)
    sg_f = kb.cst.rearrange("r c -> (r c)")
    off_sg, _ = kb.catlas["s2_sg"]
    off_sd, _ = kb.catlas["s2_sd"]
    sgu_f = sg_f[off_sg: off_sg + S2_KCPR * 22016].rearrange("(r c) -> r c", c=22016)
    sdn_f = sg_f[off_sd: off_sd + S2_ICPR * 4096].rearrange("(r c) -> r c", c=4096)
    for pc in range(43):
        if pc > 0:
            s.wait_ge(kb.sems["ch_vector"], kb.cnt["ve"])
        kb.inc(s.dma_start(out=wsf[0:S2_KCPR, 3584:3584 + 512],
                           in_=sgu_f[:, pc * 512:(pc + 1) * 512]), "ld", 16)
        vs = kb.cnt["ld"]
        ve.wait_ge(kb.sems["ld"], vs)
        ve.wait_ge(kb.sems["gd"], kb.cnt["gd"])
        _unpack_pi(ve, wsu[0:S2_KCPR, pc * 128:(pc + 1) * 128],
                   wsu[0:S2_KCPR, 6656:6656 + 512])
        ve.tensor_copy(wsb[0:S2_KCPR, 8192:8192 + 512], wsu[0:S2_KCPR, 6656:6656 + 512])
        ve.tensor_copy(wsb[0:S2_KCPR, 8704:8704 + 512],
                       wsf[0:S2_KCPR, 3584:3584 + 512])
        ve.tensor_sub(wsf[0:S2_KCPR, 4608:4608 + 512], wsf[0:S2_KCPR, 3584:3584 + 512],
                      wsb[0:S2_KCPR, 8704:8704 + 512])
        kb.inc(ve.tensor_copy(wsb[0:S2_KCPR, 9216:9216 + 512],
                              wsf[0:S2_KCPR, 4608:4608 + 512]), "ve", 1)
        g.wait_ge(kb.sems["ch_vector"], kb.cnt["ve"])
        kb.inc(g.dma_start(out=r2(S2_ZGL, S2_KCPR, 22016)[:, pc * 512:(pc + 1) * 512],
                           in_=wsb[0:S2_KCPR, 8192:8192 + 512]), "gd", 16)
        kb.inc(g.dma_start(out=r2(S2_SGL, S2_KCPR, 22016)[:, pc * 512:(pc + 1) * 512],
                           in_=wsb[0:S2_KCPR, 8704:8704 + 512]), "gd", 16)
        kb.inc(g.dma_start(out=r2(S2_SGL2, S2_KCPR, 22016)[:, pc * 512:(pc + 1) * 512],
                           in_=wsb[0:S2_KCPR, 9216:9216 + 512]), "gd", 16)
    v_zg = kb.cnt["gd"]
    s.wait_ge(kb.sems["gd"], v_zg)
    kb.inc(s.dma_start(out=wsi[0:S2_ICPR, 0:512], in_=ri2(S2L_ZD, S2_ICPR, 512)),
           "ld", 16)
    for pc in range(8):
        if pc > 0:
            s.wait_ge(kb.sems["ch_vector"], kb.cnt["ve"])
        kb.inc(s.dma_start(out=wsf[0:S2_ICPR, 3584:3584 + 512],
                           in_=sdn_f[:, pc * 512:(pc + 1) * 512]), "ld", 16)
        vs = kb.cnt["ld"]
        ve.wait_ge(kb.sems["ld"], vs)
        ve.wait_ge(kb.sems["gd"], kb.cnt["gd"])
        _unpack_strided(ve, wsu[0:S2_ICPR, pc * 128:(pc + 1) * 128],
                        wsu[0:S2_ICPR, 6656:6656 + 512])
        ve.tensor_copy(wsb[0:S2_ICPR, 8192:8192 + 512], wsu[0:S2_ICPR, 6656:6656 + 512])
        ve.tensor_copy(wsb[0:S2_ICPR, 8704:8704 + 512],
                       wsf[0:S2_ICPR, 3584:3584 + 512])
        ve.tensor_sub(wsf[0:S2_ICPR, 4608:4608 + 512], wsf[0:S2_ICPR, 3584:3584 + 512],
                      wsb[0:S2_ICPR, 8704:8704 + 512])
        kb.inc(ve.tensor_copy(wsb[0:S2_ICPR, 9216:9216 + 512],
                              wsf[0:S2_ICPR, 4608:4608 + 512]), "ve", 1)
        g.wait_ge(kb.sems["ch_vector"], kb.cnt["ve"])
        kb.inc(g.dma_start(out=r2(S2_ZDL, S2_ICPR, 4096)[:, pc * 512:(pc + 1) * 512],
                           in_=wsb[0:S2_ICPR, 8192:8192 + 512]), "gd", 16)
        kb.inc(g.dma_start(out=r2(S2_SDL, S2_ICPR, 4096)[:, pc * 512:(pc + 1) * 512],
                           in_=wsb[0:S2_ICPR, 8704:8704 + 512]), "gd", 16)
        kb.inc(g.dma_start(out=r2(S2_SDL2, S2_ICPR, 4096)[:, pc * 512:(pc + 1) * 512],
                           in_=wsb[0:S2_ICPR, 9216:9216 + 512]), "gd", 16)

    grp_sizes = [2] * 21 + [1]
    cwg = scr[S2_CWG: S2_CWG + S2_KCPR * 128 * 22016].rearrange(
        "(kc p c) -> kc p c", p=128, c=22016)
    v_row_prev = kb.cnt["ve"]
    v_bc_prev = None
    v_store_prev = None
    for kc in range(S2_KCPR):
        if v_row_prev is not None:
            s.wait_ge(kb.sems["ch_vector"], v_row_prev)
        kb.inc(s.dma_start(
            out=wsi[:, 0:2752],
            in_=ri2(S2L_QG, 128, S2_KCPR * 2752)[:, kc * 2752:(kc + 1) * 2752]),
            "ld", 16)
        v_q = kb.cnt["ld"]
        pc0 = 0
        for gi, gsz in enumerate(grp_sizes):
            ncol = gsz * 512
            if v_bc_prev is not None:
                s.wait_ge(kb.sems["pe"], v_bc_prev)
            kb.inc(s.dma_start(
                out=wsb[0:1, 5632:5632 + ncol],
                in_=r2(S2_SGL, S2_KCPR, 22016)[kc:kc + 1, pc0 * 512:pc0 * 512 + ncol]),
                "ld", 16)
            kb.inc(s.dma_start(
                out=wsb[0:1, 6656:6656 + ncol],
                in_=r2(S2_ZGL, S2_KCPR, 22016)[kc:kc + 1, pc0 * 512:pc0 * 512 + ncol]),
                "ld", 16)
            kb.inc(s.dma_start(
                out=wsb[0:1, 7680:7680 + ncol],
                in_=r2(S2_SGL2, S2_KCPR, 22016)[kc:kc + 1, pc0 * 512:pc0 * 512 + ncol]),
                "ld", 16)
            v_rows = kb.cnt["ld"]
            for q in range(gsz):
                pc = pc0 + q
                t_.wait_ge(kb.sems["ld"], v_rows)
                if v_row_prev is not None:
                    t_.wait_ge(kb.sems["ch_vector"], v_row_prev)
                t_.matmul(pC2[:, 0:512], kb.sb["ones_bf"][0:1, :],
                          wsb[0:1, 5632 + q * 512:5632 + (q + 1) * 512],
                          start=True, stop=False)
                t_.matmul(pC2[:, 0:512], kb.sb["ones_bf"][0:1, :],
                          wsb[0:1, 7680 + q * 512:7680 + (q + 1) * 512],
                          start=False, stop=True)
                kb.inc(t_.matmul(pC2[:, 512:1024], kb.sb["ones_bf"][0:1, :],
                                 wsb[0:1, 6656 + q * 512:6656 + (q + 1) * 512],
                                 start=True, stop=True), "pe", 1)
                v_bc = kb.cnt["pe"]
                v_bc_prev = v_bc
                ve.wait_ge(kb.sems["ld"], v_q)
                _unpack_pi(ve, wsu[:, pc * 128:(pc + 1) * 128], wsu[:, 8704:8704 + 512])
                ve.tensor_copy(wsf[:, 4608:4608 + 512], wsu[:, 8704:8704 + 512])
                ve.wait_ge(kb.sems["pe"], v_bc)
                if q == 0 and v_store_prev is not None:
                    ve.wait_ge(kb.sems["gd"], v_store_prev)
                ve.tensor_sub(wsf[:, 4608:4608 + 512], wsf[:, 4608:4608 + 512],
                              pC2[:, 512:1024])
                kb.inc(ve.tensor_mul(
                    wsb[:, 10240 + q * 512:10240 + q * 512 + 512],
                    wsf[:, 4608:4608 + 512], pC2[:, 0:512]), "ve", 1)
                v_row_prev = kb.cnt["ve"]
            g.wait_ge(kb.sems["ch_vector"], kb.cnt["ve"])
            kb.inc(g.dma_start(out=cwg[kc, :, pc0 * 512: pc0 * 512 + ncol],
                               in_=wsb[:, 10240:10240 + ncol]), "gd", 16)
            v_store_prev = kb.cnt["gd"]
            pc0 += gsz

    cwd = scr[S2_CWD: S2_CWD + S2_ICPR * 128 * 4096].rearrange(
        "(s p c) -> s p c", p=128, c=4096)
    v_dstore = None
    for ic in range(S2_ICPR):
        if v_row_prev is not None:
            s.wait_ge(kb.sems["ch_vector"], v_row_prev)
        kb.inc(s.dma_start(
            out=wsi[:, 0:512],
            in_=ri2(S2L_QD, 128, S2_ICPR * 512)[:, ic * 512:(ic + 1) * 512]), "ld", 16)
        v_qd = kb.cnt["ld"]
        for oh in range(2):
            if oh == 1:
                s.wait_ge(kb.sems["pe"], kb.cnt["pe"])
            kb.inc(s.dma_start(
                out=wsb[0:1, 5632:5632 + 2048],
                in_=r2(S2_SDL, S2_ICPR, 4096)[ic:ic + 1, oh * 2048:(oh + 1) * 2048]),
                "ld", 16)
            kb.inc(s.dma_start(
                out=wsb[0:1, 7680:7680 + 2048],
                in_=r2(S2_SDL2, S2_ICPR, 4096)[ic:ic + 1, oh * 2048:(oh + 1) * 2048]),
                "ld", 16)
            kb.inc(s.dma_start(
                out=wsb[0:1, 9728:9728 + 2048],
                in_=r2(S2_ZDL, S2_ICPR, 4096)[ic:ic + 1, oh * 2048:(oh + 1) * 2048]),
                "ld", 16)
            v_in = kb.cnt["ld"]
            ve.wait_ge(kb.sems["ld"], v_qd)
            if v_row_prev is not None:
                ve.wait_ge(kb.sems["ch_vector"], v_row_prev)
            _unpack_strided(ve, wsu[:, oh * 512:(oh + 1) * 512],
                            wsu[:, 1024:1024 + 2048])
            kb.inc(ve.engine_nop(), "ve", 1)
            for q in range(4):
                oc = oh * 2048 + q * 512
                t_.wait_ge(kb.sems["ld"], v_in)
                if v_row_prev is not None:
                    t_.wait_ge(kb.sems["ch_vector"], v_row_prev)
                t_.matmul(pC2[:, 0:512], kb.sb["ones_bf"][0:1, :],
                          wsb[0:1, 5632 + q * 512:5632 + (q + 1) * 512],
                          start=True, stop=False)
                t_.matmul(pC2[:, 0:512], kb.sb["ones_bf"][0:1, :],
                          wsb[0:1, 7680 + q * 512:7680 + (q + 1) * 512],
                          start=False, stop=True)
                kb.inc(t_.matmul(pC2[:, 512:1024], kb.sb["ones_bf"][0:1, :],
                                 wsb[0:1, 9728 + q * 512:9728 + (q + 1) * 512],
                                 start=True, stop=True), "pe", 1)
                v_bc = kb.cnt["pe"]
                ve.tensor_copy(wsf[:, 1536:1536 + 512],
                               wsu[:, 1024 + q * 512:1024 + (q + 1) * 512])
                ve.wait_ge(kb.sems["pe"], v_bc)
                if v_dstore is not None:
                    ve.wait_ge(kb.sems["gd"], v_dstore)
                ve.tensor_sub(wsf[:, 1536:1536 + 512], wsf[:, 1536:1536 + 512],
                              pC2[:, 512:1024])
                kb.inc(ve.tensor_mul(wsb[:, 4096:4096 + 512],
                                     wsf[:, 1536:1536 + 512], pC2[:, 0:512]), "ve", 1)
                v_row_prev = kb.cnt["ve"]
                g.wait_ge(kb.sems["ch_vector"], v_row_prev)
                kb.inc(g.dma_start(out=cwd[ic, :, oc: oc + 512],
                                   in_=wsb[:, 4096:4096 + 512]), "gd", 16)
                v_dstore = kb.cnt["gd"]

    g.wait_ge(kb.sems["gd"], kb.cnt["gd"])
    scr2d = kb.dram["scr"].ap()
    inst = g.collective_compute(
        "AllGather", ALU.bypass, replica_groups=ALLG,
        ins=[scr2d[S2_CWG // 512: S2_CWG // 512 + 22016, :].opt()],
        outs=[kb.dram["scrW"].ap()[0:176128, :].opt()])
    kb.inc(inst, "cc", 1)
    g.wait_ge(kb.sems["cc"], kb.cnt["cc"])
    inst = g.collective_compute(
        "AllGather", ALU.bypass, replica_groups=ALLG,
        ins=[scr2d[S2_CWD // 512: S2_CWD // 512 + 11264, :].opt()],
        outs=[scr2d[S2_WDF // 512: S2_WDF // 512 + 90112, :].opt()])
    kb.inc(inst, "cc", 1)
    g.wait_ge(kb.sems["cc"], kb.cnt["cc"])
    return kb.cnt["cc"]


def emit_s2(kb):
    pA, pB, pC2, pE = kb.ps["pA"], kb.ps["pB"], kb.ps["pC2"], kb.ps["pE"]
    ws = kb.sb["ws"]
    wsb = ws.bitcast(BF16)
    scr = kb.dram["scr"].ap().rearrange("r c -> (r c)")
    WG5 = kb.dram["scrW"].ap().rearrange("r c -> (r c)")[0:32 * 128 * 22016] \
        .rearrange("(kc p c) -> kc p c", p=128, c=22016)
    WD5 = scr[S2_WDF: S2_WDF + 88 * 128 * 4096].rearrange(
        "(s p c) -> s p c", p=128, c=4096)

    v_deq = emit_s2_dequant(kb)

    v_mm_hist = []
    v_ev_hist = []
    for h in range(2):
        ydone = emit_yprod(kb, h)
        t_, ve, ac, s, g = kb.bodies("tensor", "vector", "scalar", "sync", "gpsimd")
        s.wait_ge(kb.sems["cc"], v_deq)
        for nc in range(NIC):
            gnc = h * NIC + nc
            pg = pA if nc % 2 == 0 else pB
            for kh in range(2):
                bi = (nc * 2 + kh) % 2
                B0 = bi * 4096
                if len(v_mm_hist) >= 2:
                    s.wait_ge(kb.sems["pe"], v_mm_hist[-2])
                kb.inc(s.dma_start(
                    out=wsb[:, B0:B0 + 2048].rearrange("p (k m) -> p k m", m=128),
                    in_=WG5[kh * 16:kh * 16 + 16, :, nc * 128:(nc + 1) * 128]
                    .rearrange("kc p c -> p kc c")), "ld", 16)
                kb.inc(s.dma_start(
                    out=wsb[:, B0 + 2048:B0 + 4096].rearrange("p (k m) -> p k m", m=128),
                    in_=WG5[kh * 16:kh * 16 + 16, :, IN2 + nc * 128:IN2 + (nc + 1) * 128]
                    .rearrange("kc p c -> p kc c")), "ld", 16)
                v_ld = kb.cnt["ld"]
                t_.wait_ge(kb.sems["ld"], v_ld)
                if nc == 0 and kh == 0:
                    t_.wait_ge(kb.sems["ve"], ydone)
                if gnc >= 2 and kh == 0:
                    t_.wait_ge(kb.sems["ve"], v_ev_hist[gnc - 2])
                last = None
                for k in range(16):
                    kc = kh * 16 + k
                    last = t_.matmul(pg[:, 0:512], wsb[:, B0 + k * 128:B0 + (k + 1) * 128],
                                     kb.sb["yh"][:, kc * TH:(kc + 1) * TH],
                                     start=(kc == 0), stop=(kc == 31))
                    last = t_.matmul(pg[:, 512:1024],
                                     wsb[:, B0 + 2048 + k * 128:B0 + 2048 + (k + 1) * 128],
                                     kb.sb["yh"][:, kc * TH:(kc + 1) * TH],
                                     start=(kc == 0), stop=(kc == 31))
                kb.inc(last, "pe", 1)
                v_mm_hist.append(kb.cnt["pe"])
            v_mm = kb.cnt["pe"]
            ac.wait_ge(kb.sems["pe"], v_mm)
            wsf = kb.sb["ws"].bitcast(F32)
            kb.inc(ac.activation(out=wsf[:, 4096:4096 + 512], in_=pg[:, 0:512],
                                 func=AF.Silu), "ac", 1)
            hb = 9216 + (nc % 2) * 512
            ve.wait_ge(kb.sems["ac"], kb.cnt["ac"])
            ve.wait_ge(kb.sems["pe"], v_mm)
            if nc >= 2:
                ve.wait_ge(kb.sems["gd"], kb.cnt["gd"])
            kb.inc(ve.tensor_mul(wsb[:, hb:hb + 512], wsf[:, 4096:4096 + 512],
                                 pg[:, 512:1024]), "ve", 1)
            v_ev_hist.append(kb.cnt["ve"])
            g.wait_ge(kb.sems["ch_vector"], kb.cnt["ve"])
            row_view = scr[S2_H0 + nc * 128 * 1024: S2_H0 + (nc + 1) * 128 * 1024] \
                .rearrange("(r t) -> r t", t=1024)
            rv = row_view.rearrange("(c b j) t -> j c b t", c=16, b=2, j=4)
            kb.inc(g.dma_start(out=rv[:, :, :, h * 512:(h + 1) * 512],
                               in_=wsb[:, hb:hb + 512]), "gd", 16)

    v_evB = None
    for h in range(2):
        t_, ve, ac, s = kb.bodies("tensor", "vector", "scalar", "sync")
        s.wait_ge(kb.sems["gd"], kb.cnt["gd"])
        stats_st = kb._s2_stats if h == 1 else {}
        v_prev = kb.cnt["pe"]
        for mg in range(8):
            v_ld_hist = []
            for ic in range(NIC):
                bi = ic % 2
                s.wait_ge(kb.sems["pe"], v_prev)
                s.wait_ge(kb.sems["pe"], kb.cnt["pe"])
                if len(v_ld_hist) >= 2:
                    s.wait_ge(kb.sems["pe"], v_ld_hist[-2][1])
                if ic == 0 and v_evB is not None:
                    t_.wait_ge(kb.sems["ve"], v_evB)
                kb.inc(s.dma_start(out=wsb[:, bi * 512:bi * 512 + 512],
                                   in_=WD5[ic, :, mg * 512:(mg + 1) * 512]), "ld", 16)
                hsrc = scr[S2_H0 + ic * 128 * 1024: S2_H0 + (ic + 1) * 128 * 1024] \
                    .rearrange("(r t) -> r t", t=1024)
                kb.inc(s.dma_start(out=wsb[:, 1024 + bi * 512:1024 + bi * 512 + 512],
                                   in_=hsrc[:, h * 512:(h + 1) * 512]), "ld", 16)
                v_ld = kb.cnt["ld"]
                t_.wait_ge(kb.sems["ld"], v_ld)
                if mg == 0 and ic == 0 and h == 0:
                    t_.wait_ge(kb.sems["gd"], kb.cnt["gd"])
                    t_.wait_ge(kb.sems["ve"], v_ev_hist[-1])
                last = None
                for i in range(4):
                    pt = pA if i < 2 else pB
                    c0 = (i % 2) * 512
                    last = t_.matmul(pt[:, c0:c0 + 512],
                                     wsb[:, bi * 512 + i * 128:bi * 512 + (i + 1) * 128],
                                     wsb[:, 1024 + bi * 512:1024 + bi * 512 + 512],
                                     start=(ic == 0), stop=(ic == NIC - 1))
                kb.inc(last, "pe", 1)
                v_ld_hist.append((v_ld, kb.cnt["pe"]))
            v_out = kb.cnt["pe"]
            ve.wait_ge(kb.sems["pe"], v_out)
            for i in range(4):
                m = mg * 4 + i
                pt = pA if i < 2 else pB
                c0 = (i % 2) * 512
                xs = kb.sb["xres"][:, m * T + h * TH: m * T + h * TH + TH]
                kb.inc(ve.tensor_add(xs, xs, pt[:, c0:c0 + 512]), "ve", 1)
                v_evB = kb.cnt["ve"]
                emit_stats_chunk(kb, ac, ve, t_, m, h, first=(mg == 0 and i == 0),
                                 st=stats_st)
            v_prev = kb.cnt["pe"]
        kb._s2_stats = stats_st



def att_consts(core, CP):
    r = core % 4
    pos = np.arange(r * T, (r + 1) * T, dtype=np.float32)
    inv = 1.0 / (10000.0 ** (np.arange(64, dtype=np.float32) / 64.0))
    ang = inv[:, None] * pos[None, :]
    cos = np.concatenate([np.cos(ang), np.cos(ang)], axis=0)
    sin = np.concatenate([-np.sin(ang), np.sin(ang)], axis=0)
    CP.add("rope_cos", cos.astype(np.float32))
    CP.add("rope_sin", sin.astype(np.float32))
    m = np.zeros((128, 8 * 8 * 512), np.float32)
    for qt in range(8):
        q0 = r * T + qt * 128
        for kb in range(8):
            k0 = kb * 512
            blk = np.zeros((128, 512), np.float32)
            qi = q0 + np.arange(128)[:, None]
            ki = k0 + np.arange(512)[None, :]
            blk[qi < ki] = -1e9
            m[:, (qt * 8 + kb) * 512:(qt * 8 + kb + 1) * 512] = blk
    u = np.ascontiguousarray(m.astype(BF)).view(np.uint16).reshape(-1, 2)
    packed = np.ascontiguousarray(u).view(np.uint32).reshape(-1).view(np.float32)
    CP.add_flat("m1", packed)
    return


def pack_s1(inputs, PB):
    PB.add("w1q", img_lhsT(np.asarray(inputs["L0_attn_a_Wq"], np.float32)))
    PB.add("w1k", img_lhsT(np.asarray(inputs["L0_attn_a_Wk"], np.float32)))
    PB.add("w1v", img_lhsT(np.asarray(inputs["L0_attn_a_Wv"], np.float32)))
    PB.add("w1o", img_lhsT(np.asarray(inputs["L0_attn_a_Wo"], np.float32)))


def pack_s7(inputs, PB, PF):
    PB.add("w7qkv", img_lhsT(np.asarray(inputs["L1_attn_b_qkv_w"], np.float32)))
    PB.add("w7o", img_lhsT(np.asarray(inputs["L1_attn_b_out_w"], np.float32)))
    PF.add_flat("b7qkv", np.asarray(inputs["L1_attn_b_qkv_b"], np.float32).reshape(-1))
    PF.add_flat("b7o", np.asarray(inputs["L1_attn_b_out_b"], np.float32).reshape(-1))


def emit_proj_epi_rope(kb, ve, pg, c0, cosb, sinb, tsc2, dst_bf, v_mm):
    tsc = kb.sb["tscr"]
    ve.wait_ge(kb.sems["pe"], v_mm)
    ve.tensor_mul(tsc[:, :], pg[:, c0:c0 + TH], cosb)
    ve.tensor_mul(tsc2[0:64, :], pg[64:128, c0:c0 + TH], sinb[0:64, :])
    ve.tensor_mul(tsc2[64:128, :], pg[0:64, c0:c0 + TH], sinb[64:128, :])
    return ve.tensor_add(dst_bf, tsc[:, :], tsc2[:, :])


def emit_outproj(kb, wname, src_dram_off, stats_first, bias_name=None, nk=NCH):
    pA, pB = kb.ps["pA"], kb.ps["pB"]
    ws = kb.sb["ws"]
    wsb = ws.bitcast(BF16)
    scr = kb.dram["scr"].ap().rearrange("r c -> (r c)")
    src = scr[src_dram_off: src_dram_off + nk * 128 * 1024].rearrange(
        "(k p t) -> k p t", p=128, t=1024)
    wsf = ws.bitcast(F32)
    brow = wsf[:, 1024:1028]
    if bias_name is not None:
        bflat = kb.wflat("F", bias_name)
    v_ev = None
    stats_st = {}
    v_src_ready = kb.cnt["gd"]
    for h in range(2):
        t_, ve, ac, s = kb.bodies("tensor", "vector", "scalar", "sync")
        s.wait_ge(kb.sems["gd"], v_src_ready)
        v_prev = kb.cnt["pe"]
        for mg in range(8):
            if bias_name is not None:
                s.wait_ge(kb.sems["ch_vector"], kb.cnt["ve"])
                kb.inc(s.dma_start(
                    out=brow,
                    in_=bflat[mg * 512:(mg + 1) * 512]
                    .rearrange("(m p) -> p m", p=128),
                    allow_slow_non_contiguous=True), "ld", 16)
            v_ld_hist = []
            for k in range(nk):
                bi = k % 2
                s.wait_ge(kb.sems["pe"], v_prev)
                if len(v_ld_hist) >= 2:
                    s.wait_ge(kb.sems["pe"], v_ld_hist[-2])
                kb.inc(s.dma_start(
                    out=wsb[:, bi * 512:bi * 512 + 512],
                    in_=kb.wreg("B", wname, c0=k * 4096 + mg * 512, cols=512)),
                    "ld", 16)
                kb.inc(s.dma_start(
                    out=wsb[:, 1024 + bi * 512:1024 + bi * 512 + 512],
                    in_=src[k, :, h * TH:(h + 1) * TH]), "ld", 16)
                v_ld = kb.cnt["ld"]
                t_.wait_ge(kb.sems["ld"], v_ld)
                if k == 0 and v_ev is not None:
                    t_.wait_ge(kb.sems["ve"], v_ev)
                last = None
                for i in range(4):
                    pt = pA if i < 2 else pB
                    c0 = (i % 2) * 512
                    last = t_.matmul(pt[:, c0:c0 + 512],
                                     wsb[:, bi * 512 + i * 128:bi * 512 + (i + 1) * 128],
                                     wsb[:, 1024 + bi * 512:1024 + bi * 512 + 512],
                                     start=(k == 0), stop=(k == nk - 1))
                kb.inc(last, "pe", 1)
                v_ld_hist.append(kb.cnt["pe"])
            v_out = kb.cnt["pe"]
            ve.wait_ge(kb.sems["pe"], v_out)
            if bias_name is not None:
                ve.wait_ge(kb.sems["ld"], kb.cnt["ld"])
            for i in range(4):
                m = mg * 4 + i
                pt = pA if i < 2 else pB
                c0 = (i % 2) * 512
                xs = kb.sb["xres"][:, m * T + h * TH: m * T + h * TH + TH]
                if bias_name is not None:
                    ve.tensor_scalar_add(kb.sb["tscr"][:, :], pt[:, c0:c0 + 512],
                                         brow[:, i:i + 1])
                    kb.inc(ve.tensor_add(xs, xs, kb.sb["tscr"][:, :]), "ve", 1)
                else:
                    kb.inc(ve.tensor_add(xs, xs, pt[:, c0:c0 + 512]), "ve", 1)
                v_ev = kb.cnt["ve"]
                emit_stats_chunk(kb, ac, ve, t_, m, h,
                                 first=(stats_first and mg == 0 and i == 0 and h == 0)
                                 or (stats_first and mg == 0 and i == 0 and h == 1),
                                 st=stats_st)
            v_prev = kb.cnt["pe"]


S1_Q, S1_AO, S1_CCI, S1_CCO = 0, 4194304, 8388608, 10485760
S1_KPER = 2097152
S1_END = 18874368
S7_Q, S7_AO, S7_CCI, S7_CCO = 0, 4194304, 8388608, 16777216
S7_KPER = 8388608
S7_END = 50331648


def emit_attn_proj(kb, wspec, nq, nkv, qoff, ccin_off, bias_name=None,
                   tabs=("rope_cos", "rope_sin")):
    pA, pB, pT = kb.ps["pA"], kb.ps["pB"], kb.ps["pT"]
    ws = kb.sb["ws"]
    wsb = ws.bitcast(BF16)
    wsf = ws.bitcast(F32)
    scr = kb.dram["scr"].ap().rearrange("r c -> (r c)")
    ntot = nq + 2 * nkv
    cosb = wsf[:, 4096:4096 + 512]
    sinb = wsf[:, 4608:4608 + 512]
    tsc2 = wsf[:, 5632:5632 + 512]
    rout = wsb[:, 12288:12288 + 512]
    vstage = wsb[:, 12800:12800 + 512]
    ptse = wsb[:, 13312:13312 + 512]
    btile = wsb[:, 13824:13824 + 128]
    v_mm_hist = {}
    v_done = {}
    for h in range(2):
        ydone = emit_yprod(kb, h)
        t_, ve, s, g = kb.bodies("tensor", "vector", "sync", "gpsimd")
        s.wait_ge(kb.sems["cc"], kb.cnt["cc"])
        if h == 1:
            s.wait_ge(kb.sems["ch_vector"], kb.cnt["ve"])
        kb.inc(s.dma_start(out=cosb, in_=kb.creg(tabs[0], c0=h * TH, cols=TH)),
               "ld", 16)
        kb.inc(s.dma_start(out=sinb, in_=kb.creg(tabs[1], c0=h * TH, cols=TH)),
               "ld", 16)
        v_tab = kb.cnt["ld"]
        for oc in range(ntot):
            wname, M, col0 = wspec(oc)
            bi = oc % 2
            B0 = bi * 4096
            goc = h * ntot + oc
            if goc >= 2:
                s.wait_ge(kb.sems["pe"], v_mm_hist[goc - 2])
            img = kb.wreg("B", wname)
            strip = img.rearrange("p (k m) -> p k m", m=M)[:, :, col0:col0 + 128]
            kb.inc(s.dma_start(
                out=wsb[:, B0:B0 + 4096].rearrange("p (k m) -> p k m", m=128),
                in_=strip), "ld", 16)
            if bias_name is not None:
                if goc >= 1:
                    g.wait_ge(kb.sems["pe"], v_mm_hist[goc - 1])
                kb.inc(g.dma_start(
                    out=btile[0:1, 0:128],
                    in_=kb.wflat("F", bias_name)[oc * 128:(oc + 1) * 128]
                    .rearrange("(one m) -> one m", one=1)), "gd", 16)
            v_ld = kb.cnt["ld"]
            v_gd = kb.cnt["gd"]
            pg = pA if oc % 2 == 0 else pB
            t_.wait_ge(kb.sems["ld"], v_ld)
            if bias_name is not None:
                t_.wait_ge(kb.sems["gd"], v_gd)
            if oc == 0:
                t_.wait_ge(kb.sems["ve"], ydone)
                t_.wait_ge(kb.sems["ld"], v_tab)
            if goc >= 2:
                t_.wait_ge(kb.sems["ve"], v_done[goc - 2])
            last = None
            for k in range(NCH):
                last = t_.matmul(pg[:, 0:512], wsb[:, B0 + k * 128:B0 + (k + 1) * 128],
                                 kb.sb["yh"][:, k * TH:(k + 1) * TH],
                                 start=(k == 0), stop=(k == 31 and bias_name is None))
            if bias_name is not None:
                last = t_.matmul(pg[:, 0:512], btile[0:1, 0:128],
                                 kb.sb["rowsb"][0:1, 0:512], start=False, stop=True)
            kb.inc(last, "pe", 1)
            v_mm = kb.cnt["pe"]
            v_mm_hist[goc] = v_mm
            if oc < nq + nkv:
                kb.inc(emit_proj_epi_rope(kb, ve, pg, 0, cosb, sinb, tsc2, rout, v_mm),
                       "ve", 1)
                g.wait_ge(kb.sems["ch_vector"], kb.cnt["ve"])
                if oc < nq:
                    base = qoff + oc * 131072
                else:
                    base = ccin_off + (oc - nq) * 131072
                dv = scr[base: base + 131072].rearrange("(d t) -> d t", t=1024)
                kb.inc(g.dma_start(out=dv[:, h * TH:(h + 1) * TH], in_=rout), "gd", 16)
            else:
                kv = oc - nq - nkv
                ve.wait_ge(kb.sems["pe"], v_mm)
                kb.inc(ve.tensor_copy(vstage, pg[:, 0:512]), "ve", 1)
                t_.wait_ge(kb.sems["ve"], kb.cnt["ve"])
                t_.wait_ge(kb.sems["po"], kb.cnt["po"])
                lastt = None
                for tt in range(4):
                    lastt = t_.transpose(pT[:, tt * 128:(tt + 1) * 128],
                                         vstage[:, tt * 128:(tt + 1) * 128],
                                         kb.sb["id_bf"][:, :])
                kb.inc(lastt, "pe", 1)
                v_mm_hist[goc] = kb.cnt["pe"]
                ve.wait_ge(kb.sems["pe"], kb.cnt["pe"])
                kb.inc(ve.tensor_copy(ptse, pT[:, 0:512]), "ve", 1)
                g.wait_ge(kb.sems["ch_vector"], kb.cnt["ve"])
                base = ccin_off + (nkv + kv) * 131072
                dv = scr[base: base + 131072]
                kb.inc(g.dma_start(
                    out=dv.rearrange("(tt p d) -> p tt d", p=128, d=128)[
                        :, h * 4:h * 4 + 4, :],
                    in_=ptse.rearrange("p (tt d) -> p tt d", d=128)), "gd", 16)
            v_done[goc] = kb.cnt["ve"]


def emit_kv_ag(kb, ccin_off, ccout_off, nelem):
    (g,) = kb.bodies("gpsimd")
    g.wait_ge(kb.sems["gd"], kb.cnt["gd"])
    g.wait_ge(kb.sems["ch_vector"], kb.cnt["ve"])
    scr2d = kb.dram["scr"].ap()
    inst = g.collective_compute(
        "AllGather", ALU.bypass, replica_groups=GROUPS,
        ins=[scr2d[ccin_off // 512: (ccin_off + nelem) // 512, :].opt()],
        outs=[scr2d[ccout_off // 512: (ccout_off + 4 * nelem) // 512, :].opt()])
    kb.inc(inst, "cc", 1)
    g.wait_ge(kb.sems["cc"], kb.cnt["cc"])
    return kb.cnt["cc"]


def emit_attn_core(kb, nkv, qpk, qoff, ao_off, ccout_off, kper, scale, v_ag,
                   dump_at=None):
    pA, pC2, pT = kb.ps["pA"], kb.ps["pC2"], kb.ps["pT"]
    ws = kb.sb["ws"]
    wsb = ws.bitcast(BF16)
    wsf = ws.bitcast(F32)
    yb = kb.sb["yh"]
    mbuf = yb[:, 0:4096]
    scb = yb.bitcast(F32)[:, 2048:2048 + 4096]
    Pb = yb[:, 12288:12288 + 4096]
    Kw = wsb[:, 0:4096]
    Vw = wsb[:, 4096:8192]
    Qw = wsb[:, 8192:8192 + qpk * 1024]
    pts = wsb[:, 13312:13312 + 512]
    aost = wsb[:, 14336:14336 + 1024]
    m1 = wsf[:, 7720:7721]
    nm1 = wsf[:, 7721:7722]
    lrow = wsf[:, 7722:7723]
    rl = wsf[:, 7723:7724]
    scr = kb.dram["scr"].ap().rearrange("r c -> (r c)")
    t_, ve, ac, s, g = kb.bodies("tensor", "vector", "scalar", "sync", "gpsimd")
    s.wait_ge(kb.sems["cc"], v_ag)
    v_sc_ev = None
    v_pv_ev = None
    v_ao = None
    v_head_mm = 0
    for kv in range(nkv):
        s.wait_ge(kb.sems["pe"], v_head_mm)
        for r in range(4):
            src = scr[ccout_off + r * kper + kv * 131072:
                      ccout_off + r * kper + (kv + 1) * 131072] \
                .rearrange("(d t) -> d t", t=1024)
            kb.inc(s.dma_start(out=Kw[:, r * 1024:(r + 1) * 1024], in_=src), "ld", 16)
            vsrc = scr[ccout_off + r * kper + (nkv + kv) * 131072:
                       ccout_off + r * kper + (nkv + kv + 1) * 131072] \
                .rearrange("(tt p d) -> p tt d", p=128, d=128)
            kb.inc(s.dma_start(
                out=Vw.rearrange("p (rr tt d) -> p rr tt d", rr=4, tt=8)[:, r, :, :],
                in_=vsrc), "ld", 16)
        for gq in range(qpk):
            hq = kv * qpk + gq
            kb.inc(s.dma_start(
                out=Qw[:, gq * 1024:(gq + 1) * 1024],
                in_=scr[qoff + hq * 131072: qoff + (hq + 1) * 131072]
                .rearrange("(d t) -> d t", t=1024)), "ld", 16)
        v_kv = kb.cnt["ld"]
        mreg = kb.creg("m1").bitcast(BF16).rearrange("p (qq c) -> p qq c", c=4096)
        for gq in range(qpk):
            hq = kv * qpk + gq
            for qt in range(8):
                if v_sc_ev is not None:
                    s.wait_ge(kb.sems["ch_vector"], v_sc_ev)
                kb.inc(s.dma_start(out=mbuf, in_=mreg[:, qt, :]), "ld", 16)
                v_mk = kb.cnt["ld"]
                ev_list = []
                for kb_ in range(8):
                    pc0 = (kb_ % 2) * 512
                    t_.wait_ge(kb.sems["ld"], v_kv)
                    if kb_ >= 2:
                        t_.wait_ge(kb.sems["ch_vector"], ev_list[kb_ - 2])
                    elif v_sc_ev is not None:
                        t_.wait_ge(kb.sems["ch_vector"], v_sc_ev)
                    kb.inc(t_.matmul(
                        pC2[:, pc0:pc0 + 512],
                        Qw[:, gq * 1024 + qt * 128:gq * 1024 + (qt + 1) * 128],
                        Kw[:, kb_ * 512:(kb_ + 1) * 512],
                        start=True, stop=True), "pe", 1)
                    v_s = kb.cnt["pe"]
                    ve.wait_ge(kb.sems["pe"], v_s)
                    if kb_ == 0:
                        ve.wait_ge(kb.sems["ld"], v_mk)
                        ve.wait_ge(kb.sems["ac"], kb.cnt["ac"])
                    kb.inc(ve.tensor_add(scb[:, kb_ * 512:(kb_ + 1) * 512],
                                         pC2[:, pc0:pc0 + 512],
                                         mbuf[:, kb_ * 512:(kb_ + 1) * 512]), "ve", 1)
                    ev_list.append(kb.cnt["ve"])
                v_sc_ev = kb.cnt["ve"]
                ve.reduce_max(m1, scb, axis=AX)
                kb.inc(ve.tensor_scalar_mul(nm1, m1, -scale), "ve", 1)
                ac.wait_ge(kb.sems["ve"], kb.cnt["ve"])
                ac.wait_ge(kb.sems["pe"], kb.cnt["pe"])
                kb.inc(ac.activation(out=Pb, in_=scb, func=AF.Exp, bias=nm1,
                                     scale=scale, accum_out=lrow), "ac", 1)
                ve.wait_ge(kb.sems["ac"], kb.cnt["ac"])
                ve.reciprocal(rl, lrow)
                kb.inc(ve.tensor_scalar_mul(Pb, Pb, rl), "ve", 1)
                v_pn = kb.cnt["ve"]
                for kc in range(32):
                    tj = kc % 4
                    if tj == 0:
                        if kc > 0:
                            t_.wait_ge(kb.sems["ch_vector"], v_pv_ev)
                        else:
                            t_.wait_ge(kb.sems["ve"], v_pn)
                            t_.wait_ge(kb.sems["po"], kb.cnt["po"])
                            if v_ao is not None:
                                t_.wait_ge(kb.sems["ch_vector"], v_ao)
                    ti = t_.transpose(pT[:, tj * 128:(tj + 1) * 128],
                                      Pb[:, kc * 128:(kc + 1) * 128],
                                      kb.sb["id_bf"][:, :])
                    if tj == 3:
                        kb.inc(ti, "pe", 1)
                        v_tp = kb.cnt["pe"]
                        ve.wait_ge(kb.sems["pe"], v_tp)
                        kb.inc(ve.tensor_copy(pts, pT[:, 0:512]), "ve", 1)
                        v_pv_ev = kb.cnt["ve"]
                        t_.wait_ge(kb.sems["ch_vector"], v_pv_ev)
                        last = None
                        for j in range(4):
                            kcj = kc - 3 + j
                            last = t_.matmul(pA[:, 0:128],
                                             Vw[:, kcj * 128:(kcj + 1) * 128],
                                             pts[:, j * 128:(j + 1) * 128],
                                             start=(kcj == 0), stop=(kcj == 31))
                        kb.inc(last, "pe", 1)
                v_pv = kb.cnt["pe"]
                v_head_mm = v_pv
                ve.wait_ge(kb.sems["pe"], v_pv)
                if qt == 0:
                    ve.wait_ge(kb.sems["gd"], kb.cnt["gd"])
                kb.inc(ve.tensor_copy(aost[:, qt * 128:(qt + 1) * 128], pA[:, 0:128]),
                       "ve", 1)
                v_ao = kb.cnt["ve"]
                if dump_at is not None and dump_at[:3] == (kv, gq, qt):
                    xob = kb.dram["xO"].ap().bitcast(BF16)
                    xof = kb.dram["xO"].ap()
                    (sd,) = kb.bodies("sync")
                    sd.wait_ge(kb.sems["ch_vector"], kb.cnt["ve"])
                    kb.inc(sd.dma_start(out=xof[3000:3001, 0:128], in_=lrow), "ld", 16)
                    kb.inc(sd.dma_start(out=xof[3001:3002, 0:128], in_=rl), "ld", 16)
                    kb.inc(sd.dma_start(out=xof[3002:3130, 0:1024],
                                        in_=scb[:, 0:1024]), "ld", 16)
                    kb.inc(sd.dma_start(out=xob[3130:3386, 0:2048]
                                        .rearrange("(a b) c -> a (b c)", b=2), in_=Pb),
                           "ld", 16)
                    kb.inc(sd.dma_start(out=xob[3400:3528, 0:1024],
                                        in_=mbuf[:, 0:1024]), "ld", 16)
                    sd.wait_ge(kb.sems["ld"], kb.cnt["ld"])
            g.wait_ge(kb.sems["ch_vector"], kb.cnt["ve"])
            kb.inc(g.dma_start(
                out=scr[ao_off + hq * 131072: ao_off + (hq + 1) * 131072]
                .rearrange("(d t) -> d t", t=1024),
                in_=aost), "gd", 16)
    return


def emit_s1(kb):
    def wspec(oc):
        if oc < 32:
            return ("w1q", 4096, oc * 128)
        if oc < 40:
            return ("w1k", 1024, (oc - 32) * 128)
        return ("w1v", 1024, (oc - 40) * 128)
    emit_attn_proj(kb, wspec, 32, 8, S1_Q, S1_CCI)
    v_ag = emit_kv_ag(kb, S1_CCI, S1_CCO, S1_KPER)
    emit_attn_core(kb, 8, 4, S1_Q, S1_AO, S1_CCO, S1_KPER,
                   1.0 / float(np.sqrt(128.0)), v_ag)
    emit_outproj(kb, "w1o", S1_AO, stats_first=True)


def emit_s7(kb):
    def wspec(oc):
        return ("w7qkv", 12288, oc * 128)
    emit_attn_proj(kb, wspec, 32, 32, S7_Q, S7_CCI, bias_name="b7qkv")
    v_ag = emit_kv_ag(kb, S7_CCI, S7_CCO, S7_KPER)
    emit_attn_core(kb, 32, 1, S7_Q, S7_AO, S7_CCO, S7_KPER,
                   1.0 / float(np.sqrt(128.0)), v_ag)
    emit_outproj(kb, "w7o", S7_AO, stats_first=True, bias_name="b7o")



S3_QKV = 0
S3_O2 = 24 * 131072
S3_END = S3_O2 + 8 * 131072


def pack_s3(inputs, PB, PF):
    cw = np.asarray(inputs["L0_attn_b_conv_w"], np.float32)
    for k in range(3):
        PB.add("w3c%d" % k, img_lhsT(np.ascontiguousarray(cw[k])))
    PB.add("w3q", img_lhsT(
        np.asarray(inputs["L0_attn_b_wq"], np.float32).reshape(1024, 1024)))
    PB.add("w3k", img_lhsT(
        np.asarray(inputs["L0_attn_b_wk"], np.float32).reshape(1024, 1024)))
    PB.add("w3v", img_lhsT(
        np.asarray(inputs["L0_attn_b_wv"], np.float32).reshape(1024, 1024)))
    PB.add("w3o", img_lhsT(
        np.asarray(inputs["L0_attn_b_wo"], np.float32).reshape(1024, 1024)))
    PB.add("w3p", img_lhsT(np.asarray(inputs["L0_attn_b_proj_w"], np.float32)))
    PF.add_flat("b3p", np.asarray(inputs["L0_attn_b_proj_b"], np.float32).reshape(-1))


def s3_consts(inputs, CP):
    CP.add("c3b", np.ascontiguousarray(
        np.asarray(inputs["L0_attn_b_conv_b"], np.float32).reshape(24, 128).T))
    bqk = np.zeros((128, 32), np.float32)
    bqk[0:64, 0:16] = np.asarray(inputs["L0_attn_b_bq"], np.float32).T
    bqk[0:64, 16:32] = np.asarray(inputs["L0_attn_b_bk"], np.float32).T
    CP.add("b3qk", bqk)
    CP.add_flat("b3v", np.asarray(inputs["L0_attn_b_bv"], np.float32).reshape(-1))
    CP.add("cb3o", np.ascontiguousarray(
        np.asarray(inputs["L0_attn_b_bo"], np.float32).reshape(8, 128).T))
    msk = np.zeros((128, 128), np.float32)
    qi = np.arange(128)[:, None]
    ki = np.arange(128)[None, :]
    msk[ki > qi] = -1e9
    CP.add("c3m", msk)


def emit_s3(kb):
    pA, pB, pC2, pE, pT = (kb.ps["pA"], kb.ps["pB"], kb.ps["pC2"],
                           kb.ps["pE"], kb.ps["pT"])
    ws = kb.sb["ws"]
    wsb = ws.bitcast(BF16)
    wsf = ws.bitcast(F32)
    yh = kb.sb["yh"]
    rowsb = kb.sb["rowsb"]
    xb = kb.sb["xb"]
    scr = kb.dram["scr"].ap().rearrange("r c -> (r c)")
    cflat = kb.cst.rearrange("r c -> (r c)")
    yhv = yh.rearrange("p (kc t) -> p kc t", t=TH)
    tcv = wsf[:, 6144:6656]
    cb = wsf[:, 6656:6680]
    hxf = wsf[:, 6688:6752]
    SR = 6752
    bc = wsf[:, 6768:6772]
    htmp = wsf[:, 6912:6944]
    hyb = wsb[:, 13568:13632]
    hxb = wsb[:, 13632:13696]
    hsq = wsb[:, 13696:13760]
    ecolL = wsb[:, 13760:13792]
    ecolR = wsb[:, 13792:13824]

    t_, ve, ac, s, g = kb.bodies("tensor", "vector", "scalar", "sync", "gpsimd")
    s.wait_ge(kb.sems["cc"], kb.cnt["cc"])
    kb.inc(s.dma_start(out=hxf, in_=kb.dram["xh"].ap()), "ld", 16)
    kb.inc(s.dma_start(out=cb, in_=kb.creg("c3b")), "ld", 16)
    v_hx = kb.cnt["ld"]
    ve.wait_ge(kb.sems["ld"], v_hx)
    ac.wait_ge(kb.sems["ld"], v_hx)
    kb.inc(ve.tensor_copy(hxb, hxf), "ve", 1)
    kb.inc(ac.activation(out=hsq, in_=hxf, func=AF.Square), "ac", 1)
    t_.wait_ge(kb.sems["ve"], kb.cnt["ve"])
    t_.wait_ge(kb.sems["ac"], kb.cnt["ac"])
    t_.matmul(pE[0:1, 0:64], kb.sb["ones_bf"][:, 0:1], hxb, start=True, stop=True)
    kb.inc(t_.matmul(pE[32:33, 0:64], kb.sb["ones_bf"][:, 0:1], hsq,
                     start=True, stop=True), "pe", 1)
    ve.wait_ge(kb.sems["pe"], kb.cnt["pe"])
    sc = lambda c, n=1: wsf[0:1, SR + c:SR + c + n]
    for t in range(2):
        ve.reduce_sum(sc(t), pE[0:1, t * 32:(t + 1) * 32], axis=AX)
        ve.reduce_sum(sc(2 + t), pE[32:33, t * 32:(t + 1) * 32], axis=AX)
    ve.tensor_scalar_mul(sc(4, 2), sc(0, 2), 1.0 / D)
    ve.tensor_scalar_mul(sc(6, 2), sc(2, 2), 1.0 / D)
    ve.tensor_mul(sc(8, 2), sc(4, 2), sc(4, 2))
    ve.tensor_sub(sc(10, 2), sc(6, 2), sc(8, 2))
    ve.tensor_scalar_add(sc(10, 2), sc(10, 2), 1e-5)
    kb.inc(ve.tensor_scalar_mul(sc(12, 2), sc(4, 2), -1.0), "ve", 1)
    ac.wait_ge(kb.sems["ve"], kb.cnt["ve"])
    kb.inc(ac.activation(out=sc(10, 2), in_=sc(10, 2), func=AF.Sqrt), "ac", 1)
    ve.wait_ge(kb.sems["ac"], kb.cnt["ac"])
    ve.reciprocal(sc(10, 2), sc(10, 2))
    ve.tensor_copy(rowsb[0:1, 512:514], sc(12, 2))
    kb.inc(ve.tensor_copy(rowsb[0:1, 514:516], sc(10, 2)), "ve", 1)
    t_.wait_ge(kb.sems["ve"], kb.cnt["ve"])
    last = None
    for j in range(2):
        t_.matmul(pE[:, 64 + j:65 + j], kb.sb["ones_bf"][0:1, 0:128],
                  rowsb[0:1, 512 + j:513 + j], start=True, stop=True)
        last = t_.matmul(pE[:, 66 + j:67 + j], kb.sb["ones_bf"][0:1, 0:128],
                         rowsb[0:1, 514 + j:515 + j], start=True, stop=True)
    kb.inc(last, "pe", 1)
    ve.wait_ge(kb.sems["pe"], kb.cnt["pe"])
    kb.inc(ve.tensor_copy(bc, pE[:, 64:68]), "ve", 1)
    for t in range(2):
        ve.tensor_scalar_add(htmp, hxf[:, t * 32:(t + 1) * 32], bc[:, t:t + 1])
        kb.inc(ve.tensor_scalar_mul(hyb[:, t * 32:(t + 1) * 32], htmp,
                                    bc[:, 2 + t:3 + t]), "ve", 1)

    emit_yprod(kb, 1)
    (ve,) = kb.bodies("vector")
    kb.inc(ve.tensor_copy(ecolR, yhv[:, :, 0:1]), "ve", 1)

    v_sp = None
    v_cmb = None
    for h in range(2):
        ydone = emit_yprod(kb, h)
        t_, ve, s, g = kb.bodies("tensor", "vector", "sync", "gpsimd")
        if h == 0:
            lcol = lambda kc: hyb[:, kc:kc + 1]
            rcol = lambda kc: ecolR[:, kc:kc + 1]
        else:
            lcol = lambda kc: ecolL[:, kc:kc + 1]
            rcol = lambda kc: hyb[:, 32 + kc:33 + kc]
        v_mm_prev = kb.cnt["pe"]
        for oc in range(24):
            s.wait_ge(kb.sems["pe"], v_mm_prev)
            for k3 in range(3):
                kb.inc(s.dma_start(
                    out=wsb[:, k3 * 4096:(k3 + 1) * 4096]
                    .rearrange("p (kc m) -> p kc m", m=128),
                    in_=kb.wreg("B", "w3c%d" % k3)
                    .rearrange("p (kc m) -> p kc m", m=3072)
                    [:, :, oc * 128:(oc + 1) * 128]), "ld", 16)
            v_ld = kb.cnt["ld"]
            t_.wait_ge(kb.sems["ld"], v_ld)
            if oc == 0:
                t_.wait_ge(kb.sems["ve"], ydone)
            else:
                t_.wait_ge(kb.sems["ch_vector"], v_cmb)
            for k3, pg in ((0, pA), (1, pB), (2, pC2)):
                for kc in range(NCH):
                    t_.matmul(pg[:, 0:512],
                              wsb[:, k3 * 4096 + kc * 128:k3 * 4096 + (kc + 1) * 128],
                              yh[:, kc * TH:(kc + 1) * TH],
                              start=(kc == 0), stop=(kc == 31))
            for kc in range(NCH):
                t_.matmul(pE[:, 0:1], wsb[:, kc * 128:(kc + 1) * 128],
                          lcol(kc), start=(kc == 0), stop=(kc == 31))
            last = None
            for kc in range(NCH):
                last = t_.matmul(pE[:, 1:2],
                                 wsb[:, 2 * 4096 + kc * 128:2 * 4096 + (kc + 1) * 128],
                                 rcol(kc), start=(kc == 0), stop=(kc == 31))
            kb.inc(last, "pe", 1)
            v_mm_prev = kb.cnt["pe"]
            ve.wait_ge(kb.sems["pe"], v_mm_prev)
            if v_sp is not None:
                ve.wait_ge(kb.sems["gd"], v_sp)
            ve.tensor_copy(tcv[:, 1:512], pA[:, 0:511])
            ve.tensor_copy(tcv[:, 0:1], pE[:, 0:1])
            ve.tensor_add(tcv, tcv, pB[:, 0:512])
            ve.tensor_add(tcv[:, 0:511], tcv[:, 0:511], pC2[:, 1:512])
            ve.tensor_add(tcv[:, 511:512], tcv[:, 511:512], pE[:, 1:2])
            kb.inc(ve.tensor_scalar_add(xb[:, 0:512], tcv, cb[:, oc:oc + 1]),
                   "ve", 1)
            v_cmb = kb.cnt["ve"]
            g.wait_ge(kb.sems["ch_vector"], v_cmb)
            kb.inc(g.dma_start(
                out=scr[S3_QKV + oc * 131072:S3_QKV + (oc + 1) * 131072]
                .rearrange("(p t) -> p t", t=1024)[:, h * TH:h * TH + TH],
                in_=xb[:, 0:512]), "gd", 16)
            v_sp = kb.cnt["gd"]
        if h == 0:
            kb.inc(ve.tensor_copy(ecolL, yhv[:, :, 511:512]), "ve", 1)

    t_, ve, ac, s, g = kb.bodies("tensor", "vector", "scalar", "sync", "gpsimd")
    qkv3 = wsb[:, 0:3072]
    qh_s = wsb[0:64, 3072:3200]
    kh_s = wsb[0:64, 3200:3328]
    vh_sb = wsb[:, 3328:3392]
    P3 = wsb[:, 3392:3520]
    PT3 = wsb[:, 3520:3648]
    opair = wsb[:, 3648:3776]
    o2b = wsb[:, 3776:4800]
    oall = wsb[:, 4800:5824]
    scf = wsf[:, 2912:3040]
    m3 = wsf[:, 3040:3041]
    nm3 = wsf[:, 3041:3042]
    l3 = wsf[:, 3042:3043]
    r3 = wsf[:, 3043:3044]
    msk3 = wsf[:, 3044:3172]
    cbo = wsf[:, 3204:3212]
    bvf = wsf[0:1, 3212:4236]
    bvh_f = wsf[0:1, 4236:5260]
    s.wait_ge(kb.sems["gd"], kb.cnt["gd"])
    s.wait_ge(kb.sems["ch_vector"], kb.cnt["ve"])
    kb.inc(s.dma_start(out=msk3, in_=kb.creg("c3m")), "ld", 16)
    kb.inc(s.dma_start(out=wsf[:, 3172:3204], in_=kb.creg("b3qk")), "ld", 16)
    kb.inc(s.dma_start(out=cbo, in_=kb.creg("cb3o")), "ld", 16)
    off_bv, _ = kb.catlas["b3v"]
    kb.inc(s.dma_start(out=bvf, in_=cflat[off_bv:off_bv + 1024]
                       .rearrange("(o m) -> o m", o=1)), "ld", 16)
    v_c2 = kb.cnt["ld"]
    ve.wait_ge(kb.sems["ld"], v_c2)
    ve.tensor_scalar_mul(wsf[0:64, 3172:3188], wsf[0:64, 3172:3188], 0.125)
    ve.tensor_copy(xb[0:1, 0:1024], bvf)
    ve.tensor_copy(bvh_f, xb[0:1, 0:1024])
    ve.tensor_sub(bvf, bvf, bvh_f)
    kb.inc(ve.tensor_copy(xb[0:1, 1024:2048], bvf), "ve", 1)
    v_bv = kb.cnt["ve"]

    w3qv = yh[:, 0:4096].rearrange("p (kc e) -> p kc e", e=512)
    w3kv = yh[:, 4096:8192].rearrange("p (kc e) -> p kc e", e=512)
    w3vv = yh[:, 8192:12288].rearrange("p (kc e) -> p kc e", e=512)
    v_blk_mm = kb.cnt["pe"]
    v_o2ev = None
    v_evac = None
    for b in range(8):
        s.wait_ge(kb.sems["pe"], v_blk_mm)
        for ch in range(24):
            kb.inc(s.dma_start(
                out=wsb[:, ch * 128:(ch + 1) * 128],
                in_=scr[S3_QKV + ch * 131072:S3_QKV + (ch + 1) * 131072]
                .rearrange("(p t) -> p t", t=1024)[:, b * 128:(b + 1) * 128]),
                "ld", 16)
        v_qkv = kb.cnt["ld"]
        for gph in range(2):
            s.wait_ge(kb.sems["pe"], kb.cnt["pe"])
            kb.inc(s.dma_start(
                out=yh[:, 0:4096].rearrange("p (kc e) -> p kc e", e=512),
                in_=kb.wreg("B", "w3q").rearrange("p (kc e) -> p kc e", e=1024)
                [:, :, gph * 512:(gph + 1) * 512]), "ld", 16)
            kb.inc(s.dma_start(
                out=yh[:, 4096:8192].rearrange("p (kc e) -> p kc e", e=512),
                in_=kb.wreg("B", "w3k").rearrange("p (kc e) -> p kc e", e=1024)
                [:, :, gph * 512:(gph + 1) * 512]), "ld", 16)
            kb.inc(s.dma_start(
                out=yh[:, 8192:12288].rearrange("p (kc e) -> p kc e", e=512),
                in_=kb.wreg("B", "w3v").rearrange("p (kc e) -> p kc e", e=1024)
                [:, :, gph * 512:(gph + 1) * 512]), "ld", 16)
            v_wg = kb.cnt["ld"]
            for h8 in range(8):
                hh = gph * 8 + h8
                t_.wait_ge(kb.sems["ld"], v_wg)
                if gph == 0 and h8 == 0:
                    t_.wait_ge(kb.sems["ld"], v_qkv)
                    t_.wait_ge(kb.sems["ve"], v_bv)
                if v_evac is not None:
                    t_.wait_ge(kb.sems["ch_vector"], v_evac)
                for kc8 in range(8):
                    t_.matmul(pE[0:64, 0:128], w3qv[:, kc8, h8 * 64:(h8 + 1) * 64],
                              qkv3[:, kc8 * 128:(kc8 + 1) * 128],
                              start=(kc8 == 0), stop=(kc8 == 7))
                for kc8 in range(8):
                    t_.matmul(pE[64:128, 0:128], w3kv[:, kc8, h8 * 64:(h8 + 1) * 64],
                              qkv3[:, (8 + kc8) * 128:(9 + kc8) * 128],
                              start=(kc8 == 0), stop=(kc8 == 7))
                for kc8 in range(8):
                    t_.matmul(pE[:, 384:448], qkv3[:, (16 + kc8) * 128:(17 + kc8) * 128],
                              w3vv[:, kc8, h8 * 64:(h8 + 1) * 64],
                              start=(kc8 == 0), stop=False)
                t_.matmul(pE[:, 384:448], kb.sb["rowsb"][0:1, 0:128],
                          xb[0:1, hh * 64:(hh + 1) * 64], start=False, stop=False)
                kb.inc(t_.matmul(pE[:, 384:448], kb.sb["rowsb"][0:1, 0:128],
                                 xb[0:1, 1024 + hh * 64:1024 + (hh + 1) * 64],
                                 start=False, stop=True), "pe", 1)
                v_proj = kb.cnt["pe"]
                ve.wait_ge(kb.sems["pe"], v_proj)
                ve.tensor_scalar(out=qh_s, in0=pE[0:64, 0:128], scalar1=0.125,
                                 scalar2=wsf[0:64, 3172 + hh:3173 + hh],
                                 op0=ALU.mult, op1=ALU.add)
                ve.tensor_scalar_add(kh_s, pE[64:128, 0:128],
                                     wsf[0:64, 3188 + hh:3189 + hh])
                kb.inc(ve.tensor_copy(vh_sb, pE[:, 384:448]), "ve", 1)
                v_evac = kb.cnt["ve"]
                t_.wait_ge(kb.sems["ch_vector"], v_evac)
                kb.inc(t_.matmul(pB[:, 0:128], qh_s, kh_s, start=True, stop=True),
                       "pe", 1)
                ve.wait_ge(kb.sems["pe"], kb.cnt["pe"])
                ve.wait_ge(kb.sems["ac"], kb.cnt["ac"])
                ve.tensor_add(scf, pB[:, 0:128], msk3)
                ve.reduce_max(m3, scf, axis=AX)
                kb.inc(ve.tensor_scalar_mul(nm3, m3, -1.0), "ve", 1)
                ac.wait_ge(kb.sems["ve"], kb.cnt["ve"])
                ac.wait_ge(kb.sems["pe"], kb.cnt["pe"])
                kb.inc(ac.activation(out=P3, in_=scf, func=AF.Exp, bias=nm3,
                                     scale=1.0, accum_out=l3), "ac", 1)
                ve.wait_ge(kb.sems["ac"], kb.cnt["ac"])
                ve.reciprocal(r3, l3)
                kb.inc(ve.tensor_scalar_mul(P3, P3, r3), "ve", 1)
                t_.wait_ge(kb.sems["ch_vector"], kb.cnt["ve"])
                t_.wait_ge(kb.sems["po"], kb.cnt["po"])
                kb.inc(t_.transpose(pT[:, 0:128], P3, kb.sb["id_bf"][:, :]),
                       "pe", 1)
                ve.wait_ge(kb.sems["pe"], kb.cnt["pe"])
                kb.inc(ve.tensor_copy(PT3, pT[:, 0:128]), "ve", 1)
                t_.wait_ge(kb.sems["ch_vector"], kb.cnt["ve"])
                kb.inc(t_.matmul(pC2[(h8 % 2) * 64:(h8 % 2) * 64 + 64, 0:128],
                                 vh_sb, PT3, start=True, stop=True), "pe", 1)
                v_blk_mm = kb.cnt["pe"]
                if h8 % 2 == 1:
                    jj = hh // 2
                    ve.wait_ge(kb.sems["pe"], v_blk_mm)
                    kb.inc(ve.tensor_copy(oall[:, jj * 128:(jj + 1) * 128],
                                          pC2[:, 0:128]), "ve", 1)
        s.wait_ge(kb.sems["pe"], kb.cnt["pe"])
        kb.inc(s.dma_start(out=yh[:, 0:8192], in_=kb.wreg("B", "w3o")), "ld", 16)
        v_wo = kb.cnt["ld"]
        w3of = yh[:, 0:8192].rearrange("p (jj d) -> p jj d", d=1024)
        t_.wait_ge(kb.sems["ld"], v_wo)
        t_.wait_ge(kb.sems["ch_vector"], kb.cnt["ve"])
        if v_o2ev is not None:
            t_.wait_ge(kb.sems["ch_vector"], v_o2ev)
        lastm = None
        for dc in range(8):
            for jj in range(8):
                lastm = t_.matmul(pA[:, dc * 128:(dc + 1) * 128],
                                  w3of[:, jj, dc * 128:(dc + 1) * 128],
                                  oall[:, jj * 128:(jj + 1) * 128],
                                  start=(jj == 0), stop=(jj == 7))
        kb.inc(lastm, "pe", 1)
        v_blk_mm = kb.cnt["pe"]
        ve.wait_ge(kb.sems["pe"], v_blk_mm)
        ve.wait_ge(kb.sems["gd"], kb.cnt["gd"])
        inst = None
        for dc in range(8):
            inst = ve.tensor_scalar_add(o2b[:, dc * 128:(dc + 1) * 128],
                                        pA[:, dc * 128:(dc + 1) * 128],
                                        cbo[:, dc:dc + 1])
        kb.inc(inst, "ve", 1)
        v_o2ev = kb.cnt["ve"]
        g.wait_ge(kb.sems["ch_vector"], v_o2ev)
        for wc in range(8):
            kb.inc(g.dma_start(
                out=scr[S3_O2 + wc * 131072:S3_O2 + (wc + 1) * 131072]
                .rearrange("(p t) -> p t", t=1024)[:, b * 128:(b + 1) * 128],
                in_=o2b[:, wc * 128:(wc + 1) * 128]), "gd", 16)
    emit_outproj(kb, "w3p", S3_O2, stats_first=True, bias_name="b3p", nk=8)


def halo_cols(x, c):
    b, t0 = c // 4, (c % 4) * T
    hcols = np.zeros((128, 64), np.float32)
    for t, tok in ((0, t0 - 1), (1, t0 + T)):
        if 0 <= tok < S:
            hcols[:, t * 32:(t + 1) * 32] = x[b, tok, :].reshape(32, 128).T
    return hcols


S5_Q = 0
S5_AO = 32 * 131072
S5_CCI = S5_AO + 32 * 131072
S5_KPER = 262144
S5_CCO = S5_CCI + S5_KPER
S5_END = S5_CCO + 4 * S5_KPER


def pack_s5(inputs, PB):
    PB.add("w5q", img_lhsT(np.ascontiguousarray(
        np.asarray(inputs["L1_attn_a_Wq"], np.float32).T)))
    PB.add("w5k", img_lhsT(np.ascontiguousarray(
        np.asarray(inputs["L1_attn_a_Wk"], np.float32).T)))
    PB.add("w5v", img_lhsT(np.ascontiguousarray(
        np.asarray(inputs["L1_attn_a_Wv"], np.float32).T)))
    PB.add("w5o", img_lhsT(np.ascontiguousarray(
        np.asarray(inputs["L1_attn_a_Wo"], np.float32).T)))


def att5_consts(core, CP):
    r = core % 4
    g0 = r * T
    pos = np.arange(g0, g0 + T, dtype=np.float32)
    inv = 1.0 / (10000.0 ** (np.arange(64, dtype=np.float32) / 64.0))
    invd = inv[np.arange(128) // 2]
    ang = invd[:, None] * pos[None, :]
    cosb = np.cos(ang).astype(np.float32)
    sinb = np.sin(ang).astype(np.float32)
    sinb[0:64] = -sinb[0:64]
    CP.add("rope5_cos", cosb)
    CP.add("rope5_sin", sinb)
    al = np.zeros((128, 1024), np.float32)
    for qb4 in range(4):
        a = np.linspace(0.0, 1.0, 256, dtype=np.float32)
        if r == 0 and qb4 == 0:
            a = np.ones(256, np.float32)
        al[:, qb4 * 256:(qb4 + 1) * 256] = a[None, :]
    CP.add("c5al", al)
    m = np.zeros((128, 16 * 4096), np.float32)
    for qt in range(8):
        i2 = g0 + (qt // 2) * 256
        wins = [(max(i2 - 256, 0), i2 + 256), (i2, min(i2 + 512, S))]
        for w, (a, b) in enumerate(wins):
            blk = np.full((128, 4096), -1e9, np.float32)
            blk[:, a:b] = 0.0
            m[:, (qt * 2 + w) * 4096:(qt * 2 + w + 1) * 4096] = blk
    u = np.ascontiguousarray(m.astype(BF)).view(np.uint16).reshape(-1, 2)
    packed = np.ascontiguousarray(u).view(np.uint32).reshape(-1).view(np.float32)
    CP.add_flat("m5", packed)


def emit_s5_core(kb, v_ag):
    pA, pC2, pT = kb.ps["pA"], kb.ps["pC2"], kb.ps["pT"]
    ws = kb.sb["ws"]
    wsb = ws.bitcast(BF16)
    wsf = ws.bitcast(F32)
    yb = kb.sb["yh"]
    mbuf = yb[:, 0:4096]
    scb = yb.bitcast(F32)[:, 2048:2048 + 4096]
    Pb = yb[:, 12288:12288 + 4096]
    Kw = wsb[:, 0:4096]
    Vw = wsb[:, 4096:8192]
    Qw = wsb[:, 8192:9216]
    pts = wsb[:, 13312:13824]
    aost = wsb[:, 14336:15360]
    alf = wsf[:, 5120:6144]
    tb = wsf[:, 6144:6272]
    tba = wsf[:, 6272:6400]
    lrow = wsf[:, 7722:7723]
    rl = wsf[:, 7723:7724]
    scr = kb.dram["scr"].ap().rearrange("r c -> (r c)")
    t_, ve, ac, s, g = kb.bodies("tensor", "vector", "scalar", "sync", "gpsimd")
    s.wait_ge(kb.sems["cc"], v_ag)
    for r in range(4):
        src = scr[S5_CCO + r * S5_KPER: S5_CCO + r * S5_KPER + 131072] \
            .rearrange("(d t) -> d t", t=1024)
        kb.inc(s.dma_start(out=Kw[:, r * 1024:(r + 1) * 1024], in_=src), "ld", 16)
        vsrc = scr[S5_CCO + r * S5_KPER + 131072: S5_CCO + (r + 1) * S5_KPER] \
            .rearrange("(tt p d) -> p tt d", p=128, d=128)
        kb.inc(s.dma_start(
            out=Vw.rearrange("p (rr tt d) -> p rr tt d", rr=4, tt=8)[:, r, :, :],
            in_=vsrc), "ld", 16)
    kb.inc(s.dma_start(out=alf, in_=kb.creg("c5al")), "ld", 16)
    v_kv = kb.cnt["ld"]
    ve.wait_ge(kb.sems["ld"], v_kv)
    m5 = kb.creg("m5").bitcast(BF16).rearrange("p (qq c) -> p qq c", c=4096)
    scale = 1.0 / float(np.sqrt(128.0))
    v_sc_ev = None
    v_pv_ev = None
    v_ao = None
    v_q_mm = 0
    for hq in range(32):
        s.wait_ge(kb.sems["pe"], v_q_mm)
        kb.inc(s.dma_start(
            out=Qw, in_=scr[S5_Q + hq * 131072: S5_Q + (hq + 1) * 131072]
            .rearrange("(d t) -> d t", t=1024)), "ld", 16)
        v_qld = kb.cnt["ld"]
        for qt in range(8):
            for w in range(2):
                if v_sc_ev is not None:
                    s.wait_ge(kb.sems["ch_vector"], v_sc_ev)
                kb.inc(s.dma_start(out=mbuf, in_=m5[:, qt * 2 + w, :]), "ld", 16)
                v_mk = kb.cnt["ld"]
                ev_list = []
                for kb_ in range(8):
                    pc0 = (kb_ % 2) * 512
                    t_.wait_ge(kb.sems["ld"], v_kv)
                    t_.wait_ge(kb.sems["ld"], v_qld)
                    if kb_ >= 2:
                        t_.wait_ge(kb.sems["ch_vector"], ev_list[kb_ - 2])
                    elif v_sc_ev is not None:
                        t_.wait_ge(kb.sems["ch_vector"], v_sc_ev)
                    kb.inc(t_.matmul(
                        pC2[:, pc0:pc0 + 512],
                        Qw[:, qt * 128:(qt + 1) * 128],
                        Kw[:, kb_ * 512:(kb_ + 1) * 512],
                        start=True, stop=True), "pe", 1)
                    v_s = kb.cnt["pe"]
                    ve.wait_ge(kb.sems["pe"], v_s)
                    if kb_ == 0:
                        ve.wait_ge(kb.sems["ld"], v_mk)
                        ve.wait_ge(kb.sems["ac"], kb.cnt["ac"])
                    kb.inc(ve.tensor_add(scb[:, kb_ * 512:(kb_ + 1) * 512],
                                         pC2[:, pc0:pc0 + 512],
                                         mbuf[:, kb_ * 512:(kb_ + 1) * 512]), "ve", 1)
                    ev_list.append(kb.cnt["ve"])
                v_sc_ev = kb.cnt["ve"]
                ac.wait_ge(kb.sems["ve"], kb.cnt["ve"])
                ac.wait_ge(kb.sems["pe"], kb.cnt["pe"])
                kb.inc(ac.activation(out=Pb, in_=scb, func=AF.Sigmoid,
                                     scale=scale, accum_out=lrow), "ac", 1)
                ve.wait_ge(kb.sems["ac"], kb.cnt["ac"])
                ve.reciprocal(rl, lrow)
                kb.inc(ve.tensor_scalar_mul(Pb, Pb, rl), "ve", 1)
                v_pn = kb.cnt["ve"]
                for kc in range(32):
                    tj = kc % 4
                    if tj == 0:
                        if kc > 0:
                            t_.wait_ge(kb.sems["ch_vector"], v_pv_ev)
                        else:
                            t_.wait_ge(kb.sems["ve"], v_pn)
                            t_.wait_ge(kb.sems["po"], kb.cnt["po"])
                            if v_ao is not None:
                                t_.wait_ge(kb.sems["ch_vector"], v_ao)
                    ti = t_.transpose(pT[:, tj * 128:(tj + 1) * 128],
                                      Pb[:, kc * 128:(kc + 1) * 128],
                                      kb.sb["id_bf"][:, :])
                    if tj == 3:
                        kb.inc(ti, "pe", 1)
                        v_tp = kb.cnt["pe"]
                        ve.wait_ge(kb.sems["pe"], v_tp)
                        kb.inc(ve.tensor_copy(pts, pT[:, 0:512]), "ve", 1)
                        v_pv_ev = kb.cnt["ve"]
                        t_.wait_ge(kb.sems["ch_vector"], v_pv_ev)
                        last = None
                        for j in range(4):
                            kcj = kc - 3 + j
                            last = t_.matmul(pA[:, w * 128:(w + 1) * 128],
                                             Vw[:, kcj * 128:(kcj + 1) * 128],
                                             pts[:, j * 128:(j + 1) * 128],
                                             start=(kcj == 0), stop=(kcj == 31))
                        kb.inc(last, "pe", 1)
                v_q_mm = kb.cnt["pe"]
            ve.wait_ge(kb.sems["pe"], v_q_mm)
            if qt == 0:
                ve.wait_ge(kb.sems["gd"], kb.cnt["gd"])
            asl = alf[:, (qt // 2) * 256 + (qt % 2) * 128:
                      (qt // 2) * 256 + (qt % 2) * 128 + 128]
            ve.tensor_copy(tba, pA[:, 0:128])
            ve.tensor_sub(tb, pA[:, 128:256], tba)
            ve.tensor_mul(tb, tb, asl)
            kb.inc(ve.tensor_add(aost[:, qt * 128:(qt + 1) * 128],
                                 tba, tb), "ve", 1)
            v_ao = kb.cnt["ve"]
        g.wait_ge(kb.sems["ch_vector"], kb.cnt["ve"])
        kb.inc(g.dma_start(
            out=scr[S5_AO + hq * 131072: S5_AO + (hq + 1) * 131072]
            .rearrange("(d t) -> d t", t=1024),
            in_=aost), "gd", 16)
    return


def emit_s5(kb):
    def wspec(oc):
        if oc < 32:
            return ("w5q", 4096, oc * 128)
        if oc == 32:
            return ("w5k", 128, 0)
        return ("w5v", 128, 0)
    emit_attn_proj(kb, wspec, 32, 1, S5_Q, S5_CCI,
                   tabs=("rope5_cos", "rope5_sin"))
    v_ag = emit_kv_ag(kb, S5_CCI, S5_CCO, S5_KPER)
    emit_s5_core(kb, v_ag)
    emit_outproj(kb, "w5o", S5_AO, stats_first=True)


# version 06 (final)
# speedup vs baseline: 1.0775x; 1.0775x over previous
import numpy as np
import ml_dtypes
import concourse.bass as bass
import concourse.mybir as mybir
from concourse.bass_utils import run_bass_kernel_spmd

F32 = mybir.dt.float32
BF16 = mybir.dt.bfloat16
I32 = mybir.dt.int32
U16 = mybir.dt.uint16
U8 = mybir.dt.uint8
AF = mybir.ActivationFunctionType
ALU = mybir.AluOpType
AX = mybir.AxisListType.X

D = 4096
T = 1024
TH = 512
NCH = 32
S = 4096
GROUPS = [[0, 1, 2, 3], [4, 5, 6, 7]]
ALLG = [list(range(8))]
BF = ml_dtypes.bfloat16
ts = bass.ts



class Pack:

    def __init__(self, dtype):
        self.dtype = dtype
        self.parts = []
        self.offs = {}
        self.n = 0

    def add(self, name, arr):
        assert arr.ndim == 2 and arr.shape[0] == 128, (name, arr.shape)
        arr = np.ascontiguousarray(arr.astype(self.dtype, copy=False))
        F = arr.shape[1]
        Fp = -(-F // 4) * 4
        if Fp != F:
            a2 = np.zeros((128, Fp), dtype=self.dtype)
            a2[:, :F] = arr
            arr = a2
        self.offs[name] = (self.n, Fp)
        self.parts.append(arr)
        self.n += 128 * Fp
        return self.offs[name]

    def add_flat(self, name, arr1d):
        arr1d = np.ascontiguousarray(arr1d.reshape(-1).astype(self.dtype, copy=False))
        n = arr1d.size
        npad = -(-n // 512) * 512
        a2 = np.zeros(npad, dtype=self.dtype)
        a2[:n] = arr1d
        self.offs[name] = (self.n, npad // 128)
        self.parts.append(a2.reshape(128, -1))
        self.n += npad
        return self.offs[name]

    def finish(self):
        flat = (np.concatenate([p.reshape(-1) for p in self.parts])
                if self.parts else np.zeros((8 * 512,), dtype=self.dtype))
        rows = -(-flat.size // 512)
        rows = max(-(-rows // 8) * 8, 8)
        out = np.zeros((rows, 512), dtype=self.dtype)
        out.reshape(-1)[:flat.size] = flat
        return out


def img_lhsT(W):
    K, M = W.shape
    assert K % 128 == 0
    return np.ascontiguousarray(
        W.reshape(K // 128, 128, M).transpose(1, 0, 2).reshape(128, -1))


def img_lhsT_strips(W, strip=128):
    K, M = W.shape
    assert K % 128 == 0 and M % strip == 0
    return np.ascontiguousarray(
        W.reshape(K // 128, 128, M // strip, strip).transpose(1, 2, 0, 3)
        .reshape(128, -1))



SEMS = ["ld", "gd", "pe", "ve", "ac", "po", "cc"]



class KB:
    def __init__(self, nc, blk, sems, sb, ps, wall, atlas, dram, cst_ap, catlas):
        self.nc = nc
        self.blk = blk
        self.sems = sems
        self.cnt = {k: 0 for k in SEMS}
        self.sb = sb
        self.ps = ps
        self.wall = wall
        self.atlas = atlas
        self.dram = dram
        self.cst = cst_ap
        self.catlas = catlas
        self.recs = {n: Rec(n) for n in
                     ("tensor", "vector", "scalar", "sync", "gpsimd")}

    def inc(self, inst, sem, k=1):
        if sem in ("ve", "ac", "po"):
            eng = {"ve": "vector", "ac": "scalar", "po": "gpsimd"}[sem]
            self.cnt[sem] = self.recs[eng].chn
            return self.cnt[sem]
        inst.then_inc(self.sems[sem], k)
        self.cnt[sem] += k
        return self.cnt[sem]

    def wreg(self, pack, name, rows=128, c0=0, cols=None):
        off, F = self.atlas[pack][name]
        base = self.wall[pack].rearrange("r c -> (r c)")
        reg = base[off: off + 128 * F].rearrange("(p f) -> p f", f=F)
        if cols is None:
            cols = F - c0
        return reg[0:rows, c0:c0 + cols]

    def wflat(self, pack, name):
        off, F = self.atlas[pack][name]
        base = self.wall[pack].rearrange("r c -> (r c)")
        return base[off: off + 128 * F]

    def creg(self, name, rows=128, c0=0, cols=None):
        off, F = self.catlas[name]
        base = self.cst.rearrange("r c -> (r c)")
        reg = base[off: off + 128 * F].rearrange("(p f) -> p f", f=F)
        if cols is None:
            cols = F - c0
        return reg[0:rows, c0:c0 + cols]

    def bodies(self, *names):
        return [self.recs[n] for n in names]

    NOCHAIN = {"dma_start", "dma_start_transpose", "wait_ge", "sem_inc",
               "collective_compute", "indirect_dma_start", "engine_nop"}
    CHAINED = {"vector", "scalar", "gpsimd"}

    def finalize(self):
        for n, rec in self.recs.items():
            if not rec.ops:
                continue
            chain = n in KB.CHAINED
            csem = self.sems.get("ch_" + n)

            def body(eng, rec=rec, chain=chain, csem=csem):
                nch = 0
                for (meth, args, kwargs, incs) in rec.ops:
                    do_chain = chain and meth not in KB.NOCHAIN
                    if do_chain and nch > 0:
                        eng.wait_ge(csem, nch)
                    inst = getattr(eng, meth)(*args, **kwargs)
                    assert len(incs) <= 1 or not do_chain, (n, meth, incs)
                    for (sem, k) in incs:
                        inst.then_inc(sem, k)
                    if do_chain:
                        assert not incs, (n, meth, "chained op with explicit inc")
                        inst.then_inc(csem, 1)
                        nch += 1
            getattr(self.blk, n)(body)


class RecInst:
    def __init__(self):
        self.incs = []

    def then_inc(self, sem, k=1):
        self.incs.append((sem, k))
        return self


class Rec:

    def __init__(self, name):
        self.name = name
        self.ops = []
        self.chn = 0

    def __getattr__(self, meth):
        if meth.startswith("_"):
            raise AttributeError(meth)

        def call(*args, **kwargs):
            ri = RecInst()
            self.ops.append((meth, args, kwargs, ri.incs))
            if self.name in KB.CHAINED and meth not in KB.NOCHAIN:
                self.chn += 1
            return ri
        return call



def emit_preamble(kb, use_packs, skip_x=False):
    (g,) = kb.bodies("gpsimd")
    g.memset(kb.sb["ones_bf"][:, :], 1.0)
    g.memset(kb.sb["rowsb"][0:1, 0:512], 1.0)
    g.memset(kb.sb["id_bf"][:, :], 0.0)
    g.affine_select(out=kb.sb["id_bf"][:, :], in_=kb.sb["id_bf"][:, :],
                    compare_op=ALU.not_equal, fill=1.0, base=0,
                    pattern=[[-1, 128]], channel_multiplier=1)
    g.memset(kb.sb["id_f"][:, :], 0.0)
    inst = g.affine_select(out=kb.sb["id_f"][:, :], in_=kb.sb["id_f"][:, :],
                           compare_op=ALU.not_equal, fill=1.0, base=0,
                           pattern=[[-1, 128]], channel_multiplier=1)
    kb.inc(inst, "po", 1)
    for p in use_packs:
        src = {"B": "cbf", "C": "ccf", "F": "cf", "I": "ci"}[p]
        kb.inc(g.dma_start(out=kb.dram[src].ap(), in_=kb.dram["wsh" + p].ap()),
               "gd", 16)
    g.wait_ge(kb.sems["gd"], kb.cnt["gd"])
    for p in use_packs:
        src = {"B": "cbf", "C": "ccf", "F": "cf", "I": "ci"}[p]
        dst = "W" + p
        inst = g.collective_compute(
            "AllGather", ALU.bypass, replica_groups=ALLG,
            ins=[kb.dram[src].ap().opt()], outs=[kb.dram[dst].ap().opt()])
        kb.inc(inst, "cc", 1)
    g.wait_ge(kb.sems["cc"], kb.cnt["cc"])

    (s,) = kb.bodies("sync")
    for k in range(NCH):
        inst = s.dma_start(
            out=kb.sb["xres"][:, k * T:(k + 1) * T],
            in_=kb.dram["xT"].ap()[k * 128:(k + 1) * 128, :])
        kb.inc(inst, "ld", 16)

    ac, ve, t_ = kb.bodies("scalar", "vector", "tensor")
    ac.wait_ge(kb.sems["ld"], kb.cnt["ld"])
    ve.wait_ge(kb.sems["ld"], kb.cnt["ld"])
    t_.wait_ge(kb.sems["ch_gpsimd"], kb.cnt["po"])
    st = {}
    for m in range(NCH):
        for h in range(2):
            emit_stats_chunk(kb, ac, ve, t_, m, h, first=(m == 0), st=st)


def emit_stats_chunk(kb, ac, ve, t_, m, h, first, st):
    xb = kb.sb["xb"]
    rows = kb.sb["rows"]
    pE = kb.ps["pE"]
    hist = st.setdefault("pe_hist", [])
    slot = len(hist) % 2
    xbs = xb[:, slot * 1024:(slot + 1) * 1024]
    xs = kb.sb["xres"][:, m * T + h * TH: m * T + h * TH + TH]
    ac.wait_ge(kb.sems["ve"], kb.cnt["ve"])
    if len(hist) >= 2:
        ac.wait_ge(kb.sems["pe"], hist[-2])
        ve.wait_ge(kb.sems["pe"], hist[-2])
    kb.inc(ac.activation(out=xbs[:, TH:2 * TH], in_=xs, func=AF.Square), "ac", 1)
    kb.inc(ve.tensor_copy(xbs[:, 0:TH], xs), "ve", 1)
    v_ac, v_ve = kb.cnt["ac"], kb.cnt["ve"]
    t_.wait_ge(kb.sems["ac"], v_ac)
    t_.wait_ge(kb.sems["ve"], v_ve)
    if st.get("guard") is not None:
        t_.wait_ge(kb.sems["ve"], st["guard"])
    t_.matmul(pE[0:1, :], kb.sb["ones_bf"][:, 0:1], xbs[:, 0:TH], start=True, stop=True)
    kb.inc(t_.matmul(pE[32:33, :], kb.sb["ones_bf"][:, 0:1], xbs[:, TH:2 * TH],
                     start=True, stop=True), "pe", 1)
    v_pe = kb.cnt["pe"]
    hist.append(v_pe)
    ve.wait_ge(kb.sems["pe"], v_pe)
    su = rows[0:1, h * TH:(h + 1) * TH]
    sq = rows[32:33, h * TH:(h + 1) * TH]
    if first:
        ve.tensor_copy(su, pE[0:1, :])
        kb.inc(ve.tensor_copy(sq, pE[32:33, :]), "ve", 1)
    else:
        ve.tensor_add(su, su, pE[0:1, :])
        kb.inc(ve.tensor_add(sq, sq, pE[32:33, :]), "ve", 1)
    st["guard"] = kb.cnt["ve"]
    return st["guard"]


def emit_yprod(kb, h):
    rows, rowsb = kb.sb["rows"], kb.sb["rowsb"]
    pA, pB = kb.ps["pA"], kb.ps["pB"]
    ve, ac, t_ = kb.bodies("vector", "scalar", "tensor")
    r = lambda b, c: rows[b:b + 1, c * TH:(c + 1) * TH]
    rb = lambda c: rowsb[64:65, c * TH:(c + 1) * TH]
    mean, var, trow = r(64, 0), r(64, 1), r(64, 2)
    ve.tensor_scalar_mul(mean, r(0, h), 1.0 / D)
    ve.tensor_scalar_mul(var, r(32, h), 1.0 / D)
    ve.tensor_mul(trow, mean, mean)
    ve.tensor_sub(var, var, trow)
    kb.inc(ve.tensor_scalar_add(var, var, 1e-5), "ve", 1)
    ac.wait_ge(kb.sems["ve"], kb.cnt["ve"])
    kb.inc(ac.activation(out=var, in_=var, func=AF.Sqrt), "ac", 1)
    ve.wait_ge(kb.sems["ac"], kb.cnt["ac"])
    ve.reciprocal(var, var)
    ve.tensor_copy(rb(0), mean)
    ve.tensor_sub(trow, mean, rb(0))
    ve.tensor_copy(rb(1), trow)
    ve.tensor_copy(rb(2), var)
    ve.tensor_sub(trow, var, rb(2))
    kb.inc(ve.tensor_copy(rb(3), trow), "ve", 1)
    t_.wait_ge(kb.sems["ve"], kb.cnt["ve"])
    t_.matmul(pA[:, 0:TH], kb.sb["ones_bf"][64:65, :], rb(2), start=True, stop=False)
    t_.matmul(pA[:, 0:TH], kb.sb["ones_bf"][64:65, :], rb(3), start=False, stop=True)
    t_.matmul(pB[:, 0:TH], kb.sb["ones_bf"][64:65, :], rb(0), start=True, stop=False)
    kb.inc(t_.matmul(pB[:, 0:TH], kb.sb["ones_bf"][64:65, :], rb(1),
                     start=False, stop=True), "pe", 1)
    ve.wait_ge(kb.sems["pe"], kb.cnt["pe"])
    tsc = kb.sb["tscr"]
    inst = None
    for k in range(NCH):
        xs = kb.sb["xres"][:, k * T + h * TH: k * T + h * TH + TH]
        ve.tensor_sub(tsc[:, :], xs, pB[:, 0:TH])
        inst = ve.tensor_mul(kb.sb["yh"][:, k * TH:(k + 1) * TH], tsc[:, :], pA[:, 0:TH])
    kb.inc(inst, "ve", 1)
    return kb.cnt["ve"]


def emit_store_x(kb):
    (s,) = kb.bodies("sync")
    s.wait_ge(kb.sems["ve"], kb.cnt["ve"])
    s.wait_ge(kb.sems["ac"], kb.cnt["ac"])
    for k in range(NCH):
        inst = s.dma_start(
            out=kb.dram["xO"].ap()[k * 128:(k + 1) * 128, :],
            in_=kb.sb["xres"][:, k * T:(k + 1) * T])
        kb.inc(inst, "ld", 16)
    s.wait_ge(kb.sems["ld"], kb.cnt["ld"])



def pack_s6(inputs, PB, PF, PI):
    rw = np.asarray(inputs["L1_ffn_router_w"], np.float32)
    A = np.asarray(inputs["L1_ffn_A"], np.float32)
    Bm = np.asarray(inputs["L1_ffn_B"], np.float32)
    PB.add("s6_rt", img_lhsT(np.ascontiguousarray(rw.T)))
    aimg = np.concatenate(
        [img_lhsT(np.ascontiguousarray(A[e].T)) for e in range(4)], axis=1)
    PB.add("s6_a", aimg)
    tiles = []
    for mc in range(32):
        for e in range(4):
            tiles.append(np.ascontiguousarray(Bm[e].T[:, mc * 128:(mc + 1) * 128]))
    PB.add_flat("s6_b", np.stack(tiles).reshape(-1))


def emit_s6(kb):
    pA, pB, pC2, pE = kb.ps["pA"], kb.ps["pB"], kb.ps["pC2"], kb.ps["pE"]
    ws = kb.sb["ws"]
    wsb = ws.bitcast(BF16)
    wsf = ws.bitcast(F32)
    A0, RT0, MID0, BT0 = 0, 2048, 2176, 6272
    GF0 = 3648
    LG0, GT0, GS0, NEG0, SC0 = 5696, 6720, 6752, 6784, 6788
    GHI4, GLO4, GSLH, GSLL = 9344, 10368, 11392, 0
    mid = lambda e, h: wsb[0:16, MID0 + (e * 2 + h) * TH: MID0 + (e * 2 + h + 1) * TH]

    (s,) = kb.bodies("sync")
    s.wait_ge(kb.sems["cc"], kb.cnt["cc"])
    kb.inc(s.dma_start(out=wsb[:, A0:A0 + 2048], in_=kb.wreg("B", "s6_a")), "ld", 16)
    kb.inc(s.dma_start(out=wsb[:, RT0:RT0 + 128], in_=kb.wreg("B", "s6_rt")), "ld", 16)
    v_wld = kb.cnt["ld"]
    (ve0,) = kb.bodies("vector")
    kb.inc(ve0.memset(wsf[:, NEG0:NEG0 + 4], -1e30), "ve", 1)

    mtgt = [(pC2, 0), (pC2, TH), (pA, 0), (pA, TH)]
    for h in range(2):
        ydone = emit_yprod(kb, h)
        t_, ve = kb.bodies("tensor", "vector")
        t_.wait_ge(kb.sems["ve"], ydone)
        t_.wait_ge(kb.sems["ld"], v_wld)
        lrow = 0 if h == 0 else 32
        for k in range(NCH):
            t_.matmul(pE[lrow:lrow + 4, :], wsb[:, RT0 + k * 4:RT0 + (k + 1) * 4],
                      kb.sb["yh"][:, k * TH:(k + 1) * TH],
                      start=(k == 0), stop=(k == NCH - 1))
        last = None
        for e in range(4):
            pt, c0 = mtgt[e]
            for k in range(NCH):
                last = t_.matmul(
                    pt[0:16, c0:c0 + TH],
                    wsb[:, A0 + (e * 32 + k) * 16:A0 + (e * 32 + k + 1) * 16],
                    kb.sb["yh"][:, k * TH:(k + 1) * TH],
                    start=(k == 0), stop=(k == NCH - 1))
        kb.inc(last, "pe", 1)
        ve.wait_ge(kb.sems["pe"], kb.cnt["pe"])
        ve.tensor_copy(wsf[0:4, LG0 + h * TH:LG0 + (h + 1) * TH], pE[lrow:lrow + 4, :])
        last = None
        for e in range(4):
            pt, c0 = mtgt[e]
            last = ve.tensor_copy(mid(e, h), pt[0:16, c0:c0 + TH])
        kb.inc(last, "ve", 1)

    t_, ve, ac = kb.bodies("tensor", "vector", "scalar")
    t_.wait_ge(kb.sems["ve"], kb.cnt["ve"])
    t_.wait_ge(kb.sems["po"], kb.cnt["po"])
    last = None
    for j in range(8):
        last = t_.transpose(pC2[:, j * 4:(j + 1) * 4],
                            wsf[0:4, LG0 + j * 128:LG0 + (j + 1) * 128],
                            kb.sb["id_f"][0:4, 0:4])
    kb.inc(last, "pe", 1)
    ve.wait_ge(kb.sems["pe"], kb.cnt["pe"])
    kb.inc(ve.tensor_copy(wsf[:, GT0:GT0 + 32], pC2[:, 0:32]), "ve", 1)
    for j in range(8):
        gt = wsf[:, GT0 + j * 4:GT0 + (j + 1) * 4]
        gs = wsf[:, GS0 + j * 4:GS0 + (j + 1) * 4]
        m1 = wsf[:, SC0:SC0 + 1]
        m2 = wsf[:, SC0 + 1:SC0 + 2]
        nm1 = wsf[:, SC0 + 2:SC0 + 3]
        eqm = ws[:, 27200:27204]
        eqf = wsf[:, SC0 + 4:SC0 + 8]
        ve.reduce_max(m1, gt, axis=AX)
        ve.tensor_single_scalar(eqm, gt, m1, ALU.is_ge)
        ve.select(gs, eqm, wsf[:, NEG0:NEG0 + 4], gt)
        ve.reduce_max(m2, gs, axis=AX)
        kb.inc(ve.tensor_scalar_mul(nm1, m1, -1.0), "ve", 1)
        ac.wait_ge(kb.sems["ve"], kb.cnt["ve"])
        kb.inc(ac.activation(out=gs, in_=gt, func=AF.Exp, bias=nm1, scale=1.0), "ac", 1)
        ve.wait_ge(kb.sems["ac"], kb.cnt["ac"])
        ve.tensor_single_scalar(eqf, gt, m2, ALU.is_ge)
        ve.tensor_mul(gs, gs, eqf)
        ve.reduce_sum(m2, gs, axis=AX)
        ve.reciprocal(m2, m2)
        ve.tensor_scalar_mul(gs, gs, m2)
        kb.inc(ve.tensor_scalar_mul(gs, gs, 4.0), "ve", 1)
    t_.wait_ge(kb.sems["ve"], kb.cnt["ve"])
    last = None
    for j in range(8):
        last = t_.transpose(pA[0:4, j * 128:(j + 1) * 128],
                            wsf[:, GS0 + j * 4:GS0 + (j + 1) * 4],
                            kb.sb["id_f"][:, :])
    kb.inc(last, "pe", 1)
    v_tp = kb.cnt["pe"]
    ve.wait_ge(kb.sems["pe"], v_tp)
    g_f = wsf[0:4, GF0:GF0 + 1024]
    ghi4 = wsb[0:4, GHI4:GHI4 + 1024]
    glo4 = wsb[0:4, GLO4:GLO4 + 1024]
    gslh = wsb[0:1, GSLH:GSLH + 4096]
    gsll = wsb[0:1, GSLL:GSLL + 4096]
    ve.tensor_copy(g_f, pA[0:4, 0:1024])
    ve.tensor_copy(ghi4, g_f)
    ve.tensor_sub(g_f, g_f, ghi4)
    kb.inc(ve.tensor_copy(glo4, g_f), "ve", 1)
    (gd,) = kb.bodies("gpsimd")
    gd.wait_ge(kb.sems["ve"], kb.cnt["ve"])
    gd.wait_ge(kb.sems["pe"], v_tp)
    scrf = kb.dram["scr"].ap().rearrange("r c -> (r c)")
    kb.inc(gd.dma_start(out=scrf[0:4096].rearrange("(p f) -> p f", f=1024), in_=ghi4),
           "gd", 16)
    kb.inc(gd.dma_start(out=scrf[4096:8192].rearrange("(p f) -> p f", f=1024), in_=glo4),
           "gd", 16)
    gd.wait_ge(kb.sems["gd"], kb.cnt["gd"])
    kb.inc(gd.dma_start(out=gslh, in_=scrf[0:4096].rearrange("(p f) -> p f", f=4096)),
           "gd", 16)
    kb.inc(gd.dma_start(out=gsll, in_=scrf[4096:8192].rearrange("(p f) -> p f", f=4096)),
           "gd", 16)
    vg = kb.cnt["gd"]

    for h in range(2):
        t_, ve, ac, s = kb.bodies("tensor", "vector", "scalar", "sync")
        t_.wait_ge(kb.sems["gd"], vg)
        btgt = [(pB, 0), (pB, TH), (pA, 0), (pA, TH)]
        last = None
        for e in range(4):
            pt, c0 = btgt[e]
            t_.matmul(pt[0:16, c0:c0 + TH], kb.sb["ones_bf"][0:1, 0:16],
                      wsb[0:1, GSLH + e * 1024 + h * TH:GSLH + e * 1024 + (h + 1) * TH],
                      start=True, stop=False)
            last = t_.matmul(pt[0:16, c0:c0 + TH], kb.sb["ones_bf"][0:1, 0:16],
                             wsb[0:1, GSLL + e * 1024 + h * TH:GSLL + e * 1024 + (h + 1) * TH],
                             start=False, stop=True)
        kb.inc(last, "pe", 1)
        ve.wait_ge(kb.sems["pe"], kb.cnt["pe"])
        last = None
        for e in range(4):
            pt, c0 = btgt[e]
            last = ve.tensor_mul(mid(e, h), mid(e, h), pt[0:16, c0:c0 + TH])
        kb.inc(last, "ve", 1)
        vmid = kb.cnt["ve"]
        stats_st = {} if h == 0 else stats_st_carry
        v_prev_mm = kb.cnt["pe"]
        for mg in range(16):
            ms = [2 * mg, 2 * mg + 1]
            s.wait_ge(kb.sems["pe"], v_prev_mm)
            for i, m in enumerate(ms):
                src = kb.wflat("B", "s6_b")[m * 8192:(m + 1) * 8192] \
                    .rearrange("(e r mm) -> r e mm", e=4, r=16)
                dst = wsb[0:16, BT0 + i * 512:BT0 + (i + 1) * 512] \
                    .rearrange("r (e mm) -> r e mm", e=4)
                kb.inc(s.dma_start(out=dst, in_=src), "ld", 16)
            t_.wait_ge(kb.sems["ld"], kb.cnt["ld"])
            if mg == 0:
                t_.wait_ge(kb.sems["ve"], vmid)
            last = None
            for i, m in enumerate(ms):
                for e in range(4):
                    last = t_.matmul(
                        pC2[:, i * TH:(i + 1) * TH],
                        wsb[0:16, BT0 + i * 512 + e * 128:BT0 + i * 512 + (e + 1) * 128],
                        mid(e, h), start=(e == 0), stop=(e == 3))
            kb.inc(last, "pe", 1)
            v_prev_mm = kb.cnt["pe"]
            ve.wait_ge(kb.sems["pe"], v_prev_mm)
            for i, m in enumerate(ms):
                xs = kb.sb["xres"][:, m * T + h * TH: m * T + h * TH + TH]
                kb.inc(ve.tensor_add(xs, xs, pC2[:, i * TH:(i + 1) * TH]), "ve", 1)
                emit_stats_chunk(kb, ac, ve, t_, m, h,
                                 first=(mg == 0 and i == 0), st=stats_st)
        stats_st_carry = stats_st



def build_program(name, emits, packs_np, catlas, cst_rows, use_packs, scr_bytes=0,
                  scrw_bytes=0, kv_specs=(), store=True, int_rows=0, halo=False):
    nc = bass.Bass("TRN2", target_bir_lowering=False, debug=False)
    dram = {}
    dram["xT"] = nc.dram_tensor("xT", [D, T], F32, kind="ExternalInput")
    dram["xO"] = nc.dram_tensor("xO", [D, T], F32, kind="ExternalOutput")
    cst = nc.dram_tensor("cst", [cst_rows, 512], F32, kind="ExternalInput")
    if halo:
        dram["xh"] = nc.dram_tensor("xh", [128, 64], F32, kind="ExternalInput")
    shapes = {"B": BF16, "C": BF16, "F": F32, "I": I32}
    wall = {}
    for p in use_packs:
        rows = packs_np[p].shape[0]
        dt = shapes[p]
        dram["wsh" + p] = nc.dram_tensor("wsh" + p, [rows // 8, 512],
                                         F32 if p != "I" else I32, kind="ExternalInput")
        dram[{"B": "cbf", "C": "ccf", "F": "cf", "I": "ci"}[p]] = nc.dram_tensor(
            "c" + p, [rows // 8, 512], dt)
        dram["W" + p] = nc.dram_tensor("W" + p, [rows, 512], dt)
    if int_rows:
        dram["wshI"] = nc.dram_tensor("wshI", [int_rows, 512], I32, kind="ExternalInput")
    if scr_bytes:
        dram["scr"] = nc.dram_tensor("scr", [scr_bytes // 512 // 2, 512], BF16)
    if scrw_bytes:
        dram["scrW"] = nc.dram_tensor("scrW", [scrw_bytes // 512 // 2, 512], BF16)
    for kname, in_shape, out_shape, dt in kv_specs:
        dram[kname + "_in"] = nc.dram_tensor(kname + "_in", list(in_shape), dt)
        dram[kname + "_out"] = nc.dram_tensor(kname + "_out", list(out_shape), dt)

    from contextlib import ExitStack
    with ExitStack() as ctx:
        xres = ctx.enter_context(nc.sbuf_tensor("xres", [128, NCH * T], F32))
        yh = ctx.enter_context(nc.sbuf_tensor("yh", [128, NCH * TH], BF16))
        ws = ctx.enter_context(nc.sbuf_tensor("ws", [128, 31232], U8))
        rows = ctx.enter_context(nc.sbuf_tensor("rows", [128, 1536], F32))
        rowsb = ctx.enter_context(nc.sbuf_tensor("rowsb", [128, 2048], BF16))
        xb = ctx.enter_context(nc.sbuf_tensor("xb", [128, 2048], BF16))
        tscr = ctx.enter_context(nc.sbuf_tensor("tscr", [128, 512], F32))
        ones_bf = ctx.enter_context(nc.sbuf_tensor("ones_bf", [128, 128], BF16))
        id_bf = ctx.enter_context(nc.sbuf_tensor("id_bf", [128, 128], BF16))
        id_f = ctx.enter_context(nc.sbuf_tensor("id_f", [128, 128], F32))
        pA = ctx.enter_context(nc.psum_tensor("pA", [128, 1024], F32))
        pB = ctx.enter_context(nc.psum_tensor("pB", [128, 1024], F32))
        pC2 = ctx.enter_context(nc.psum_tensor("pC2", [128, 1024], F32))
        pE = ctx.enter_context(nc.psum_tensor("pE", [128, 512], F32))
        pT = ctx.enter_context(nc.psum_tensor("pT", [128, 512], BF16))
        s_ld = ctx.enter_context(nc.semaphore("s_ld"))
        s_gd = ctx.enter_context(nc.semaphore("s_gd"))
        s_pe = ctx.enter_context(nc.semaphore("s_pe"))
        s_ve = ctx.enter_context(nc.semaphore("s_ve"))
        s_ac = ctx.enter_context(nc.semaphore("s_ac"))
        s_po = ctx.enter_context(nc.semaphore("s_po"))
        s_cc = ctx.enter_context(nc.semaphore("s_cc"))
        s_chv = ctx.enter_context(nc.semaphore("s_chv"))
        s_chs = ctx.enter_context(nc.semaphore("s_chs"))
        s_chg = ctx.enter_context(nc.semaphore("s_chg"))
        blk = ctx.enter_context(nc.Block())
        sems = {"ld": s_ld, "gd": s_gd, "pe": s_pe, "ve": s_chv, "ac": s_chs,
                "po": s_chg, "cc": s_cc, "ch_vector": s_chv, "ch_scalar": s_chs,
                "ch_gpsimd": s_chg}
        sb = {"xres": xres[:, :], "yh": yh[:, :], "ws": ws[:, :], "rows": rows[:, :],
              "rowsb": rowsb[:, :], "xb": xb[:, :], "tscr": tscr[:, :],
              "ones_bf": ones_bf[:, :], "id_bf": id_bf[:, :], "id_f": id_f[:, :]}
        ps = {"pA": pA[:, :], "pB": pB[:, :], "pC2": pC2[:, :], "pE": pE[:, :],
              "pT": pT[:, :]}
        wall_aps = {}
        atlas = {}
        for p in use_packs:
            wall_aps[p] = dram["W" + p].ap()
            atlas[p] = PACK_OFFS[p]
        kb = KB(nc, blk, sems, sb, ps, wall_aps, atlas, dram, cst.ap(), catlas)
        emit_preamble(kb, use_packs)
        for em in emits:
            em(kb)
        if store:
            emit_store_x(kb)
        kb.finalize()
        global LAST_CNT
        LAST_CNT = dict(kb.cnt)
        for n, rec in kb.recs.items():
            LAST_CNT["n_" + n] = len(rec.ops)
        LAST_CNT["ve_chain"] = kb.recs["vector"].chn
        LAST_CNT["ac_chain"] = kb.recs["scalar"].chn
        LAST_CNT["po_chain"] = kb.recs["gpsimd"].chn
    return nc


LAST_CNT = {}


PACK_OFFS = {}



_CACHE = {}


CONSTS_PARTS = ("s2", "att", "s3", "s5")


def _consts_pack(core, inputs, parts=CONSTS_PARTS):
    P = Pack(np.float32)
    P.add("zero", np.zeros((128, 4), np.float32))
    if "s2" in parts:
        s2_consts(inputs, core % 8, P)
    if "att" in parts:
        att_consts(core, P)
    if "s3" in parts:
        s3_consts(inputs, P, core)
    if "s5" in parts:
        att5_consts(core, P)
    return P.finish(), dict(P.offs)


SUB_ORDER = ["s1", "s2", "s3", "s4", "s5", "s6", "s7", "s8"]
SUB_KEY = {"s4": "s2", "s8": "s6"}


def _build_sub(which, inputs):
    PB = Pack(np.float32)
    PF = Pack(np.float32)
    PI = Pack(np.int32)
    kwargs = {}
    if which == "s2":
        cparts = ("s2",)
        use = ("B",)
        emits = [emit_s2]
        kwargs = dict(scr_bytes=2 * S2_END + 1024, scrw_bytes=2 * S2_WEND + 1024,
                      int_rows=S2L_ROWS)
    elif which == "s6":
        pack_s6(inputs, PB, PF, PI)
        cparts = ()
        use = ("B",)
        emits = [emit_s6]
        kwargs = dict(scr_bytes=65536)
    elif which == "s1":
        pack_s1(inputs, PB)
        cparts = ("att",)
        use = ("B",)
        emits = [emit_s1]
        kwargs = dict(scr_bytes=2 * S1_END + 1024)
    elif which == "s7":
        pack_s7(inputs, PB, PF)
        cparts = ("att",)
        use = ("B", "F")
        emits = [emit_s7]
        kwargs = dict(scr_bytes=2 * S7_END + 1024)
    elif which == "s3":
        pack_s3(inputs, PB, PF)
        cparts = ("s3",)
        use = ("B", "F")
        emits = [emit_s3]
        kwargs = dict(scr_bytes=2 * S3_END + 1024, halo=True)
    elif which == "s5":
        pack_s5(inputs, PB)
        cparts = ("s5",)
        use = ("B",)
        emits = [emit_s5]
        kwargs = dict(scr_bytes=2 * S5_END + 1024)
    packs = {"B": PB.finish(), "F": PF.finish(), "I": PI.finish()}
    PACK_OFFS["B"] = dict(PB.offs)
    PACK_OFFS["F"] = dict(PF.offs)
    PACK_OFFS["I"] = dict(PI.offs)
    csts = [_consts_pack(c, inputs, cparts) for c in range(8)]
    nc = _CACHE.get(which)
    if nc is None:
        nc = build_program(which, emits, packs, csts[0][1], csts[0][0].shape[0],
                           use, **kwargs)
        _CACHE[which] = nc
    return nc, packs, csts, use


def _run_sub(which, built, x, inputs, ishards):
    nc, packs, csts, use = built
    xT = [np.ascontiguousarray(x[c // 4].T[:, (c % 4) * T:(c % 4 + 1) * T])
          for c in range(8)]
    in_maps = []
    for c in range(8):
        m = {"xT": xT[c], "cst": csts[c][0]}
        for p in use:
            R = packs[p].shape[0]
            m["wsh" + p] = packs[p][(c * R // 8):((c + 1) * R // 8)]
        if which == "s2":
            m["wshI"] = ishards[c]
        if which == "s3":
            m["xh"] = halo_cols(x, c)
        in_maps.append(m)
    res = run_bass_kernel_spmd(nc, in_maps, core_ids=list(range(8)))
    out = np.empty((2, S, D), np.float32)
    for c in range(8):
        out[c // 4, (c % 4) * T:(c % 4 + 1) * T, :] = res.results[c]["xO"].T
    return out


def _build_mega(inputs):
    global PK_S7
    PB = Pack(np.float32)
    PC = Pack(np.float32)
    PF = Pack(np.float32)
    PI = Pack(np.int32)
    pack_s1(inputs, PB)
    pack_s3(inputs, PB, PF)
    pack_s6(inputs, PB, PF, PI)
    pack_s7(inputs, PC, PF)
    pack_s5(inputs, PC)
    packs = {"B": PB.finish(), "C": PC.finish(), "F": PF.finish(),
             "I": PI.finish()}
    PACK_OFFS["B"] = dict(PB.offs)
    PACK_OFFS["C"] = dict(PC.offs)
    PACK_OFFS["F"] = dict(PF.offs)
    PACK_OFFS["I"] = dict(PI.offs)
    csts = [_consts_pack(c, inputs, CONSTS_PARTS) for c in range(8)]
    import os
    nst = int(os.environ.get("KSTAGES", "8"))
    key = "mega%d" % nst
    nc = _CACHE.get(key)
    if nc is None:
        PK_S7 = "C"
        nc = build_program(key, [lambda kb: emit_mega(kb, nst)], packs,
                           csts[0][1], csts[0][0].shape[0], ("B", "C", "F"),
                           scr_bytes=2 * S7_END + 1024,
                           scrw_bytes=2 * S2_WEND + 1024, int_rows=S2L_ROWS)
        PK_S7 = "B"
        _CACHE[key] = nc
    return nc, packs, csts


def kernel(**inputs):
    import os
    which = os.environ.get("KSUB", "mega")
    x = np.asarray(inputs["L0_attn_a_x"], np.float32)
    if which == "mega":
        nc, packs, csts = _build_mega(inputs)
        ishards = [s2_int_shard(inputs, c % 8) for c in range(8)]
        xT = [np.ascontiguousarray(x[c // 4].T[:, (c % 4) * T:(c % 4 + 1) * T])
              for c in range(8)]
        in_maps = []
        for c in range(8):
            m = {"xT": xT[c], "cst": csts[c][0], "wshI": ishards[c]}
            for p in ("B", "C", "F"):
                R = packs[p].shape[0]
                m["wsh" + p] = packs[p][(c * R // 8):((c + 1) * R // 8)]
            in_maps.append(m)
        res = run_bass_kernel_spmd(nc, in_maps, core_ids=list(range(8)))
        out = np.empty((2, S, D), np.float32)
        for c in range(8):
            out[c // 4, (c % 4) * T:(c % 4 + 1) * T, :] = res.results[c]["xO"].T
        return out
    subs = SUB_ORDER if which == "all" else [which]
    ishards = None
    built = {}
    for sub in subs:
        key = SUB_KEY.get(sub, sub)
        if key == "s2" and ishards is None:
            ishards = [s2_int_shard(inputs, c % 8) for c in range(8)]
        if key not in built:
            built[key] = _build_sub(key, inputs)
        x = _run_sub(key, built[key], x, inputs, ishards)
    return x


IN2 = 11008
NIC = 86

def _pi_perm():
    p = np.zeros(128, np.int64)
    for j in range(4):
        for c in range(16):
            for b in range(2):
                p[j * 32 + 2 * c + b] = 8 * c + 4 * b + j
    return p


def _perm_cols_128(a):
    p = _pi_perm()
    R, N = a.shape
    a2 = a.reshape(R, N // 128, 128)[:, :, p]
    return np.ascontiguousarray(a2.reshape(R, N))


def pack_s2(inputs, PB, PF, PI):
    pass


S2_KCPR = 4
S2_ICPR = 11
S2L_QG = 0
S2L_ZG = 128 * S2_KCPR * 2752
S2L_QD = S2L_ZG + S2_KCPR * 2752
S2L_ZD = S2L_QD + 128 * S2_ICPR * 512
S2L_END = S2L_ZD + S2_ICPR * 512
S2L_ROWS = (-(-S2L_END // 512) // 8 * 8 + 8)


def s2_int_shard(inputs, r):
    qgu = np.asarray(inputs["L0_ffn_qweight_gu"])
    qzg = np.asarray(inputs["L0_ffn_qzeros_gu"])
    qdn = np.asarray(inputs["L0_ffn_qweight_dn"])
    qzd = np.asarray(inputs["L0_ffn_qzeros_dn"])
    gimg = img_lhsT(qgu)
    dimg = img_lhsT(qdn)
    out = np.zeros(S2L_ROWS * 512, np.int32)
    out[S2L_QG:S2L_QG + 128 * S2_KCPR * 2752] = \
        gimg[:, r * S2_KCPR * 2752:(r + 1) * S2_KCPR * 2752].reshape(-1)
    out[S2L_ZG:S2L_ZG + S2_KCPR * 2752] = \
        qzg[r * S2_KCPR:(r + 1) * S2_KCPR].reshape(-1)
    ics = list(range(r * S2_ICPR, min((r + 1) * S2_ICPR, NIC)))
    dpart = np.zeros((128, S2_ICPR * 512), np.int32)
    dpart[:, 0:len(ics) * 512] = dimg[:, ics[0] * 512:(ics[-1] + 1) * 512]
    out[S2L_QD:S2L_QD + 128 * S2_ICPR * 512] = dpart.reshape(-1)
    zpart = np.zeros((S2_ICPR, 512), np.int32)
    zpart[0:len(ics)] = qzd[ics[0]:ics[-1] + 1]
    out[S2L_ZD:S2L_ZD + S2_ICPR * 512] = zpart.reshape(-1)
    return out.reshape(S2L_ROWS, 512)


def s2_consts(inputs, r, CP):
    sg = np.asarray(inputs["L0_ffn_scales_gu"], np.float32)
    sd = np.asarray(inputs["L0_ffn_scales_dn"], np.float32)
    sgp = np.concatenate([_perm_cols_128(sg[:, :IN2]), _perm_cols_128(sg[:, IN2:])],
                         axis=1)
    CP.add_flat("s2_sg", sgp[r * S2_KCPR:(r + 1) * S2_KCPR].reshape(-1))
    ics = list(range(r * S2_ICPR, min((r + 1) * S2_ICPR, NIC)))
    sdp = np.zeros((S2_ICPR, 4096), np.float32)
    sdp[0:len(ics)] = sd[ics[0]:ics[-1] + 1]
    CP.add_flat("s2_sd", sdp.reshape(-1))


I16 = mybir.dt.int16I16 = mybir.dt.int16


def _unpack_pi(ve, src_u16, dst_u16):
    s3 = src_u16.rearrange("p (blk u) -> p blk u", u=32)
    d4 = dst_u16.rearrange("p (blk q u) -> p blk q u", q=4, u=32)
    last = None
    for j in range(4):
        last = ve.tensor_scalar(out=d4[:, :, j, :], in0=s3, scalar1=4 * j,
                                scalar2=15, op0=ALU.logical_shift_right,
                                op1=ALU.bitwise_and)
    return last


def _unpack_strided(ve, src_u16, dst_u16):
    sv = src_u16.rearrange("p (w b2) -> p w b2", b2=2)
    d = dst_u16.rearrange("p (w n) -> p w n", n=8)
    last = None
    for b in range(2):
        for j in range(4):
            last = ve.tensor_scalar(out=d[:, :, 4 * b + j], in0=sv[:, :, b],
                                    scalar1=4 * j, scalar2=15,
                                    op0=ALU.logical_shift_right,
                                    op1=ALU.bitwise_and)
    return last


S2_SGL, S2_ZGL = 0, 88064
S2_SGL2 = 176128
S2_SDL, S2_ZDL = 264192, 309248
S2_SDL2 = 354304
S2_CWG = 399360
S2_CWD = 11671552
S2_WDF = 17438720
S2_H0 = 63576064
S2_END = S2_H0 + IN2 * 1024
S2_WEND = 32 * 128 * 22016


def emit_s2_dequant(kb):
    pC2 = kb.ps["pC2"]
    ws = kb.sb["ws"]
    wsb, wsi, wsu, wsf = (ws.bitcast(BF16), ws.bitcast(I32), ws.bitcast(U16),
                          ws.bitcast(F32))
    scr = kb.dram["scr"].ap().rearrange("r c -> (r c)")
    ish = kb.dram["wshI"].ap().rearrange("r c -> (r c)")

    def r2(off, rows, cols):
        return scr[off: off + rows * cols].rearrange("(r c) -> r c", c=cols)

    def ri2(off, rows, cols):
        return ish[off: off + rows * cols].rearrange("(r c) -> r c", c=cols)

    t_, ve, s, g = kb.bodies("tensor", "vector", "sync", "gpsimd")
    s.wait_ge(kb.sems["cc"], kb.cnt["cc"])

    kb.inc(s.dma_start(out=wsi[0:S2_KCPR, 0:2752], in_=ri2(S2L_ZG, S2_KCPR, 2752)),
           "ld", 16)
    sg_c = kb.wflat
    sgl = kb.creg("s2_sg", rows=1)
    sg_f = kb.cst.rearrange("r c -> (r c)")
    off_sg, _ = kb.catlas["s2_sg"]
    off_sd, _ = kb.catlas["s2_sd"]
    sgu_f = sg_f[off_sg: off_sg + S2_KCPR * 22016].rearrange("(r c) -> r c", c=22016)
    sdn_f = sg_f[off_sd: off_sd + S2_ICPR * 4096].rearrange("(r c) -> r c", c=4096)
    for pc in range(43):
        if pc > 0:
            s.wait_ge(kb.sems["ch_vector"], kb.cnt["ve"])
        kb.inc(s.dma_start(out=wsf[0:S2_KCPR, 3584:3584 + 512],
                           in_=sgu_f[:, pc * 512:(pc + 1) * 512]), "ld", 16)
        vs = kb.cnt["ld"]
        ve.wait_ge(kb.sems["ld"], vs)
        ve.wait_ge(kb.sems["gd"], kb.cnt["gd"])
        _unpack_pi(ve, wsu[0:S2_KCPR, pc * 128:(pc + 1) * 128],
                   wsu[0:S2_KCPR, 6656:6656 + 512])
        ve.tensor_copy(wsb[0:S2_KCPR, 8192:8192 + 512], wsu[0:S2_KCPR, 6656:6656 + 512])
        ve.tensor_copy(wsb[0:S2_KCPR, 8704:8704 + 512],
                       wsf[0:S2_KCPR, 3584:3584 + 512])
        ve.tensor_sub(wsf[0:S2_KCPR, 4608:4608 + 512], wsf[0:S2_KCPR, 3584:3584 + 512],
                      wsb[0:S2_KCPR, 8704:8704 + 512])
        kb.inc(ve.tensor_copy(wsb[0:S2_KCPR, 9216:9216 + 512],
                              wsf[0:S2_KCPR, 4608:4608 + 512]), "ve", 1)
        g.wait_ge(kb.sems["ch_vector"], kb.cnt["ve"])
        kb.inc(g.dma_start(out=r2(S2_ZGL, S2_KCPR, 22016)[:, pc * 512:(pc + 1) * 512],
                           in_=wsb[0:S2_KCPR, 8192:8192 + 512]), "gd", 16)
        kb.inc(g.dma_start(out=r2(S2_SGL, S2_KCPR, 22016)[:, pc * 512:(pc + 1) * 512],
                           in_=wsb[0:S2_KCPR, 8704:8704 + 512]), "gd", 16)
        kb.inc(g.dma_start(out=r2(S2_SGL2, S2_KCPR, 22016)[:, pc * 512:(pc + 1) * 512],
                           in_=wsb[0:S2_KCPR, 9216:9216 + 512]), "gd", 16)
    v_zg = kb.cnt["gd"]
    s.wait_ge(kb.sems["gd"], v_zg)
    kb.inc(s.dma_start(out=wsi[0:S2_ICPR, 0:512], in_=ri2(S2L_ZD, S2_ICPR, 512)),
           "ld", 16)
    for pc in range(8):
        if pc > 0:
            s.wait_ge(kb.sems["ch_vector"], kb.cnt["ve"])
        kb.inc(s.dma_start(out=wsf[0:S2_ICPR, 3584:3584 + 512],
                           in_=sdn_f[:, pc * 512:(pc + 1) * 512]), "ld", 16)
        vs = kb.cnt["ld"]
        ve.wait_ge(kb.sems["ld"], vs)
        ve.wait_ge(kb.sems["gd"], kb.cnt["gd"])
        _unpack_strided(ve, wsu[0:S2_ICPR, pc * 128:(pc + 1) * 128],
                        wsu[0:S2_ICPR, 6656:6656 + 512])
        ve.tensor_copy(wsb[0:S2_ICPR, 8192:8192 + 512], wsu[0:S2_ICPR, 6656:6656 + 512])
        ve.tensor_copy(wsb[0:S2_ICPR, 8704:8704 + 512],
                       wsf[0:S2_ICPR, 3584:3584 + 512])
        ve.tensor_sub(wsf[0:S2_ICPR, 4608:4608 + 512], wsf[0:S2_ICPR, 3584:3584 + 512],
                      wsb[0:S2_ICPR, 8704:8704 + 512])
        kb.inc(ve.tensor_copy(wsb[0:S2_ICPR, 9216:9216 + 512],
                              wsf[0:S2_ICPR, 4608:4608 + 512]), "ve", 1)
        g.wait_ge(kb.sems["ch_vector"], kb.cnt["ve"])
        kb.inc(g.dma_start(out=r2(S2_ZDL, S2_ICPR, 4096)[:, pc * 512:(pc + 1) * 512],
                           in_=wsb[0:S2_ICPR, 8192:8192 + 512]), "gd", 16)
        kb.inc(g.dma_start(out=r2(S2_SDL, S2_ICPR, 4096)[:, pc * 512:(pc + 1) * 512],
                           in_=wsb[0:S2_ICPR, 8704:8704 + 512]), "gd", 16)
        kb.inc(g.dma_start(out=r2(S2_SDL2, S2_ICPR, 4096)[:, pc * 512:(pc + 1) * 512],
                           in_=wsb[0:S2_ICPR, 9216:9216 + 512]), "gd", 16)

    grp_sizes = [2] * 21 + [1]
    cwg = scr[S2_CWG: S2_CWG + S2_KCPR * 128 * 22016].rearrange(
        "(kc p c) -> kc p c", p=128, c=22016)
    v_row_prev = kb.cnt["ve"]
    v_bc_prev = None
    v_store_prev = None
    for kc in range(S2_KCPR):
        if v_row_prev is not None:
            s.wait_ge(kb.sems["ch_vector"], v_row_prev)
        kb.inc(s.dma_start(
            out=wsi[:, 0:2752],
            in_=ri2(S2L_QG, 128, S2_KCPR * 2752)[:, kc * 2752:(kc + 1) * 2752]),
            "ld", 16)
        v_q = kb.cnt["ld"]
        pc0 = 0
        for gi, gsz in enumerate(grp_sizes):
            ncol = gsz * 512
            if v_bc_prev is not None:
                s.wait_ge(kb.sems["pe"], v_bc_prev)
            kb.inc(s.dma_start(
                out=wsb[0:1, 5632:5632 + ncol],
                in_=r2(S2_SGL, S2_KCPR, 22016)[kc:kc + 1, pc0 * 512:pc0 * 512 + ncol]),
                "ld", 16)
            kb.inc(s.dma_start(
                out=wsb[0:1, 6656:6656 + ncol],
                in_=r2(S2_ZGL, S2_KCPR, 22016)[kc:kc + 1, pc0 * 512:pc0 * 512 + ncol]),
                "ld", 16)
            kb.inc(s.dma_start(
                out=wsb[0:1, 7680:7680 + ncol],
                in_=r2(S2_SGL2, S2_KCPR, 22016)[kc:kc + 1, pc0 * 512:pc0 * 512 + ncol]),
                "ld", 16)
            v_rows = kb.cnt["ld"]
            for q in range(gsz):
                pc = pc0 + q
                t_.wait_ge(kb.sems["ld"], v_rows)
                if v_row_prev is not None:
                    t_.wait_ge(kb.sems["ch_vector"], v_row_prev)
                t_.matmul(pC2[:, 0:512], kb.sb["ones_bf"][0:1, :],
                          wsb[0:1, 5632 + q * 512:5632 + (q + 1) * 512],
                          start=True, stop=False)
                t_.matmul(pC2[:, 0:512], kb.sb["ones_bf"][0:1, :],
                          wsb[0:1, 7680 + q * 512:7680 + (q + 1) * 512],
                          start=False, stop=True)
                kb.inc(t_.matmul(pC2[:, 512:1024], kb.sb["ones_bf"][0:1, :],
                                 wsb[0:1, 6656 + q * 512:6656 + (q + 1) * 512],
                                 start=True, stop=True), "pe", 1)
                v_bc = kb.cnt["pe"]
                v_bc_prev = v_bc
                ve.wait_ge(kb.sems["ld"], v_q)
                _unpack_pi(ve, wsu[:, pc * 128:(pc + 1) * 128], wsu[:, 8704:8704 + 512])
                ve.tensor_copy(wsf[:, 4608:4608 + 512], wsu[:, 8704:8704 + 512])
                ve.wait_ge(kb.sems["pe"], v_bc)
                if q == 0 and v_store_prev is not None:
                    ve.wait_ge(kb.sems["gd"], v_store_prev)
                ve.tensor_sub(wsf[:, 4608:4608 + 512], wsf[:, 4608:4608 + 512],
                              pC2[:, 512:1024])
                kb.inc(ve.tensor_mul(
                    wsb[:, 10240 + q * 512:10240 + q * 512 + 512],
                    wsf[:, 4608:4608 + 512], pC2[:, 0:512]), "ve", 1)
                v_row_prev = kb.cnt["ve"]
            g.wait_ge(kb.sems["ch_vector"], kb.cnt["ve"])
            kb.inc(g.dma_start(out=cwg[kc, :, pc0 * 512: pc0 * 512 + ncol],
                               in_=wsb[:, 10240:10240 + ncol]), "gd", 16)
            v_store_prev = kb.cnt["gd"]
            pc0 += gsz

    cwd = scr[S2_CWD: S2_CWD + S2_ICPR * 128 * 4096].rearrange(
        "(s p c) -> s p c", p=128, c=4096)
    v_dstore = None
    for ic in range(S2_ICPR):
        if v_row_prev is not None:
            s.wait_ge(kb.sems["ch_vector"], v_row_prev)
        kb.inc(s.dma_start(
            out=wsi[:, 0:512],
            in_=ri2(S2L_QD, 128, S2_ICPR * 512)[:, ic * 512:(ic + 1) * 512]), "ld", 16)
        v_qd = kb.cnt["ld"]
        for oh in range(2):
            if oh == 1:
                s.wait_ge(kb.sems["pe"], kb.cnt["pe"])
            kb.inc(s.dma_start(
                out=wsb[0:1, 5632:5632 + 2048],
                in_=r2(S2_SDL, S2_ICPR, 4096)[ic:ic + 1, oh * 2048:(oh + 1) * 2048]),
                "ld", 16)
            kb.inc(s.dma_start(
                out=wsb[0:1, 7680:7680 + 2048],
                in_=r2(S2_SDL2, S2_ICPR, 4096)[ic:ic + 1, oh * 2048:(oh + 1) * 2048]),
                "ld", 16)
            kb.inc(s.dma_start(
                out=wsb[0:1, 9728:9728 + 2048],
                in_=r2(S2_ZDL, S2_ICPR, 4096)[ic:ic + 1, oh * 2048:(oh + 1) * 2048]),
                "ld", 16)
            v_in = kb.cnt["ld"]
            ve.wait_ge(kb.sems["ld"], v_qd)
            if v_row_prev is not None:
                ve.wait_ge(kb.sems["ch_vector"], v_row_prev)
            _unpack_strided(ve, wsu[:, oh * 512:(oh + 1) * 512],
                            wsu[:, 1024:1024 + 2048])
            kb.inc(ve.engine_nop(), "ve", 1)
            for q in range(4):
                oc = oh * 2048 + q * 512
                t_.wait_ge(kb.sems["ld"], v_in)
                if v_row_prev is not None:
                    t_.wait_ge(kb.sems["ch_vector"], v_row_prev)
                t_.matmul(pC2[:, 0:512], kb.sb["ones_bf"][0:1, :],
                          wsb[0:1, 5632 + q * 512:5632 + (q + 1) * 512],
                          start=True, stop=False)
                t_.matmul(pC2[:, 0:512], kb.sb["ones_bf"][0:1, :],
                          wsb[0:1, 7680 + q * 512:7680 + (q + 1) * 512],
                          start=False, stop=True)
                kb.inc(t_.matmul(pC2[:, 512:1024], kb.sb["ones_bf"][0:1, :],
                                 wsb[0:1, 9728 + q * 512:9728 + (q + 1) * 512],
                                 start=True, stop=True), "pe", 1)
                v_bc = kb.cnt["pe"]
                ve.tensor_copy(wsf[:, 1536:1536 + 512],
                               wsu[:, 1024 + q * 512:1024 + (q + 1) * 512])
                ve.wait_ge(kb.sems["pe"], v_bc)
                if v_dstore is not None:
                    ve.wait_ge(kb.sems["gd"], v_dstore)
                ve.tensor_sub(wsf[:, 1536:1536 + 512], wsf[:, 1536:1536 + 512],
                              pC2[:, 512:1024])
                kb.inc(ve.tensor_mul(wsb[:, 4096:4096 + 512],
                                     wsf[:, 1536:1536 + 512], pC2[:, 0:512]), "ve", 1)
                v_row_prev = kb.cnt["ve"]
                g.wait_ge(kb.sems["ch_vector"], v_row_prev)
                kb.inc(g.dma_start(out=cwd[ic, :, oc: oc + 512],
                                   in_=wsb[:, 4096:4096 + 512]), "gd", 16)
                v_dstore = kb.cnt["gd"]

    g.wait_ge(kb.sems["gd"], kb.cnt["gd"])
    scr2d = kb.dram["scr"].ap()
    inst = g.collective_compute(
        "AllGather", ALU.bypass, replica_groups=ALLG,
        ins=[scr2d[S2_CWG // 512: S2_CWG // 512 + 22016, :].opt()],
        outs=[kb.dram["scrW"].ap()[0:176128, :].opt()])
    kb.inc(inst, "cc", 1)
    g.wait_ge(kb.sems["cc"], kb.cnt["cc"])
    inst = g.collective_compute(
        "AllGather", ALU.bypass, replica_groups=ALLG,
        ins=[scr2d[S2_CWD // 512: S2_CWD // 512 + 11264, :].opt()],
        outs=[scr2d[S2_WDF // 512: S2_WDF // 512 + 90112, :].opt()])
    kb.inc(inst, "cc", 1)
    g.wait_ge(kb.sems["cc"], kb.cnt["cc"])
    return kb.cnt["cc"]


def emit_s2(kb, dequant=True):
    pA, pB, pC2, pE = kb.ps["pA"], kb.ps["pB"], kb.ps["pC2"], kb.ps["pE"]
    ws = kb.sb["ws"]
    wsb = ws.bitcast(BF16)
    scr = kb.dram["scr"].ap().rearrange("r c -> (r c)")
    WG5 = kb.dram["scrW"].ap().rearrange("r c -> (r c)")[0:32 * 128 * 22016] \
        .rearrange("(kc p c) -> kc p c", p=128, c=22016)
    WD5 = scr[S2_WDF: S2_WDF + 88 * 128 * 4096].rearrange(
        "(s p c) -> s p c", p=128, c=4096)

    v_deq = emit_s2_dequant(kb) if dequant else kb.cnt["cc"]

    v_mm_hist = []
    v_ev_hist = []
    for h in range(2):
        ydone = emit_yprod(kb, h)
        t_, ve, ac, s, g = kb.bodies("tensor", "vector", "scalar", "sync", "gpsimd")
        s.wait_ge(kb.sems["cc"], v_deq)
        for nc in range(NIC):
            gnc = h * NIC + nc
            pg = pA if nc % 2 == 0 else pB
            for kh in range(2):
                bi = (nc * 2 + kh) % 2
                B0 = bi * 4096
                if len(v_mm_hist) >= 2:
                    s.wait_ge(kb.sems["pe"], v_mm_hist[-2])
                kb.inc(s.dma_start(
                    out=wsb[:, B0:B0 + 2048].rearrange("p (k m) -> p k m", m=128),
                    in_=WG5[kh * 16:kh * 16 + 16, :, nc * 128:(nc + 1) * 128]
                    .rearrange("kc p c -> p kc c")), "ld", 16)
                kb.inc(s.dma_start(
                    out=wsb[:, B0 + 2048:B0 + 4096].rearrange("p (k m) -> p k m", m=128),
                    in_=WG5[kh * 16:kh * 16 + 16, :, IN2 + nc * 128:IN2 + (nc + 1) * 128]
                    .rearrange("kc p c -> p kc c")), "ld", 16)
                v_ld = kb.cnt["ld"]
                t_.wait_ge(kb.sems["ld"], v_ld)
                if nc == 0 and kh == 0:
                    t_.wait_ge(kb.sems["ve"], ydone)
                if gnc >= 2 and kh == 0:
                    t_.wait_ge(kb.sems["ve"], v_ev_hist[gnc - 2])
                last = None
                for k in range(16):
                    kc = kh * 16 + k
                    last = t_.matmul(pg[:, 0:512], wsb[:, B0 + k * 128:B0 + (k + 1) * 128],
                                     kb.sb["yh"][:, kc * TH:(kc + 1) * TH],
                                     start=(kc == 0), stop=(kc == 31))
                    last = t_.matmul(pg[:, 512:1024],
                                     wsb[:, B0 + 2048 + k * 128:B0 + 2048 + (k + 1) * 128],
                                     kb.sb["yh"][:, kc * TH:(kc + 1) * TH],
                                     start=(kc == 0), stop=(kc == 31))
                kb.inc(last, "pe", 1)
                v_mm_hist.append(kb.cnt["pe"])
            v_mm = kb.cnt["pe"]
            ac.wait_ge(kb.sems["pe"], v_mm)
            wsf = kb.sb["ws"].bitcast(F32)
            kb.inc(ac.activation(out=wsf[:, 4096:4096 + 512], in_=pg[:, 0:512],
                                 func=AF.Silu), "ac", 1)
            hb = 9216 + (nc % 2) * 512
            ve.wait_ge(kb.sems["ac"], kb.cnt["ac"])
            ve.wait_ge(kb.sems["pe"], v_mm)
            if nc >= 2:
                ve.wait_ge(kb.sems["gd"], kb.cnt["gd"])
            kb.inc(ve.tensor_mul(wsb[:, hb:hb + 512], wsf[:, 4096:4096 + 512],
                                 pg[:, 512:1024]), "ve", 1)
            v_ev_hist.append(kb.cnt["ve"])
            g.wait_ge(kb.sems["ch_vector"], kb.cnt["ve"])
            row_view = scr[S2_H0 + nc * 128 * 1024: S2_H0 + (nc + 1) * 128 * 1024] \
                .rearrange("(r t) -> r t", t=1024)
            rv = row_view.rearrange("(c b j) t -> j c b t", c=16, b=2, j=4)
            kb.inc(g.dma_start(out=rv[:, :, :, h * 512:(h + 1) * 512],
                               in_=wsb[:, hb:hb + 512]), "gd", 16)

    v_evB = None
    for h in range(2):
        t_, ve, ac, s = kb.bodies("tensor", "vector", "scalar", "sync")
        s.wait_ge(kb.sems["gd"], kb.cnt["gd"])
        stats_st = kb._s2_stats if h == 1 else {}
        v_prev = kb.cnt["pe"]
        for mg in range(8):
            v_ld_hist = []
            for ic in range(NIC):
                bi = ic % 4
                s.wait_ge(kb.sems["pe"], v_prev)
                s.wait_ge(kb.sems["pe"], kb.cnt["pe"])
                if len(v_ld_hist) >= 4:
                    s.wait_ge(kb.sems["pe"], v_ld_hist[-4][1])
                if ic == 0 and v_evB is not None:
                    t_.wait_ge(kb.sems["ve"], v_evB)
                kb.inc(s.dma_start(out=wsb[:, bi * 512:bi * 512 + 512],
                                   in_=WD5[ic, :, mg * 512:(mg + 1) * 512]), "ld", 16)
                hsrc = scr[S2_H0 + ic * 128 * 1024: S2_H0 + (ic + 1) * 128 * 1024] \
                    .rearrange("(r t) -> r t", t=1024)
                kb.inc(s.dma_start(out=wsb[:, 2048 + bi * 512:2048 + bi * 512 + 512],
                                   in_=hsrc[:, h * 512:(h + 1) * 512]), "ld", 16)
                v_ld = kb.cnt["ld"]
                t_.wait_ge(kb.sems["ld"], v_ld)
                if mg == 0 and ic == 0 and h == 0:
                    t_.wait_ge(kb.sems["gd"], kb.cnt["gd"])
                    t_.wait_ge(kb.sems["ve"], v_ev_hist[-1])
                last = None
                for i in range(4):
                    pt = pA if i < 2 else pB
                    c0 = (i % 2) * 512
                    last = t_.matmul(pt[:, c0:c0 + 512],
                                     wsb[:, bi * 512 + i * 128:bi * 512 + (i + 1) * 128],
                                     wsb[:, 2048 + bi * 512:2048 + bi * 512 + 512],
                                     start=(ic == 0), stop=(ic == NIC - 1))
                kb.inc(last, "pe", 1)
                v_ld_hist.append((v_ld, kb.cnt["pe"]))
            v_out = kb.cnt["pe"]
            ve.wait_ge(kb.sems["pe"], v_out)
            for i in range(4):
                m = mg * 4 + i
                pt = pA if i < 2 else pB
                c0 = (i % 2) * 512
                xs = kb.sb["xres"][:, m * T + h * TH: m * T + h * TH + TH]
                kb.inc(ve.tensor_add(xs, xs, pt[:, c0:c0 + 512]), "ve", 1)
                v_evB = kb.cnt["ve"]
                emit_stats_chunk(kb, ac, ve, t_, m, h, first=(mg == 0 and i == 0),
                                 st=stats_st)
            v_prev = kb.cnt["pe"]
        kb._s2_stats = stats_st



def att_consts(core, CP):
    r = core % 4
    pos = np.arange(r * T, (r + 1) * T, dtype=np.float32)
    inv = 1.0 / (10000.0 ** (np.arange(64, dtype=np.float32) / 64.0))
    ang = inv[:, None] * pos[None, :]
    cos = np.concatenate([np.cos(ang), np.cos(ang)], axis=0)
    sin = np.concatenate([-np.sin(ang), np.sin(ang)], axis=0)
    CP.add("rope_cos", cos.astype(np.float32))
    CP.add("rope_sin", sin.astype(np.float32))
    m = np.zeros((128, 8 * 8 * 512), np.float32)
    for qt in range(8):
        q0 = r * T + qt * 128
        for kb in range(8):
            k0 = kb * 512
            blk = np.zeros((128, 512), np.float32)
            qi = q0 + np.arange(128)[:, None]
            ki = k0 + np.arange(512)[None, :]
            blk[qi < ki] = -1e9
            m[:, (qt * 8 + kb) * 512:(qt * 8 + kb + 1) * 512] = blk
    u = np.ascontiguousarray(m.astype(BF)).view(np.uint16).reshape(-1, 2)
    packed = np.ascontiguousarray(u).view(np.uint32).reshape(-1).view(np.float32)
    CP.add_flat("m1", packed)
    return


def pack_s1(inputs, PB):
    PB.add("w1q", img_lhsT_strips(np.asarray(inputs["L0_attn_a_Wq"], np.float32)))
    PB.add("w1k", img_lhsT_strips(np.asarray(inputs["L0_attn_a_Wk"], np.float32)))
    PB.add("w1v", img_lhsT_strips(np.asarray(inputs["L0_attn_a_Wv"], np.float32)))
    PB.add("w1o", img_lhsT(np.asarray(inputs["L0_attn_a_Wo"], np.float32)))


def pack_s7(inputs, PB, PF):
    PB.add("w7qkv", img_lhsT_strips(np.asarray(inputs["L1_attn_b_qkv_w"], np.float32)))
    PB.add("w7o", img_lhsT(np.asarray(inputs["L1_attn_b_out_w"], np.float32)))
    PF.add_flat("b7qkv", np.asarray(inputs["L1_attn_b_qkv_b"], np.float32).reshape(-1))
    PF.add_flat("b7o", np.asarray(inputs["L1_attn_b_out_b"], np.float32).reshape(-1))


def emit_proj_epi_rope(kb, ve, pg, c0, cosb, sinb, tsc2, dst_bf, v_mm):
    tsc = kb.sb["tscr"]
    ve.wait_ge(kb.sems["pe"], v_mm)
    ve.tensor_mul(tsc[:, :], pg[:, c0:c0 + TH], cosb)
    ve.tensor_mul(tsc2[0:64, :], pg[64:128, c0:c0 + TH], sinb[0:64, :])
    ve.tensor_mul(tsc2[64:128, :], pg[0:64, c0:c0 + TH], sinb[64:128, :])
    return ve.tensor_add(dst_bf, tsc[:, :], tsc2[:, :])


def emit_outproj(kb, wname, src_dram_off, stats_first, bias_name=None, nk=NCH,
                 pk="B"):
    pA, pB = kb.ps["pA"], kb.ps["pB"]
    ws = kb.sb["ws"]
    wsb = ws.bitcast(BF16)
    scr = kb.dram["scr"].ap().rearrange("r c -> (r c)")
    src = scr[src_dram_off: src_dram_off + nk * 128 * 1024].rearrange(
        "(k p t) -> k p t", p=128, t=1024)
    wsf = ws.bitcast(F32)
    brow = wsf[:, 1024:1028]
    if bias_name is not None:
        bflat = kb.wflat("F", bias_name)
    v_ev = None
    stats_st = {}
    v_src_ready = kb.cnt["gd"]
    for h in range(2):
        t_, ve, ac, s = kb.bodies("tensor", "vector", "scalar", "sync")
        s.wait_ge(kb.sems["gd"], v_src_ready)
        v_prev = kb.cnt["pe"]
        for mg in range(8):
            if bias_name is not None:
                s.wait_ge(kb.sems["ch_vector"], kb.cnt["ve"])
                kb.inc(s.dma_start(
                    out=brow,
                    in_=bflat[mg * 512:(mg + 1) * 512]
                    .rearrange("(m p) -> p m", p=128),
                    allow_slow_non_contiguous=True), "ld", 16)
            v_ld_hist = []
            for k in range(nk):
                bi = k % 2
                s.wait_ge(kb.sems["pe"], v_prev)
                if len(v_ld_hist) >= 2:
                    s.wait_ge(kb.sems["pe"], v_ld_hist[-2])
                kb.inc(s.dma_start(
                    out=wsb[:, bi * 512:bi * 512 + 512],
                    in_=kb.wreg(pk, wname, c0=k * 4096 + mg * 512, cols=512)),
                    "ld", 16)
                kb.inc(s.dma_start(
                    out=wsb[:, 1024 + bi * 512:1024 + bi * 512 + 512],
                    in_=src[k, :, h * TH:(h + 1) * TH]), "ld", 16)
                v_ld = kb.cnt["ld"]
                t_.wait_ge(kb.sems["ld"], v_ld)
                if k == 0 and v_ev is not None:
                    t_.wait_ge(kb.sems["ve"], v_ev)
                last = None
                for i in range(4):
                    pt = pA if i < 2 else pB
                    c0 = (i % 2) * 512
                    last = t_.matmul(pt[:, c0:c0 + 512],
                                     wsb[:, bi * 512 + i * 128:bi * 512 + (i + 1) * 128],
                                     wsb[:, 1024 + bi * 512:1024 + bi * 512 + 512],
                                     start=(k == 0), stop=(k == nk - 1))
                kb.inc(last, "pe", 1)
                v_ld_hist.append(kb.cnt["pe"])
            v_out = kb.cnt["pe"]
            ve.wait_ge(kb.sems["pe"], v_out)
            if bias_name is not None:
                ve.wait_ge(kb.sems["ld"], kb.cnt["ld"])
            for i in range(4):
                m = mg * 4 + i
                pt = pA if i < 2 else pB
                c0 = (i % 2) * 512
                xs = kb.sb["xres"][:, m * T + h * TH: m * T + h * TH + TH]
                if bias_name is not None:
                    ve.tensor_scalar_add(kb.sb["tscr"][:, :], pt[:, c0:c0 + 512],
                                         brow[:, i:i + 1])
                    kb.inc(ve.tensor_add(xs, xs, kb.sb["tscr"][:, :]), "ve", 1)
                else:
                    kb.inc(ve.tensor_add(xs, xs, pt[:, c0:c0 + 512]), "ve", 1)
                v_ev = kb.cnt["ve"]
                emit_stats_chunk(kb, ac, ve, t_, m, h,
                                 first=(stats_first and mg == 0 and i == 0 and h == 0)
                                 or (stats_first and mg == 0 and i == 0 and h == 1),
                                 st=stats_st)
            v_prev = kb.cnt["pe"]


PK_S7 = "B"
MK_XBASE = 74848256
S1_Q, S1_AO = MK_XBASE, MK_XBASE + 4194304
S1_CCI, S1_CCO = MK_XBASE + 8388608, MK_XBASE + 10485760
S1_KPER = 2097152
S1_END = MK_XBASE + 18874368
S7_Q, S7_AO = MK_XBASE, MK_XBASE + 4194304
S7_CCI, S7_CCO = MK_XBASE + 8388608, MK_XBASE + 16777216
S7_KPER = 8388608
S7_END = MK_XBASE + 50331648


def emit_attn_proj(kb, wspec, nq, nkv, qoff, ccin_off, bias_name=None,
                   tabs=("rope_cos", "rope_sin"), pk="B"):
    pA, pB, pT = kb.ps["pA"], kb.ps["pB"], kb.ps["pT"]
    ws = kb.sb["ws"]
    wsb = ws.bitcast(BF16)
    wsf = ws.bitcast(F32)
    scr = kb.dram["scr"].ap().rearrange("r c -> (r c)")
    ntot = nq + 2 * nkv
    cosb = wsf[:, 4096:4096 + 512]
    sinb = wsf[:, 4608:4608 + 512]
    tsc2 = wsf[:, 5632:5632 + 512]
    rout = wsb[:, 12288:12288 + 512]
    vstage = wsb[:, 12800:12800 + 512]
    ptse = wsb[:, 13312:13312 + 512]
    btile = wsb[:, 13824:13824 + 128]
    v_mm_hist = {}
    v_done = {}
    for h in range(2):
        ydone = emit_yprod(kb, h)
        t_, ve, s, g = kb.bodies("tensor", "vector", "sync", "gpsimd")
        s.wait_ge(kb.sems["cc"], kb.cnt["cc"])
        if h == 1:
            s.wait_ge(kb.sems["ch_vector"], kb.cnt["ve"])
        kb.inc(s.dma_start(out=cosb, in_=kb.creg(tabs[0], c0=h * TH, cols=TH)),
               "ld", 16)
        kb.inc(s.dma_start(out=sinb, in_=kb.creg(tabs[1], c0=h * TH, cols=TH)),
               "ld", 16)
        v_tab = kb.cnt["ld"]
        for oc in range(ntot):
            wname, M, col0 = wspec(oc)
            bi = oc % 2
            B0 = bi * 4096
            goc = h * ntot + oc
            if goc >= 2:
                s.wait_ge(kb.sems["pe"], v_mm_hist[goc - 2])
            kb.inc(s.dma_start(
                out=wsb[:, B0:B0 + 4096],
                in_=kb.wreg(pk, wname, c0=(col0 // 128) * 4096, cols=4096)),
                "ld", 16)
            if bias_name is not None:
                if goc >= 1:
                    g.wait_ge(kb.sems["pe"], v_mm_hist[goc - 1])
                kb.inc(g.dma_start(
                    out=btile[0:1, 0:128],
                    in_=kb.wflat("F", bias_name)[oc * 128:(oc + 1) * 128]
                    .rearrange("(one m) -> one m", one=1)), "gd", 16)
            v_ld = kb.cnt["ld"]
            v_gd = kb.cnt["gd"]
            pg = pA if oc % 2 == 0 else pB
            t_.wait_ge(kb.sems["ld"], v_ld)
            if bias_name is not None:
                t_.wait_ge(kb.sems["gd"], v_gd)
            if oc == 0:
                t_.wait_ge(kb.sems["ve"], ydone)
                t_.wait_ge(kb.sems["ld"], v_tab)
            if goc >= 2:
                t_.wait_ge(kb.sems["ve"], v_done[goc - 2])
            last = None
            for k in range(NCH):
                last = t_.matmul(pg[:, 0:512], wsb[:, B0 + k * 128:B0 + (k + 1) * 128],
                                 kb.sb["yh"][:, k * TH:(k + 1) * TH],
                                 start=(k == 0), stop=(k == 31 and bias_name is None))
            if bias_name is not None:
                last = t_.matmul(pg[:, 0:512], btile[0:1, 0:128],
                                 kb.sb["rowsb"][0:1, 0:512], start=False, stop=True)
            kb.inc(last, "pe", 1)
            v_mm = kb.cnt["pe"]
            v_mm_hist[goc] = v_mm
            if oc < nq + nkv:
                kb.inc(emit_proj_epi_rope(kb, ve, pg, 0, cosb, sinb, tsc2, rout, v_mm),
                       "ve", 1)
                g.wait_ge(kb.sems["ch_vector"], kb.cnt["ve"])
                if oc < nq:
                    base = qoff + oc * 131072
                else:
                    base = ccin_off + (oc - nq) * 131072
                dv = scr[base: base + 131072].rearrange("(d t) -> d t", t=1024)
                kb.inc(g.dma_start(out=dv[:, h * TH:(h + 1) * TH], in_=rout), "gd", 16)
            else:
                kv = oc - nq - nkv
                ve.wait_ge(kb.sems["pe"], v_mm)
                kb.inc(ve.tensor_copy(vstage, pg[:, 0:512]), "ve", 1)
                t_.wait_ge(kb.sems["ve"], kb.cnt["ve"])
                t_.wait_ge(kb.sems["po"], kb.cnt["po"])
                lastt = None
                for tt in range(4):
                    lastt = t_.transpose(pT[:, tt * 128:(tt + 1) * 128],
                                         vstage[:, tt * 128:(tt + 1) * 128],
                                         kb.sb["id_bf"][:, :])
                kb.inc(lastt, "pe", 1)
                v_mm_hist[goc] = kb.cnt["pe"]
                ve.wait_ge(kb.sems["pe"], kb.cnt["pe"])
                kb.inc(ve.tensor_copy(ptse, pT[:, 0:512]), "ve", 1)
                g.wait_ge(kb.sems["ch_vector"], kb.cnt["ve"])
                base = ccin_off + (nkv + kv) * 131072
                dv = scr[base: base + 131072]
                kb.inc(g.dma_start(
                    out=dv.rearrange("(tt p d) -> p tt d", p=128, d=128)[
                        :, h * 4:h * 4 + 4, :],
                    in_=ptse.rearrange("p (tt d) -> p tt d", d=128)), "gd", 16)
            v_done[goc] = kb.cnt["ve"]


def emit_kv_ag(kb, ccin_off, ccout_off, nelem):
    (g,) = kb.bodies("gpsimd")
    g.wait_ge(kb.sems["gd"], kb.cnt["gd"])
    g.wait_ge(kb.sems["ch_vector"], kb.cnt["ve"])
    scr2d = kb.dram["scr"].ap()
    inst = g.collective_compute(
        "AllGather", ALU.bypass, replica_groups=GROUPS,
        ins=[scr2d[ccin_off // 512: (ccin_off + nelem) // 512, :].opt()],
        outs=[scr2d[ccout_off // 512: (ccout_off + 4 * nelem) // 512, :].opt()])
    kb.inc(inst, "cc", 1)
    g.wait_ge(kb.sems["cc"], kb.cnt["cc"])
    return kb.cnt["cc"]


def emit_attn_core(kb, nkv, qpk, qoff, ao_off, ccout_off, kper, scale, v_ag,
                   dump_at=None):
    pA, pC2, pT = kb.ps["pA"], kb.ps["pC2"], kb.ps["pT"]
    ws = kb.sb["ws"]
    wsb = ws.bitcast(BF16)
    wsf = ws.bitcast(F32)
    yb = kb.sb["yh"]
    xbb = kb.sb["xb"]
    mbuf = yb[:, 0:4096]
    scb = yb.bitcast(F32)[:, 2048:2048 + 4096]
    Kw = wsb[:, 0:4096]
    Vw = wsb[:, 4096:8192]
    Qw = wsb[:, 8192:9216]
    Pbs = (yb[:, 12288:12288 + 4096], wsb[:, 9216:13312])
    pts = wsb[:, 13312:13312 + 512]
    aost = wsb[:, 14336:14336 + 1024]
    m1 = wsf[:, 7720:7721]
    nm1 = wsf[:, 7721:7722]
    lrows = (wsf[:, 7722:7723], wsf[:, 7724:7725])
    rls = (wsf[:, 7723:7724], wsf[:, 7725:7726])
    scr = kb.dram["scr"].ap().rearrange("r c -> (r c)")
    t_, ve, ac, s, g = kb.bodies("tensor", "vector", "scalar", "sync", "gpsimd")
    s.wait_ge(kb.sems["cc"], v_ag)
    mreg = kb.creg("m1").bitcast(BF16).rearrange("p (qq c) -> p qq c", c=4096)
    st = {"v_sc_ev": None, "v_pv_ev": None, "v_ao": None, "ti": 0,
          "v_pn": {}, "v_mk": 0}

    def stageA(gq, qt):
        i = st["ti"]
        p = i % 2
        if st["v_sc_ev"] is not None:
            s.wait_ge(kb.sems["ch_vector"], st["v_sc_ev"])
        kb.inc(s.dma_start(out=mbuf, in_=mreg[:, qt, :]), "ld", 16)
        st["v_mk"] = kb.cnt["ld"]
        ev_list = []
        for kb_ in range(8):
            pc0 = (kb_ % 2) * 512
            t_.wait_ge(kb.sems["ld"], st["v_kv"])
            if kb_ >= 2:
                t_.wait_ge(kb.sems["ch_vector"], ev_list[kb_ - 2])
            elif st["v_sc_ev"] is not None:
                t_.wait_ge(kb.sems["ch_vector"], st["v_sc_ev"])
            kb.inc(t_.matmul(
                pC2[:, pc0:pc0 + 512],
                Qw[:, qt * 128:(qt + 1) * 128],
                Kw[:, kb_ * 512:(kb_ + 1) * 512],
                start=True, stop=True), "pe", 1)
            v_s = kb.cnt["pe"]
            ve.wait_ge(kb.sems["pe"], v_s)
            if kb_ == 0:
                ve.wait_ge(kb.sems["ld"], st["v_mk"])
                ve.wait_ge(kb.sems["ac"], kb.cnt["ac"])
            kb.inc(ve.tensor_add(scb[:, kb_ * 512:(kb_ + 1) * 512],
                                 pC2[:, pc0:pc0 + 512],
                                 mbuf[:, kb_ * 512:(kb_ + 1) * 512]), "ve", 1)
            ev_list.append(kb.cnt["ve"])
        st["v_sc_ev"] = kb.cnt["ve"]
        ve.reduce_max(m1, scb, axis=AX)
        kb.inc(ve.tensor_scalar_mul(nm1, m1, -scale), "ve", 1)
        ac.wait_ge(kb.sems["ve"], kb.cnt["ve"])
        ac.wait_ge(kb.sems["pe"], kb.cnt["pe"])
        kb.inc(ac.activation(out=Pbs[p], in_=scb, func=AF.Exp, bias=nm1,
                             scale=scale, accum_out=lrows[p]), "ac", 1)
        ve.wait_ge(kb.sems["ac"], kb.cnt["ac"])
        ve.reciprocal(rls[p], lrows[p])
        kb.inc(ve.tensor_scalar_mul(Pbs[p], Pbs[p], rls[p]), "ve", 1)
        st["v_pn"][i] = kb.cnt["ve"]
        st["ti"] += 1
        return i

    def stageB(i, hq, qt):
        p = i % 2
        Pb = Pbs[p]
        for kc in range(32):
            tj = kc % 4
            if tj == 0:
                if kc > 0:
                    t_.wait_ge(kb.sems["ch_vector"], st["v_pv_ev"])
                else:
                    t_.wait_ge(kb.sems["ve"], st["v_pn"][i])
                    t_.wait_ge(kb.sems["po"], kb.cnt["po"])
                    if st["v_ao"] is not None:
                        t_.wait_ge(kb.sems["ch_vector"], st["v_ao"])
            ti = t_.transpose(pT[:, tj * 128:(tj + 1) * 128],
                              Pb[:, kc * 128:(kc + 1) * 128],
                              kb.sb["id_bf"][:, :])
            if tj == 3:
                kb.inc(ti, "pe", 1)
                v_tp = kb.cnt["pe"]
                ve.wait_ge(kb.sems["pe"], v_tp)
                kb.inc(ve.tensor_copy(pts, pT[:, 0:512]), "ve", 1)
                st["v_pv_ev"] = kb.cnt["ve"]
                t_.wait_ge(kb.sems["ch_vector"], st["v_pv_ev"])
                last = None
                for j in range(4):
                    kcj = kc - 3 + j
                    last = t_.matmul(pA[:, 0:128],
                                     Vw[:, kcj * 128:(kcj + 1) * 128],
                                     pts[:, j * 128:(j + 1) * 128],
                                     start=(kcj == 0), stop=(kcj == 31))
                kb.inc(last, "pe", 1)
        v_pv = kb.cnt["pe"]
        ve.wait_ge(kb.sems["pe"], v_pv)
        if qt == 0:
            ve.wait_ge(kb.sems["gd"], kb.cnt["gd"])
        kb.inc(ve.tensor_copy(aost[:, qt * 128:(qt + 1) * 128], pA[:, 0:128]),
               "ve", 1)
        st["v_ao"] = kb.cnt["ve"]
        if qt == 7:
            g.wait_ge(kb.sems["ch_vector"], kb.cnt["ve"])
            kb.inc(g.dma_start(
                out=scr[ao_off + hq * 131072: ao_off + (hq + 1) * 131072]
                .rearrange("(d t) -> d t", t=1024),
                in_=aost), "gd", 16)

    prev = None
    for kv in range(nkv):
        if prev is not None:
            stageB(*prev)
            prev = None
        s.wait_ge(kb.sems["pe"], kb.cnt["pe"])
        for r in range(4):
            src = scr[ccout_off + r * kper + kv * 131072:
                      ccout_off + r * kper + (kv + 1) * 131072] \
                .rearrange("(d t) -> d t", t=1024)
            kb.inc(s.dma_start(out=Kw[:, r * 1024:(r + 1) * 1024], in_=src), "ld", 16)
            vsrc = scr[ccout_off + r * kper + (nkv + kv) * 131072:
                       ccout_off + r * kper + (nkv + kv + 1) * 131072] \
                .rearrange("(tt p d) -> p tt d", p=128, d=128)
            kb.inc(s.dma_start(
                out=Vw.rearrange("p (rr tt d) -> p rr tt d", rr=4, tt=8)[:, r, :, :],
                in_=vsrc), "ld", 16)
        v_kvld = kb.cnt["ld"]
        for gq in range(qpk):
            hq = kv * qpk + gq
            s.wait_ge(kb.sems["pe"], kb.cnt["pe"])
            kb.inc(s.dma_start(
                out=Qw,
                in_=scr[qoff + hq * 131072: qoff + (hq + 1) * 131072]
                .rearrange("(d t) -> d t", t=1024)), "ld", 16)
            st["v_kv"] = kb.cnt["ld"]
            for qt in range(8):
                i = stageA(gq, qt)
                if prev is not None:
                    stageB(*prev)
                prev = (i, hq, qt)
    stageB(*prev)
    return


def emit_s1(kb):
    def wspec(oc):
        if oc < 32:
            return ("w1q", 4096, oc * 128)
        if oc < 40:
            return ("w1k", 1024, (oc - 32) * 128)
        return ("w1v", 1024, (oc - 40) * 128)
    emit_attn_proj(kb, wspec, 32, 8, S1_Q, S1_CCI)
    v_ag = emit_kv_ag(kb, S1_CCI, S1_CCO, S1_KPER)
    emit_attn_core(kb, 8, 4, S1_Q, S1_AO, S1_CCO, S1_KPER,
                   1.0 / float(np.sqrt(128.0)), v_ag)
    emit_outproj(kb, "w1o", S1_AO, stats_first=True)


def emit_s7(kb):
    def wspec(oc):
        return ("w7qkv", 12288, oc * 128)
    emit_attn_proj(kb, wspec, 32, 32, S7_Q, S7_CCI, bias_name="b7qkv",
                   pk=PK_S7)
    v_ag = emit_kv_ag(kb, S7_CCI, S7_CCO, S7_KPER)
    emit_attn_core(kb, 32, 1, S7_Q, S7_AO, S7_CCO, S7_KPER,
                   1.0 / float(np.sqrt(128.0)), v_ag)
    emit_outproj(kb, "w7o", S7_AO, stats_first=True, bias_name="b7o", pk=PK_S7)



S3_QKV = MK_XBASE
S3_O2 = S3_QKV + 24 * 131072
S3_END = S3_O2 + 8 * 131072
S3_HCI = S3_END
S3_HCO = S3_END + 16384
S3_END2 = S3_HCO + 65536


def pack_s3(inputs, PB, PF):
    cw = np.asarray(inputs["L0_attn_b_conv_w"], np.float32)
    for k in range(3):
        PB.add("w3c%d" % k, img_lhsT_strips(np.ascontiguousarray(cw[k])))
    PB.add("w3q", img_lhsT_strips(
        np.asarray(inputs["L0_attn_b_wq"], np.float32).reshape(1024, 1024), 512))
    PB.add("w3k", img_lhsT_strips(
        np.asarray(inputs["L0_attn_b_wk"], np.float32).reshape(1024, 1024), 512))
    PB.add("w3v", img_lhsT_strips(
        np.asarray(inputs["L0_attn_b_wv"], np.float32).reshape(1024, 1024), 512))
    PB.add("w3o", img_lhsT(
        np.asarray(inputs["L0_attn_b_wo"], np.float32).reshape(1024, 1024)))
    PB.add("w3p", img_lhsT(np.asarray(inputs["L0_attn_b_proj_w"], np.float32)))
    PF.add_flat("b3p", np.asarray(inputs["L0_attn_b_proj_b"], np.float32).reshape(-1))


def s3_consts(inputs, CP, core=0):
    r = core % 4
    sel = np.zeros((128, 16), np.float32)
    if r > 0:
        sel[:, (r - 1) * 2 + 1] = 1.0
    if r < 3:
        sel[:, 8 + (r + 1) * 2 + 0] = 1.0
    CP.add("h3sel", sel)
    CP.add("c3b", np.ascontiguousarray(
        np.asarray(inputs["L0_attn_b_conv_b"], np.float32).reshape(24, 128).T))
    bqk = np.zeros((128, 32), np.float32)
    bqk[0:64, 0:16] = np.asarray(inputs["L0_attn_b_bq"], np.float32).T
    bqk[0:64, 16:32] = np.asarray(inputs["L0_attn_b_bk"], np.float32).T
    CP.add("b3qk", bqk)
    CP.add_flat("b3v", np.asarray(inputs["L0_attn_b_bv"], np.float32).reshape(-1))
    CP.add("cb3o", np.ascontiguousarray(
        np.asarray(inputs["L0_attn_b_bo"], np.float32).reshape(8, 128).T))
    msk = np.zeros((128, 128), np.float32)
    qi = np.arange(128)[:, None]
    ki = np.arange(128)[None, :]
    msk[ki > qi] = -1e9
    CP.add("c3m", msk)


def emit_s3(kb, fused=False):
    pA, pB, pC2, pE, pT = (kb.ps["pA"], kb.ps["pB"], kb.ps["pC2"],
                           kb.ps["pE"], kb.ps["pT"])
    ws = kb.sb["ws"]
    wsb = ws.bitcast(BF16)
    wsf = ws.bitcast(F32)
    yh = kb.sb["yh"]
    rowsb = kb.sb["rowsb"]
    xb = kb.sb["xb"]
    scr = kb.dram["scr"].ap().rearrange("r c -> (r c)")
    cflat = kb.cst.rearrange("r c -> (r c)")
    yhv = yh.rearrange("p (kc t) -> p kc t", t=TH)
    tcv = wsf[:, 6144:6656]
    cb = wsf[:, 6656:6680]
    hxf = wsf[:, 6688:6752]
    SR = 6752
    bc = wsf[:, 6768:6772]
    htmp = wsf[:, 6912:6944]
    hyb = wsb[:, 13568:13632]
    hxb = wsb[:, 13632:13696]
    hsq = wsb[:, 13696:13760]
    ecolL = wsb[:, 13760:13792]
    ecolR = wsb[:, 13792:13824]

    t_, ve, ac, s, g = kb.bodies("tensor", "vector", "scalar", "sync", "gpsimd")
    s.wait_ge(kb.sems["cc"], kb.cnt["cc"])
    if not fused:
        kb.inc(s.dma_start(out=hxf, in_=kb.dram["xh"].ap()), "ld", 16)
    else:
        scrF = kb.dram["scr"].ap().bitcast(F32).rearrange("r c -> (r c)")
        xv = kb.sb["xres"].rearrange("p (kc t) -> p kc t", t=T)
        for ci, col in ((0, 0), (1, T - 1)):
            kb.inc(g.dma_start(
                out=scrF[S3_HCI // 2 + ci * 4096: S3_HCI // 2 + (ci + 1) * 4096]
                .rearrange("(p kc) -> p kc", kc=32),
                in_=xv[:, :, col:col + 1],
                allow_slow_non_contiguous=True), "gd", 16)
        v_hag = emit_kv_ag(kb, S3_HCI, S3_HCO, 16384)
        s.wait_ge(kb.sems["cc"], v_hag)
        hall = wsf[:, 6944:7200]
        for j in range(8):
            kb.inc(s.dma_start(
                out=hall[:, j * 32:(j + 1) * 32],
                in_=scrF[S3_HCO // 2 + j * 4096: S3_HCO // 2 + (j + 1) * 4096]
                .rearrange("(p kc) -> p kc", kc=32)), "ld", 16)
        kb.inc(s.dma_start(out=wsf[:, 7200:7216], in_=kb.creg("h3sel")), "ld", 16)
        v_h = kb.cnt["ld"]
        ve.wait_ge(kb.sems["ld"], v_h)
        inst = None
        for t in range(2):
            ve.tensor_scalar_mul(hxf[:, t * 32:(t + 1) * 32], hall[:, 0:32],
                                 wsf[:, 7200 + t * 8:7201 + t * 8])
            for j in range(1, 8):
                ve.tensor_scalar_mul(htmp, hall[:, j * 32:(j + 1) * 32],
                                     wsf[:, 7200 + t * 8 + j:7201 + t * 8 + j])
                inst = ve.tensor_add(hxf[:, t * 32:(t + 1) * 32],
                                     hxf[:, t * 32:(t + 1) * 32], htmp)
        kb.inc(inst, "ve", 1)
    kb.inc(s.dma_start(out=cb, in_=kb.creg("c3b")), "ld", 16)
    v_hx = kb.cnt["ld"]
    ve.wait_ge(kb.sems["ld"], v_hx)
    ac.wait_ge(kb.sems["ld"], v_hx)
    ac.wait_ge(kb.sems["ch_vector"], kb.cnt["ve"])
    kb.inc(ve.tensor_copy(hxb, hxf), "ve", 1)
    kb.inc(ac.activation(out=hsq, in_=hxf, func=AF.Square), "ac", 1)
    t_.wait_ge(kb.sems["ve"], kb.cnt["ve"])
    t_.wait_ge(kb.sems["ac"], kb.cnt["ac"])
    t_.matmul(pE[0:1, 0:64], kb.sb["ones_bf"][:, 0:1], hxb, start=True, stop=True)
    kb.inc(t_.matmul(pE[32:33, 0:64], kb.sb["ones_bf"][:, 0:1], hsq,
                     start=True, stop=True), "pe", 1)
    ve.wait_ge(kb.sems["pe"], kb.cnt["pe"])
    sc = lambda c, n=1: wsf[0:1, SR + c:SR + c + n]
    for t in range(2):
        ve.reduce_sum(sc(t), pE[0:1, t * 32:(t + 1) * 32], axis=AX)
        ve.reduce_sum(sc(2 + t), pE[32:33, t * 32:(t + 1) * 32], axis=AX)
    ve.tensor_scalar_mul(sc(4, 2), sc(0, 2), 1.0 / D)
    ve.tensor_scalar_mul(sc(6, 2), sc(2, 2), 1.0 / D)
    ve.tensor_mul(sc(8, 2), sc(4, 2), sc(4, 2))
    ve.tensor_sub(sc(10, 2), sc(6, 2), sc(8, 2))
    ve.tensor_scalar_add(sc(10, 2), sc(10, 2), 1e-5)
    kb.inc(ve.tensor_scalar_mul(sc(12, 2), sc(4, 2), -1.0), "ve", 1)
    ac.wait_ge(kb.sems["ve"], kb.cnt["ve"])
    kb.inc(ac.activation(out=sc(10, 2), in_=sc(10, 2), func=AF.Sqrt), "ac", 1)
    ve.wait_ge(kb.sems["ac"], kb.cnt["ac"])
    ve.reciprocal(sc(10, 2), sc(10, 2))
    ve.tensor_copy(rowsb[0:1, 512:514], sc(12, 2))
    kb.inc(ve.tensor_copy(rowsb[0:1, 514:516], sc(10, 2)), "ve", 1)
    t_.wait_ge(kb.sems["ve"], kb.cnt["ve"])
    last = None
    for j in range(2):
        t_.matmul(pE[:, 64 + j:65 + j], kb.sb["ones_bf"][0:1, 0:128],
                  rowsb[0:1, 512 + j:513 + j], start=True, stop=True)
        last = t_.matmul(pE[:, 66 + j:67 + j], kb.sb["ones_bf"][0:1, 0:128],
                         rowsb[0:1, 514 + j:515 + j], start=True, stop=True)
    kb.inc(last, "pe", 1)
    ve.wait_ge(kb.sems["pe"], kb.cnt["pe"])
    kb.inc(ve.tensor_copy(bc, pE[:, 64:68]), "ve", 1)
    for t in range(2):
        ve.tensor_scalar_add(htmp, hxf[:, t * 32:(t + 1) * 32], bc[:, t:t + 1])
        kb.inc(ve.tensor_scalar_mul(hyb[:, t * 32:(t + 1) * 32], htmp,
                                    bc[:, 2 + t:3 + t]), "ve", 1)

    emit_yprod(kb, 1)
    (ve,) = kb.bodies("vector")
    kb.inc(ve.tensor_copy(ecolR, yhv[:, :, 0:1]), "ve", 1)

    v_sp = None
    v_cmb = None
    for h in range(2):
        ydone = emit_yprod(kb, h)
        t_, ve, s, g = kb.bodies("tensor", "vector", "sync", "gpsimd")
        if h == 0:
            lcol = lambda kc: hyb[:, kc:kc + 1]
            rcol = lambda kc: ecolR[:, kc:kc + 1]
        else:
            lcol = lambda kc: ecolL[:, kc:kc + 1]
            rcol = lambda kc: hyb[:, 32 + kc:33 + kc]
        v_mm_prev = kb.cnt["pe"]
        for oc in range(24):
            s.wait_ge(kb.sems["pe"], v_mm_prev)
            for k3 in range(3):
                kb.inc(s.dma_start(
                    out=wsb[:, k3 * 4096:(k3 + 1) * 4096],
                    in_=kb.wreg("B", "w3c%d" % k3, c0=oc * 4096, cols=4096)),
                    "ld", 16)
            v_ld = kb.cnt["ld"]
            t_.wait_ge(kb.sems["ld"], v_ld)
            if oc == 0:
                t_.wait_ge(kb.sems["ve"], ydone)
            else:
                t_.wait_ge(kb.sems["ch_vector"], v_cmb)
            for k3, pg in ((0, pA), (1, pB), (2, pC2)):
                for kc in range(NCH):
                    t_.matmul(pg[:, 0:512],
                              wsb[:, k3 * 4096 + kc * 128:k3 * 4096 + (kc + 1) * 128],
                              yh[:, kc * TH:(kc + 1) * TH],
                              start=(kc == 0), stop=(kc == 31))
            for kc in range(NCH):
                t_.matmul(pE[:, 0:1], wsb[:, kc * 128:(kc + 1) * 128],
                          lcol(kc), start=(kc == 0), stop=(kc == 31))
            last = None
            for kc in range(NCH):
                last = t_.matmul(pE[:, 1:2],
                                 wsb[:, 2 * 4096 + kc * 128:2 * 4096 + (kc + 1) * 128],
                                 rcol(kc), start=(kc == 0), stop=(kc == 31))
            kb.inc(last, "pe", 1)
            v_mm_prev = kb.cnt["pe"]
            ve.wait_ge(kb.sems["pe"], v_mm_prev)
            if v_sp is not None:
                ve.wait_ge(kb.sems["gd"], v_sp)
            ve.tensor_copy(tcv[:, 1:512], pA[:, 0:511])
            ve.tensor_copy(tcv[:, 0:1], pE[:, 0:1])
            ve.tensor_add(tcv, tcv, pB[:, 0:512])
            ve.tensor_add(tcv[:, 0:511], tcv[:, 0:511], pC2[:, 1:512])
            ve.tensor_add(tcv[:, 511:512], tcv[:, 511:512], pE[:, 1:2])
            kb.inc(ve.tensor_scalar_add(xb[:, 0:512], tcv, cb[:, oc:oc + 1]),
                   "ve", 1)
            v_cmb = kb.cnt["ve"]
            g.wait_ge(kb.sems["ch_vector"], v_cmb)
            kb.inc(g.dma_start(
                out=scr[S3_QKV + oc * 131072:S3_QKV + (oc + 1) * 131072]
                .rearrange("(p t) -> p t", t=1024)[:, h * TH:h * TH + TH],
                in_=xb[:, 0:512]), "gd", 16)
            v_sp = kb.cnt["gd"]
        if h == 0:
            kb.inc(ve.tensor_copy(ecolL, yhv[:, :, 511:512]), "ve", 1)

    t_, ve, ac, s, g = kb.bodies("tensor", "vector", "scalar", "sync", "gpsimd")
    qkv3 = wsb[:, 0:3072]
    qh_s = wsb[0:64, 3072:3200]
    kh_s = wsb[0:64, 3200:3328]
    vh_sb = wsb[:, 3328:3392]
    P3 = wsb[:, 3392:3520]
    PT3 = wsb[:, 3520:3648]
    opair = wsb[:, 3648:3776]
    o2b = wsb[:, 3776:4800]
    oall = wsb[:, 4800:5824]
    scf = wsf[:, 2912:3040]
    m3 = wsf[:, 3040:3041]
    nm3 = wsf[:, 3041:3042]
    l3 = wsf[:, 3042:3043]
    r3 = wsf[:, 3043:3044]
    msk3 = wsf[:, 3044:3172]
    cbo = wsf[:, 3204:3212]
    bvf = wsf[0:1, 3212:4236]
    bvh_f = wsf[0:1, 4236:5260]
    s.wait_ge(kb.sems["gd"], kb.cnt["gd"])
    s.wait_ge(kb.sems["ch_vector"], kb.cnt["ve"])
    kb.inc(s.dma_start(out=msk3, in_=kb.creg("c3m")), "ld", 16)
    kb.inc(s.dma_start(out=wsf[:, 3172:3204], in_=kb.creg("b3qk")), "ld", 16)
    kb.inc(s.dma_start(out=cbo, in_=kb.creg("cb3o")), "ld", 16)
    off_bv, _ = kb.catlas["b3v"]
    kb.inc(s.dma_start(out=bvf, in_=cflat[off_bv:off_bv + 1024]
                       .rearrange("(o m) -> o m", o=1)), "ld", 16)
    v_c2 = kb.cnt["ld"]
    ve.wait_ge(kb.sems["ld"], v_c2)
    ve.tensor_scalar_mul(wsf[0:64, 3172:3188], wsf[0:64, 3172:3188], 0.125)
    ve.tensor_copy(xb[0:1, 0:1024], bvf)
    ve.tensor_copy(bvh_f, xb[0:1, 0:1024])
    ve.tensor_sub(bvf, bvf, bvh_f)
    kb.inc(ve.tensor_copy(xb[0:1, 1024:2048], bvf), "ve", 1)
    v_bv = kb.cnt["ve"]

    w3qv = yh[:, 0:4096].rearrange("p (kc e) -> p kc e", e=512)
    w3kv = yh[:, 4096:8192].rearrange("p (kc e) -> p kc e", e=512)
    w3vv = yh[:, 8192:12288].rearrange("p (kc e) -> p kc e", e=512)
    v_blk_mm = kb.cnt["pe"]
    v_o2ev = None
    v_evac = None
    for b in range(8):
        s.wait_ge(kb.sems["pe"], v_blk_mm)
        for ch in range(24):
            kb.inc(s.dma_start(
                out=wsb[:, ch * 128:(ch + 1) * 128],
                in_=scr[S3_QKV + ch * 131072:S3_QKV + (ch + 1) * 131072]
                .rearrange("(p t) -> p t", t=1024)[:, b * 128:(b + 1) * 128]),
                "ld", 16)
        v_qkv = kb.cnt["ld"]
        for gph in range(2):
            s.wait_ge(kb.sems["pe"], kb.cnt["pe"])
            kb.inc(s.dma_start(
                out=yh[:, 0:4096],
                in_=kb.wreg("B", "w3q", c0=gph * 4096, cols=4096)), "ld", 16)
            kb.inc(s.dma_start(
                out=yh[:, 4096:8192],
                in_=kb.wreg("B", "w3k", c0=gph * 4096, cols=4096)), "ld", 16)
            kb.inc(s.dma_start(
                out=yh[:, 8192:12288],
                in_=kb.wreg("B", "w3v", c0=gph * 4096, cols=4096)), "ld", 16)
            v_wg = kb.cnt["ld"]
            for h8 in range(8):
                hh = gph * 8 + h8
                t_.wait_ge(kb.sems["ld"], v_wg)
                if gph == 0 and h8 == 0:
                    t_.wait_ge(kb.sems["ld"], v_qkv)
                    t_.wait_ge(kb.sems["ve"], v_bv)
                if v_evac is not None:
                    t_.wait_ge(kb.sems["ch_vector"], v_evac)
                for kc8 in range(8):
                    t_.matmul(pE[0:64, 0:128], w3qv[:, kc8, h8 * 64:(h8 + 1) * 64],
                              qkv3[:, kc8 * 128:(kc8 + 1) * 128],
                              start=(kc8 == 0), stop=(kc8 == 7))
                for kc8 in range(8):
                    t_.matmul(pE[64:128, 0:128], w3kv[:, kc8, h8 * 64:(h8 + 1) * 64],
                              qkv3[:, (8 + kc8) * 128:(9 + kc8) * 128],
                              start=(kc8 == 0), stop=(kc8 == 7))
                for kc8 in range(8):
                    t_.matmul(pE[:, 384:448], qkv3[:, (16 + kc8) * 128:(17 + kc8) * 128],
                              w3vv[:, kc8, h8 * 64:(h8 + 1) * 64],
                              start=(kc8 == 0), stop=False)
                t_.matmul(pE[:, 384:448], kb.sb["rowsb"][0:1, 0:128],
                          xb[0:1, hh * 64:(hh + 1) * 64], start=False, stop=False)
                kb.inc(t_.matmul(pE[:, 384:448], kb.sb["rowsb"][0:1, 0:128],
                                 xb[0:1, 1024 + hh * 64:1024 + (hh + 1) * 64],
                                 start=False, stop=True), "pe", 1)
                v_proj = kb.cnt["pe"]
                ve.wait_ge(kb.sems["pe"], v_proj)
                ve.tensor_scalar(out=qh_s, in0=pE[0:64, 0:128], scalar1=0.125,
                                 scalar2=wsf[0:64, 3172 + hh:3173 + hh],
                                 op0=ALU.mult, op1=ALU.add)
                ve.tensor_scalar_add(kh_s, pE[64:128, 0:128],
                                     wsf[0:64, 3188 + hh:3189 + hh])
                kb.inc(ve.tensor_copy(vh_sb, pE[:, 384:448]), "ve", 1)
                v_evac = kb.cnt["ve"]
                t_.wait_ge(kb.sems["ch_vector"], v_evac)
                kb.inc(t_.matmul(pB[:, 0:128], qh_s, kh_s, start=True, stop=True),
                       "pe", 1)
                ve.wait_ge(kb.sems["pe"], kb.cnt["pe"])
                ve.wait_ge(kb.sems["ac"], kb.cnt["ac"])
                ve.tensor_add(scf, pB[:, 0:128], msk3)
                ve.reduce_max(m3, scf, axis=AX)
                kb.inc(ve.tensor_scalar_mul(nm3, m3, -1.0), "ve", 1)
                ac.wait_ge(kb.sems["ve"], kb.cnt["ve"])
                ac.wait_ge(kb.sems["pe"], kb.cnt["pe"])
                kb.inc(ac.activation(out=P3, in_=scf, func=AF.Exp, bias=nm3,
                                     scale=1.0, accum_out=l3), "ac", 1)
                ve.wait_ge(kb.sems["ac"], kb.cnt["ac"])
                ve.reciprocal(r3, l3)
                kb.inc(ve.tensor_scalar_mul(P3, P3, r3), "ve", 1)
                t_.wait_ge(kb.sems["ch_vector"], kb.cnt["ve"])
                t_.wait_ge(kb.sems["po"], kb.cnt["po"])
                kb.inc(t_.transpose(pT[:, 0:128], P3, kb.sb["id_bf"][:, :]),
                       "pe", 1)
                ve.wait_ge(kb.sems["pe"], kb.cnt["pe"])
                kb.inc(ve.tensor_copy(PT3, pT[:, 0:128]), "ve", 1)
                t_.wait_ge(kb.sems["ch_vector"], kb.cnt["ve"])
                kb.inc(t_.matmul(pC2[(h8 % 2) * 64:(h8 % 2) * 64 + 64, 0:128],
                                 vh_sb, PT3, start=True, stop=True), "pe", 1)
                v_blk_mm = kb.cnt["pe"]
                if h8 % 2 == 1:
                    jj = hh // 2
                    ve.wait_ge(kb.sems["pe"], v_blk_mm)
                    kb.inc(ve.tensor_copy(oall[:, jj * 128:(jj + 1) * 128],
                                          pC2[:, 0:128]), "ve", 1)
        s.wait_ge(kb.sems["pe"], kb.cnt["pe"])
        kb.inc(s.dma_start(out=yh[:, 0:8192], in_=kb.wreg("B", "w3o")), "ld", 16)
        v_wo = kb.cnt["ld"]
        w3of = yh[:, 0:8192].rearrange("p (jj d) -> p jj d", d=1024)
        t_.wait_ge(kb.sems["ld"], v_wo)
        t_.wait_ge(kb.sems["ch_vector"], kb.cnt["ve"])
        if v_o2ev is not None:
            t_.wait_ge(kb.sems["ch_vector"], v_o2ev)
        lastm = None
        for dc in range(8):
            for jj in range(8):
                lastm = t_.matmul(pA[:, dc * 128:(dc + 1) * 128],
                                  w3of[:, jj, dc * 128:(dc + 1) * 128],
                                  oall[:, jj * 128:(jj + 1) * 128],
                                  start=(jj == 0), stop=(jj == 7))
        kb.inc(lastm, "pe", 1)
        v_blk_mm = kb.cnt["pe"]
        ve.wait_ge(kb.sems["pe"], v_blk_mm)
        ve.wait_ge(kb.sems["gd"], kb.cnt["gd"])
        inst = None
        for dc in range(8):
            inst = ve.tensor_scalar_add(o2b[:, dc * 128:(dc + 1) * 128],
                                        pA[:, dc * 128:(dc + 1) * 128],
                                        cbo[:, dc:dc + 1])
        kb.inc(inst, "ve", 1)
        v_o2ev = kb.cnt["ve"]
        g.wait_ge(kb.sems["ch_vector"], v_o2ev)
        for wc in range(8):
            kb.inc(g.dma_start(
                out=scr[S3_O2 + wc * 131072:S3_O2 + (wc + 1) * 131072]
                .rearrange("(p t) -> p t", t=1024)[:, b * 128:(b + 1) * 128],
                in_=o2b[:, wc * 128:(wc + 1) * 128]), "gd", 16)
    emit_outproj(kb, "w3p", S3_O2, stats_first=True, bias_name="b3p", nk=8)


def emit_barrier(kb):
    t_, ve, ac, s, g = kb.bodies("tensor", "vector", "scalar", "sync", "gpsimd")
    for eng in (t_, ve, ac, s, g):
        eng.wait_ge(kb.sems["pe"], kb.cnt["pe"])
        eng.wait_ge(kb.sems["ch_vector"], kb.cnt["ve"])
        eng.wait_ge(kb.sems["ch_scalar"], kb.cnt["ac"])
        eng.wait_ge(kb.sems["ch_gpsimd"], kb.cnt["po"])
        eng.wait_ge(kb.sems["ld"], kb.cnt["ld"])
        eng.wait_ge(kb.sems["gd"], kb.cnt["gd"])
        eng.wait_ge(kb.sems["cc"], kb.cnt["cc"])


MEGA_EMS = [
    lambda kb: emit_s1(kb),
    lambda kb: emit_s2(kb),
    lambda kb: emit_s3(kb, fused=True),
    lambda kb: emit_s2(kb, dequant=False),
    lambda kb: emit_s5(kb),
    lambda kb: emit_s6(kb),
    lambda kb: emit_s7(kb),
    lambda kb: emit_s6(kb),
]


def emit_mega(kb, n=8):
    for i, em in enumerate(MEGA_EMS[:n]):
        if i:
            emit_barrier(kb)
        em(kb)


def halo_cols(x, c):
    b, t0 = c // 4, (c % 4) * T
    hcols = np.zeros((128, 64), np.float32)
    for t, tok in ((0, t0 - 1), (1, t0 + T)):
        if 0 <= tok < S:
            hcols[:, t * 32:(t + 1) * 32] = x[b, tok, :].reshape(32, 128).T
    return hcols


S5_Q = MK_XBASE
S5_AO = S5_Q + 32 * 131072
S5_CCI = S5_AO + 32 * 131072
S5_KPER = 262144
S5_CCO = S5_CCI + S5_KPER
S5_END = S5_CCO + 4 * S5_KPER


def pack_s5(inputs, PB):
    PB.add("w5q", img_lhsT_strips(np.ascontiguousarray(
        np.asarray(inputs["L1_attn_a_Wq"], np.float32).T)))
    PB.add("w5k", img_lhsT_strips(np.ascontiguousarray(
        np.asarray(inputs["L1_attn_a_Wk"], np.float32).T)))
    PB.add("w5v", img_lhsT_strips(np.ascontiguousarray(
        np.asarray(inputs["L1_attn_a_Wv"], np.float32).T)))
    PB.add("w5o", img_lhsT(np.ascontiguousarray(
        np.asarray(inputs["L1_attn_a_Wo"], np.float32).T)))


def att5_consts(core, CP):
    r = core % 4
    g0 = r * T
    pos = np.arange(g0, g0 + T, dtype=np.float32)
    inv = 1.0 / (10000.0 ** (np.arange(64, dtype=np.float32) / 64.0))
    invd = inv[np.arange(128) // 2]
    ang = invd[:, None] * pos[None, :]
    cosb = np.cos(ang).astype(np.float32)
    sinb = np.sin(ang).astype(np.float32)
    sinb[0:64] = -sinb[0:64]
    CP.add("rope5_cos", cosb)
    CP.add("rope5_sin", sinb)
    al = np.zeros((128, 1024), np.float32)
    for qb4 in range(4):
        a = np.linspace(0.0, 1.0, 256, dtype=np.float32)
        if r == 0 and qb4 == 0:
            a = np.ones(256, np.float32)
        al[:, qb4 * 256:(qb4 + 1) * 256] = a[None, :]
    CP.add("c5al", al)
    m = np.zeros((128, 16 * 4096), np.float32)
    for qt in range(8):
        i2 = g0 + (qt // 2) * 256
        wins = [(max(i2 - 256, 0), i2 + 256), (i2, min(i2 + 512, S))]
        for w, (a, b) in enumerate(wins):
            blk = np.full((128, 4096), -1e9, np.float32)
            blk[:, a:b] = 0.0
            m[:, (qt * 2 + w) * 4096:(qt * 2 + w + 1) * 4096] = blk
    u = np.ascontiguousarray(m.astype(BF)).view(np.uint16).reshape(-1, 2)
    packed = np.ascontiguousarray(u).view(np.uint32).reshape(-1).view(np.float32)
    CP.add_flat("m5", packed)


def emit_s5_core(kb, v_ag):
    pA, pC2, pT = kb.ps["pA"], kb.ps["pC2"], kb.ps["pT"]
    ws = kb.sb["ws"]
    wsb = ws.bitcast(BF16)
    wsf = ws.bitcast(F32)
    yb = kb.sb["yh"]
    mbuf = yb[:, 0:4096]
    scb = yb.bitcast(F32)[:, 2048:2048 + 4096]
    Pb = yb[:, 12288:12288 + 4096]
    Kw = wsb[:, 0:4096]
    Vw = wsb[:, 4096:8192]
    Qw = wsb[:, 8192:9216]
    pts = wsb[:, 13312:13824]
    aost = wsb[:, 14336:15360]
    alf = wsf[:, 5120:6144]
    tb = wsf[:, 6144:6272]
    tba = wsf[:, 6272:6400]
    lrow = wsf[:, 7722:7723]
    rl = wsf[:, 7723:7724]
    scr = kb.dram["scr"].ap().rearrange("r c -> (r c)")
    t_, ve, ac, s, g = kb.bodies("tensor", "vector", "scalar", "sync", "gpsimd")
    s.wait_ge(kb.sems["cc"], v_ag)
    for r in range(4):
        src = scr[S5_CCO + r * S5_KPER: S5_CCO + r * S5_KPER + 131072] \
            .rearrange("(d t) -> d t", t=1024)
        kb.inc(s.dma_start(out=Kw[:, r * 1024:(r + 1) * 1024], in_=src), "ld", 16)
        vsrc = scr[S5_CCO + r * S5_KPER + 131072: S5_CCO + (r + 1) * S5_KPER] \
            .rearrange("(tt p d) -> p tt d", p=128, d=128)
        kb.inc(s.dma_start(
            out=Vw.rearrange("p (rr tt d) -> p rr tt d", rr=4, tt=8)[:, r, :, :],
            in_=vsrc), "ld", 16)
    kb.inc(s.dma_start(out=alf, in_=kb.creg("c5al")), "ld", 16)
    v_kv = kb.cnt["ld"]
    ve.wait_ge(kb.sems["ld"], v_kv)
    m5 = kb.creg("m5").bitcast(BF16).rearrange("p (qq c) -> p qq c", c=4096)
    scale = 1.0 / float(np.sqrt(128.0))
    v_sc_ev = None
    v_pv_ev = None
    v_ao = None
    v_q_mm = 0
    for hq in range(32):
        s.wait_ge(kb.sems["pe"], v_q_mm)
        kb.inc(s.dma_start(
            out=Qw, in_=scr[S5_Q + hq * 131072: S5_Q + (hq + 1) * 131072]
            .rearrange("(d t) -> d t", t=1024)), "ld", 16)
        v_qld = kb.cnt["ld"]
        for qt in range(8):
            for w in range(2):
                if v_sc_ev is not None:
                    s.wait_ge(kb.sems["ch_vector"], v_sc_ev)
                kb.inc(s.dma_start(out=mbuf, in_=m5[:, qt * 2 + w, :]), "ld", 16)
                v_mk = kb.cnt["ld"]
                ev_list = []
                for kb_ in range(8):
                    pc0 = (kb_ % 2) * 512
                    t_.wait_ge(kb.sems["ld"], v_kv)
                    t_.wait_ge(kb.sems["ld"], v_qld)
                    if kb_ >= 2:
                        t_.wait_ge(kb.sems["ch_vector"], ev_list[kb_ - 2])
                    elif v_sc_ev is not None:
                        t_.wait_ge(kb.sems["ch_vector"], v_sc_ev)
                    kb.inc(t_.matmul(
                        pC2[:, pc0:pc0 + 512],
                        Qw[:, qt * 128:(qt + 1) * 128],
                        Kw[:, kb_ * 512:(kb_ + 1) * 512],
                        start=True, stop=True), "pe", 1)
                    v_s = kb.cnt["pe"]
                    ve.wait_ge(kb.sems["pe"], v_s)
                    if kb_ == 0:
                        ve.wait_ge(kb.sems["ld"], v_mk)
                        ve.wait_ge(kb.sems["ac"], kb.cnt["ac"])
                    kb.inc(ve.tensor_add(scb[:, kb_ * 512:(kb_ + 1) * 512],
                                         pC2[:, pc0:pc0 + 512],
                                         mbuf[:, kb_ * 512:(kb_ + 1) * 512]), "ve", 1)
                    ev_list.append(kb.cnt["ve"])
                v_sc_ev = kb.cnt["ve"]
                ac.wait_ge(kb.sems["ve"], kb.cnt["ve"])
                ac.wait_ge(kb.sems["pe"], kb.cnt["pe"])
                kb.inc(ac.activation(out=Pb, in_=scb, func=AF.Sigmoid,
                                     scale=scale, accum_out=lrow), "ac", 1)
                ve.wait_ge(kb.sems["ac"], kb.cnt["ac"])
                ve.reciprocal(rl, lrow)
                kb.inc(ve.tensor_scalar_mul(Pb, Pb, rl), "ve", 1)
                v_pn = kb.cnt["ve"]
                for kc in range(32):
                    tj = kc % 4
                    if tj == 0:
                        if kc > 0:
                            t_.wait_ge(kb.sems["ch_vector"], v_pv_ev)
                        else:
                            t_.wait_ge(kb.sems["ve"], v_pn)
                            t_.wait_ge(kb.sems["po"], kb.cnt["po"])
                            if v_ao is not None:
                                t_.wait_ge(kb.sems["ch_vector"], v_ao)
                    ti = t_.transpose(pT[:, tj * 128:(tj + 1) * 128],
                                      Pb[:, kc * 128:(kc + 1) * 128],
                                      kb.sb["id_bf"][:, :])
                    if tj == 3:
                        kb.inc(ti, "pe", 1)
                        v_tp = kb.cnt["pe"]
                        ve.wait_ge(kb.sems["pe"], v_tp)
                        kb.inc(ve.tensor_copy(pts, pT[:, 0:512]), "ve", 1)
                        v_pv_ev = kb.cnt["ve"]
                        t_.wait_ge(kb.sems["ch_vector"], v_pv_ev)
                        last = None
                        for j in range(4):
                            kcj = kc - 3 + j
                            last = t_.matmul(pA[:, w * 128:(w + 1) * 128],
                                             Vw[:, kcj * 128:(kcj + 1) * 128],
                                             pts[:, j * 128:(j + 1) * 128],
                                             start=(kcj == 0), stop=(kcj == 31))
                        kb.inc(last, "pe", 1)
                v_q_mm = kb.cnt["pe"]
            ve.wait_ge(kb.sems["pe"], v_q_mm)
            if qt == 0:
                ve.wait_ge(kb.sems["gd"], kb.cnt["gd"])
            asl = alf[:, (qt // 2) * 256 + (qt % 2) * 128:
                      (qt // 2) * 256 + (qt % 2) * 128 + 128]
            ve.tensor_copy(tba, pA[:, 0:128])
            ve.tensor_sub(tb, pA[:, 128:256], tba)
            ve.tensor_mul(tb, tb, asl)
            kb.inc(ve.tensor_add(aost[:, qt * 128:(qt + 1) * 128],
                                 tba, tb), "ve", 1)
            v_ao = kb.cnt["ve"]
        g.wait_ge(kb.sems["ch_vector"], kb.cnt["ve"])
        kb.inc(g.dma_start(
            out=scr[S5_AO + hq * 131072: S5_AO + (hq + 1) * 131072]
            .rearrange("(d t) -> d t", t=1024),
            in_=aost), "gd", 16)
    return


def emit_s5(kb):
    def wspec(oc):
        if oc < 32:
            return ("w5q", 4096, oc * 128)
        if oc == 32:
            return ("w5k", 128, 0)
        return ("w5v", 128, 0)
    emit_attn_proj(kb, wspec, 32, 1, S5_Q, S5_CCI,
                   tabs=("rope5_cos", "rope5_sin"), pk=PK_S7)
    v_ag = emit_kv_ag(kb, S5_CCI, S5_CCO, S5_KPER)
    emit_s5_core(kb, v_ag)
    emit_outproj(kb, "w5o", S5_AO, stats_first=True, pk=PK_S7)
